# version 07
# baseline (speedup 1.0000x reference)
import contextlib
import numpy as np
import ml_dtypes
import concourse.bass as bass
import concourse.mybir as mybir
from concourse.bass_utils import run_bass_kernel_spmd
from concourse.alu_op_type import AluOpType as ALU

F32 = mybir.dt.float32
BF16 = mybir.dt.bfloat16
AF = mybir.ActivationFunctionType

N = 4096
D = 1024
NCH = 8
NQB = 8
E = 64
NGRP = 4
HID = 2730
NUC = 22
BUILD_STAGE = 'full'


def _rotary_tables():
    half = np.arange(0, 64, 2, dtype=np.float64)
    inv_freq = 1.0 / 10000.0 ** (half / 64.0)
    t = np.arange(N, dtype=np.float64)
    freqs = np.concatenate([t[:, None] * inv_freq[None, :]] * 2, axis=-1)
    base = (half + 0.4 * 64.0) / (1.4 * 64.0)
    power = (t - N // 2) / 256.0
    scale = np.concatenate([base[None, :] ** power[:, None]] * 2, axis=-1)
    cos = np.cos(freqs); sin = np.sin(freqs)
    sgn = np.concatenate([-np.ones(32), np.ones(32)])[None, :]
    def pack(tab):
        return np.ascontiguousarray(np.concatenate([tab.T, tab.T], axis=0), np.float32)
    return (pack(cos * scale), pack(sin * scale * sgn),
            pack(cos / scale), pack(sin / scale * sgn))


def _rot_perm(sv):
    P = np.zeros((128, 128), dtype=np.float32)
    for half in (0, 1):
        for ep in range(64):
            e = (ep + 32) % 64
            P[half * 64 + e, half * 64 + ep] = sv[e]
    return P


def _col_tiles(v, width):
    out = np.zeros((128, width), dtype=np.float32)
    nfull = len(v) // 128
    for c in range(nfull):
        out[:, c] = v[c * 128:(c + 1) * 128]
    rem = len(v) - nfull * 128
    if rem:
        out[:rem, nfull] = v[nfull * 128:]
    return np.ascontiguousarray(out)


def host_prep(ins):
    f = np.float32
    sh = {}
    Wq, Wk, Wv = ins['L0_attn_Wq'], ins['L0_attn_Wk'], ins['L0_attn_Wv']
    sh['WqT'] = np.ascontiguousarray(np.asarray(Wq, f).transpose(1, 0, 2).reshape(D, 1024))
    sh['WkT'] = np.ascontiguousarray(np.asarray(Wk, f).transpose(1, 0, 2).reshape(D, 1024))
    sh['WvT'] = np.ascontiguousarray(np.asarray(Wv, f).transpose(1, 0, 2).reshape(D, 1024))
    sh['Wpp'] = np.ascontiguousarray(
        np.asarray(ins['L0_attn_Wp'], f).reshape(64, 16, 1024).transpose(1, 0, 2).reshape(1024, 1024))
    sh['bp_col'] = np.ascontiguousarray(np.asarray(ins['L0_attn_bp'], f).reshape(8, 128).T)
    for l in range(2):
        sh[f'wqkv{l}'] = np.ascontiguousarray(np.asarray(ins['L0_ffn_wqkv'][l], f))
        sh[f'wo{l}'] = np.ascontiguousarray(np.asarray(ins['L0_ffn_wo'][l], f))
        sh[f'w1_{l}'] = np.ascontiguousarray(np.asarray(ins['L0_ffn_w1'][l], f))
        sh[f'w2_{l}'] = np.ascontiguousarray(np.asarray(ins['L0_ffn_w2'][l], f))
        sh[f'b1col{l}'] = _col_tiles(np.asarray(ins['L0_ffn_b1'][l], f), 44)
        sh[f'b2col{l}'] = np.ascontiguousarray(np.asarray(ins['L0_ffn_b2'][l], f).reshape(8, 128).T)
        sh[f'lgcol{l}'] = np.ascontiguousarray(np.asarray(ins['L0_ffn_ln_g'][l], f).reshape(8, 128).T)
        sh[f'lbcol{l}'] = np.ascontiguousarray(np.asarray(ins['L0_ffn_ln_b'][l], f).reshape(8, 128).T)
        qs = np.asarray(ins['L0_ffn_q_scale'][l], f)
        ks = np.asarray(ins['L0_ffn_k_scale'][l], f)
        sh[f'qs_t{l}'] = np.ascontiguousarray(np.tile(qs, 2)[:, None])
        sh[f'ks_t{l}'] = np.ascontiguousarray(np.tile(ks, 2)[:, None])
        sh[f'PQ{l}'] = _rot_perm(qs)
        sh[f'PK{l}'] = _rot_perm(ks)
    sh['TAc'], sh['TAs'], sh['TBc'], sh['TBs'] = _rotary_tables()
    pm = np.arange(128)[:, None]; pq = np.arange(128)[None, :]
    sh['M0'] = np.ascontiguousarray(np.where(pq >= pm, 0.0, -1e30).astype(f))
    sh['M512'] = np.ascontiguousarray(np.where(pq <= pm, 0.0, -1e30).astype(f))
    sh['identb'] = np.ascontiguousarray(np.eye(128, dtype=ml_dtypes.bfloat16))
    E2 = np.zeros((2, 128), dtype=ml_dtypes.bfloat16)
    E2[0, 0:64] = 1; E2[1, 64:128] = 1
    sh['E2h'] = E2
    x = ins['L0_attn_query']
    cores = [{'xT': np.ascontiguousarray(np.asarray(x[b], f).T)} for b in range(8)]
    return sh, cores


class MK:
    def __init__(self, stage='full', nwin=8, sub=50):
        self.nwin = nwin
        self.sub = sub
        self.stage = stage
        self.nc = bass.Bass("TRN2", target_bir_lowering=False, debug=False)
        self.es = contextlib.ExitStack()
        self.q = {'T': [], 'V': [], 'S': [], 'G': [], 'Y': []}
        self.cnt = {}

    def sbuf(self, name, shape, dtype, st=None):
        return (st or self.es).enter_context(self.nc.sbuf_tensor(name, shape, dtype))

    def psum(self, name, shape, dtype=F32):
        return self.es.enter_context(self.nc.psum_tensor(name, shape, dtype))

    def sem(self, name):
        h = self.es.enter_context(self.nc.semaphore(name))
        self.cnt[name] = 0
        self.semh = getattr(self, 'semh', {})
        self.semh[name] = h
        return h

    def emit(self, eng, fn):
        self.q[eng].append(fn)

    def bump(self, sem, k=1):
        self.cnt[sem] += k
        return self.cnt[sem]

    def val(self, sem):
        return self.cnt[sem]

    def op(self, eng, meth, *a, inc=None, wait=None, **kw):
        if inc is not None:
            incv = [(s, k) for (s, k) in ([inc] if isinstance(inc, tuple) else inc) if s]
        else:
            incv = []
        wait = list(wait) if wait else []
        if eng in ('V', 'S') and meth != 'dma_start':
            wait.append((eng, self.cnt[eng]))
            incv.append((eng, 1))
        merged = {}
        for sm, k in incv:
            merged[sm] = merged.get(sm, 0) + k
        incv = list(merged.items())
        for sm, k in incv:
            self.bump(sm, k)
        semh = self.semh
        def fn(e):
            for sm, v in wait:
                e.wait_ge(semh[sm], v)
            instr = getattr(e, meth)(*a, **kw)
            for sm, k in incv:
                instr.then_inc(semh[sm], k)
        self.emit(eng, fn)

    def wait(self, eng, sem, v):
        semh = self.semh
        self.emit(eng, lambda e: e.wait_ge(semh[sem], v))

    def vfence(self):
        self.op('V', 'tensor_copy', self.fsc[0:1, 1:2], self.fsc[0:1, 0:1])
        return self.cnt['V']

    def sfence(self):
        self.op('S', 'copy', self.fsc[0:1, 3:4], self.fsc[0:1, 2:3])
        return self.cnt['S']

    def waitl(self, eng, tokens):
        semh = self.semh
        toks = list(tokens)
        def fn(e):
            for sm, v in toks:
                e.wait_ge(semh[sm], v)
        self.emit(eng, fn)

    NLANES = 4

    def _dma(self, eng, queue, dst, src, wait=None, accum=False):
        st = self.lanes[queue]
        lane = st['rr']
        st['rr'] = (lane + 1) % self.NLANES
        name = f'{queue}{lane}'
        prev = self.cnt[name]
        wait = list(wait) if wait else []
        if prev > 0:
            wait.append((name, prev))
        kw = {'out': dst, 'in_': src}
        if accum:
            kw['accum_op'] = ALU.add
        self.op(eng, 'dma_start', inc=(name, 16), wait=wait, **kw)
        return (name, self.cnt[name])

    def dmaY(self, dst, src, wait=None):
        return self._dma('Y', 'LH', dst, src, wait)

    def dmaS(self, dst, src, wait=None):
        return self._dma('S', 'LS', dst, src, wait)

    def dmaG(self, dst, src, wait=None, accum=False):
        return self._dma('G', 'LG', dst, src, wait, accum)

    def allw(self, queue):
        return [(f'{queue}{i}', self.cnt[f'{queue}{i}'])
                for i in range(self.NLANES) if self.cnt[f'{queue}{i}'] > 0]

    def build(self):
        nc = self.nc
        with self.es:
            self._declare_dram()
            self._globals()
            blk = self.es.enter_context(nc.Block())
            self._stage_a()
            if self.stage in ('F0', 'F0q'):
                self._ffn_layer(0)
            elif self.stage == 'full':
                self._ffn_layer(0)
                self._ffn_layer(1)
            self._finish()
            with nc.allow_low_precision(reason="bf16 matmul operands by design"):
                for tag, attr in [('T', 'tensor'), ('V', 'vector'), ('S', 'scalar'),
                                  ('G', 'gpsimd'), ('Y', 'sync')]:
                    fns = self.q[tag]
                    def body(eng, fns=fns):
                        for fn in fns:
                            fn(eng)
                    getattr(blk, attr)(body)
        return nc

    def _declare_dram(self):
        nc = self.nc
        g = self.g = {}
        def d(name, shape, dtype=F32, kind='ExternalInput'):
            g[name] = nc.dram_tensor(name, shape, dtype, kind=kind).ap()
        d('xT', [D, N])
        for k, shp in [('WqT', [D, 1024]), ('WkT', [D, 1024]), ('WvT', [D, 1024]),
                       ('Wpp', [1024, 1024]), ('bp_col', [128, 8]),
                       ('TAc', [128, N]), ('TAs', [128, N]), ('TBc', [128, N]),
                       ('TBs', [128, N]), ('M0', [128, 128]), ('M512', [128, 128])]:
            d(k, shp)
        d('identb', [128, 128], BF16)
        d('E2h', [2, 128], BF16)
        for l in range(2):
            for k, shp in [(f'wqkv{l}', [D, 1536]), (f'wo{l}', [512, 1024]),
                           (f'w1_{l}', [D, 5460]), (f'w2_{l}', [HID, 1024]),
                           (f'b1col{l}', [128, 44]), (f'b2col{l}', [128, 8]),
                           (f'lgcol{l}', [128, 8]), (f'lbcol{l}', [128, 8]),
                           (f'qs_t{l}', [128, 1]), (f'ks_t{l}', [128, 1]),
                           (f'PQ{l}', [128, 128]), (f'PK{l}', [128, 128])]:
                d(k, shp)
        d('yacc', [D, N], kind='ExternalOutput' if self.stage in ('Y', 'Y3') else 'Internal')
        d('rowsc', [2, 4096], kind='Internal')
        d('R1', [D, N], kind='ExternalOutput' if self.stage == 'A' else 'Internal')
        if self.stage == 'A0':
            d('dbgA0', [128, 64 + N], F32, 'ExternalOutput')
        if self.stage == 'A1':
            d('dbgA1', [128, 2 * N], BF16, 'ExternalOutput')
        if self.stage == 'KV':
            d('dbgKV', [128, 2 * N + 32 * 4 * 65], BF16, 'ExternalOutput')
            d('dbgW', [128, 8 * 256 * 2], BF16, 'ExternalOutput')
        if self.stage == 'X1':
            d('dbgX', [128, 4 * 512], F32, 'ExternalOutput')
        if self.stage == 'CAN':
            d('dbgC', [128, 32 * 16], F32, 'ExternalOutput')
        if self.stage in ('QOST', 'QOST3'):
            d('dbgQ', [128, 2 * 512 + 1024 + 2 * 1024], BF16, 'ExternalOutput')
        d('RA0', [D, N], kind='ExternalOutput' if self.stage == 'F0q' else 'Internal')
        d('RZ1', [D, N], kind='ExternalOutput' if self.stage == 'F0' else 'Internal')
        d('RA1', [D, N], kind='Internal')
        if self.stage != 'A':
            d('outT', [D, N], F32, 'ExternalOutput')
        self.rr = {k: g[k].rearrange("(c p) n -> p c n", p=128)
                   for k in ['xT', 'yacc', 'R1', 'RA0', 'RZ1', 'RA1']
                   + (['outT'] if self.stage != 'A' else [])}

    def _globals(self):
        for s in ['T', 'V', 'S']:
            self.sem(s)
        self.lanes = {}
        for q in ['LH', 'LS', 'LG']:
            self.lanes[q] = {'rr': 0}
            for i in range(self.NLANES):
                self.sem(f'{q}{i}')
        self.onesb = self.sbuf('onesb', [128, 1], BF16)
        self.ones1b = self.sbuf('ones1b', [1, 128], BF16)
        self.bd = self.sbuf('bd', [128, 2], BF16)
        self.E2 = self.sbuf('E2', [2, 128], BF16)
        self.identb_sb = self.sbuf('identb_sb', [128, 128], BF16)
        self.bp_sb = self.sbuf('bp_sb', [128, 8], F32)
        self.M0_sb = self.sbuf('M0_sb', [128, 128], F32)
        self.M512_sb = self.sbuf('M512_sb', [128, 128], F32)
        self.cns = self.sbuf('cns', [128, 4], F32)
        self.fsc = self.sbuf('fsc', [1, 4], F32)
        if self.stage == 'CAN':
            self.cansb = self.sbuf('cansb', [128, 32 * 16], F32)

    def canary(self, idx):
        if self.stage != 'CAN' or idx >= 16:
            return
        if not hasattr(self, '_can_init'):
            self.op('V', 'memset', self.cansb[:, :], -7.0)
            self._can_init = True
        c = self.cansb[:, idx * 32:idx * 32 + 32]
        self.op('V', 'tensor_copy', c[:, 0:8], self.bp_sb[:, :], wait=[('V', self.val('V'))])
        self.op('V', 'tensor_copy', c[:, 8:9], self.onesb[:, :])
        self.op('V', 'tensor_copy', c[:, 9:10], self.ones1b[0:1, :].rearrange("o (a b) -> o a b", b=1)[:, 0, :] if False else self.cns[:, 0:1])
        self.op('V', 'tensor_copy', c[:, 10:26], self.identb_sb[:, 0:16], inc=('V', 1))

    def _load_consts(self):
        g = self.g
        for dst, src in [(self.bp_sb, g['bp_col']), (self.M0_sb, g['M0']),
                         (self.M512_sb, g['M512']), (self.identb_sb, g['identb']),
                         (self.E2, g['E2h'])]:
            self.dmaY(dst[:, :], src)
        w = self.allw('LH')
        for t, v in [(self.onesb, 1.0), (self.ones1b, 1.0)]:
            self.op('V', 'memset', t[:, :], v)
        self.op('V', 'memset', self.bd[:, :], 0.0)
        self.op('V', 'memset', self.bd[0:64, 0:1], 1.0)
        self.op('V', 'memset', self.bd[64:128, 1:2], 1.0)
        self.op('V', 'memset', self.fsc[:, :], 0.0)
        self.op('V', 'memset', self.cns[:, 0:1], 1e-5)
        self.op('V', 'memset', self.cns[:, 1:2], 1e-24)
        self.op('V', 'memset', self.cns[:, 2:3], 2.079441542, inc=('V', 1))
        self.consts_v = self.val('V')
        self.consts_lh = w

    def _ln_rows(self, tag, psums, cS, cS2, arow, brow, wait_t, st=None):
        mc = self.sbuf(f'{tag}_mc', [128, 32], F32, st=st)
        vr = self.sbuf(f'{tag}_vr', [128, 32], F32, st=st)
        rc = self.sbuf(f'{tag}_rc', [128, 32], F32, st=st)
        ab = self.sbuf(f'{tag}_ab', [128, 64], BF16, st=st)
        tr = self.sbuf(f'{tag}_tr', [32, 256], BF16, st=st)
        ptr = self.ptr_bf
        self.op('V', 'tensor_scalar', mc[:, :], psums[:, cS:cS + 32], 1.0 / D, None,
                op0=ALU.mult, wait=[wait_t])
        self.op('V', 'tensor_scalar', vr[:, :], psums[:, cS2:cS2 + 32], 1.0 / D, None, op0=ALU.mult)
        self.op('V', 'tensor_tensor', rc[:, :], mc[:, :], mc[:, :], op=ALU.mult)
        self.op('V', 'tensor_tensor', vr[:, :], vr[:, :], rc[:, :], op=ALU.subtract,
                inc=('V', 1))
        v1 = self.vfence()
        self.op('S', 'activation', vr[:, :], vr[:, :], AF.Ln, bias=self.cns[:, 0:1], wait=[('V', v1)])
        self.op('S', 'activation', rc[:, :], vr[:, :], AF.Exp, scale=-0.5, inc=('S', 1))
        s1 = self.sfence()
        self.op('V', 'tensor_copy', ab[:, 0:32], rc[:, :], wait=[('S', s1)])
        self.op('V', 'scalar_tensor_tensor', ab[:, 32:64], mc[:, :], -1.0, rc[:, :],
                op0=ALU.mult, op1=ALU.mult, inc=('V', 1))
        v2 = self.vfence()
        self.op('T', 'transpose', ptr[0:32, 0:128], ab[:, 0:32], self.identb_sb[:, :],
                wait=[('V', v2)])
        self.op('T', 'transpose', ptr[0:32, 128:256], ab[:, 32:64], self.identb_sb[:, :],
                inc=('T', 1))
        t1 = self.val('T')
        self.op('V', 'tensor_copy', tr[:, 0:128], ptr[0:32, 0:128], wait=[('T', t1)])
        self.op('V', 'tensor_copy', tr[:, 128:256], ptr[0:32, 128:256], inc=('V', 1))
        v3 = self.vfence()
        rs = self.g['rowsc']
        rsa = rs[0:1, :].rearrange("o (r n) -> (o r) n", n=128)
        rsb = rs[1:2, :].rearrange("o (r n) -> (o r) n", n=128)
        ta = self.dmaG(rsa, tr[:, 0:128], wait=[('V', v3)])
        tb = self.dmaG(rsb, tr[:, 128:256])
        t3 = self.dmaG(arow[0:1, :], rs[0:1, :], wait=[ta])
        t4 = self.dmaG(brow[0:1, :], rs[1:2, :], wait=[tb])
        return [t3, t4]

    def _stage_a(self):
        g = self.g
        self._load_consts()
        self.pb0 = self.psum('pb0', [128, 512])
        self.pb1 = self.psum('pb1', [128, 1024])
        self.pb3 = self.psum('pb3', [128, 1024])
        self.pb5 = self.psum('pb5', [128, 512])
        self.pb6 = self.psum('pb6', [128, 512])
        self.pb7 = self.psum('pb7', [128, 512])
        self.ptr_bf = self.pb7.bitcast(BF16)
        pb0, pb1, pb3 = self.pb0, self.pb1, self.pb3
        pOa, pOb, pm = self.pb5, self.pb6, self.pb7

        self.stH0 = contextlib.ExitStack()
        arow0 = self.sbuf('arow0', [1, N], BF16, st=self.stH0)
        brow0 = self.sbuf('brow0', [1, N], BF16, st=self.stH0)
        self.arow0, self.brow0 = arow0, brow0
        stA = self.stA = contextlib.ExitStack()
        h = [self.sbuf(f'h{c}', [128, N], BF16, st=stA) for c in range(NCH)]
        sub = contextlib.ExitStack()
        xspan = [self.sbuf(f'xspan{i}', [128, 8 * 512], F32, st=sub) for i in range(2)]
        xb = [self.sbuf(f'xbspan{i}', [128, 8 * 512], BF16, st=sub) for i in range(2)]
        x2 = [self.sbuf(f'x2span{i}', [128, 8 * 512], BF16, st=sub) for i in range(2)]
        arow = self.sbuf('arowA', [1, N], BF16, st=sub)
        brow = self.sbuf('browA', [1, N], BF16, st=sub)

        vcopy_done = [0] * 8
        tsum_done = [0] * 8
        for s in range(8):
            b = s % 2
            wl = [('V', vcopy_done[s - 2])] if s >= 2 else None
            lh = self.dmaY(xspan[b].rearrange("p (c n) -> p c n", n=512)[:, :, :],
                           self.rr['xT'][:, :, s * 512:(s + 1) * 512], wait=wl)
            wv = ([lh] if s > 0 else self.allw('LH')) + \
                 ([('T', tsum_done[s - 2])] if s >= 2 else [])
            self.op('V', 'tensor_copy', xb[b][:, :], xspan[b][:, :], wait=wv)
            self.op('V', 'tensor_tensor', x2[b][:, :], xb[b][:, :], xb[b][:, :],
                    op=ALU.mult, inc=('V', 1))
            vcopy_done[s] = self.vfence()
            self.wait('T', 'V', vcopy_done[s])
            for c in range(NCH):
                for ms in range(4):
                    st = (s == 0 and c == 0 and ms == 0)
                    last = (c == NCH - 1 and ms == 3)
                    burst_end = (s == 7 and last)
                    self.op('T', 'matmul', pb0[:, s * 4 + ms:s * 4 + ms + 1],
                            xb[b][:, c * 512 + ms * 128: c * 512 + (ms + 1) * 128],
                            self.onesb[:, :], start=st, stop=False, skip_group_check=True)
                    self.op('T', 'matmul', pb0[:, 32 + s * 4 + ms:33 + s * 4 + ms],
                            x2[b][:, c * 512 + ms * 128: c * 512 + (ms + 1) * 128],
                            self.onesb[:, :], start=False, stop=burst_end, skip_group_check=True,
                            inc=(('T', 1) if last else None))
            tsum_done[s] = self.val('T')
        rows_done = self._ln_rows('lnA', pb0, 0, 32, arow, brow, ('T', self.val('T')), st=sub)
        if self.stage == 'A0':
            dbg = self.g['dbgA0']
            sums_sb = self.sbuf('dbg_sums', [128, 64], F32, st=sub)
            self.op('V', 'tensor_copy', sums_sb[:, :], pb0[:, 0:64],
                    wait=[rows_done], inc=('V', 1))
            vv = self.val('V')
            self.dmaS(dbg[0:128, 0:64], sums_sb[:, :], wait=[('V', vv)])
            self.dmaS(dbg[0:1, 64:64 + N // 2].bitcast(BF16), arow[0:1, :])
            self.dmaS(dbg[1:2, 64:64 + N // 2].bitcast(BF16), brow[0:1, :])
            sub.close()
            return

        hbuild_done = [0] * 8
        for s in range(8):
            b = s % 2
            wl = [('V', hbuild_done[s - 2])] if s >= 2 else [('V', vcopy_done[7])]
            lh = self.dmaY(xspan[b].rearrange("p (c n) -> p c n", n=512)[:, :, :],
                           self.rr['xT'][:, :, s * 512:(s + 1) * 512], wait=wl)
            wt = [('V', hbuild_done[s - 1] if s >= 1 else vcopy_done[7])]
            if s == 0:
                wt += rows_done
            self.op('T', 'matmul', pb1[:, 0:512], self.ones1b[:, :],
                    arow[0:1, s * 512:(s + 1) * 512], start=True, stop=True, wait=wt)
            self.op('T', 'matmul', pb1[:, 512:1024], self.ones1b[:, :],
                    brow[0:1, s * 512:(s + 1) * 512], start=True, stop=True, inc=('T', 1))
            tb = self.val('T')
            self.waitl('V', [lh])
            self.wait('V', 'T', tb)
            for c in range(NCH):
                self.op('V', 'tensor_tensor', xb[b][:, 0:512],
                        xspan[b][:, c * 512:(c + 1) * 512], pb1[:, 0:512], op=ALU.mult)
                self.op('V', 'tensor_tensor', h[c][:, s * 512:(s + 1) * 512],
                        xb[b][:, 0:512], pb1[:, 512:1024], op=ALU.add,
                        inc=(('V', 1) if c == NCH - 1 else None))
            hbuild_done[s] = self.vfence()
        h_done = self.vfence()
        self.canary(0)
        h_done = self.val('V')
        if self.stage == 'A1':
            dbg = self.g['dbgA1']
            self.dmaS(dbg[:, 0:N], h[0][:, :], wait=[('V', h_done)])
            self.dmaS(dbg[:, N:2 * N], h[5][:, :])
            sub.close()
            return
        sub.close()

        KT = [self.sbuf(f'KTg{p}', [128, N], BF16, st=stA) for p in range(2)]
        Vg = self.sbuf('Vg', [128, 32 * 4 * 65], BF16, st=stA)
        Vg4 = Vg.rearrange("p (m h e) -> p m h e", h=4, e=65)
        Wq_g = [self.sbuf(f'Wqg{c}', [128, 256], BF16, st=stA) for c in range(NCH)]
        Wk_g = [self.sbuf(f'Wkg{c}', [128, 256], BF16, st=stA) for c in range(NCH)]
        Wv_g = [self.sbuf(f'Wvg{c}', [128, 256], BF16, st=stA) for c in range(NCH)]
        Wp_g = [self.sbuf(f'Wpg{p}', [128, 1024], BF16, st=stA) for p in range(2)]
        Qg = self.sbuf('Qg', [128, 2 * 512], BF16, st=stA)
        Pa = self.sbuf('Pa', [128, 1024], BF16, st=stA)
        Pb = self.sbuf('Pb', [128, 1024], BF16, st=stA)
        dn = self.sbuf('dn', [64, 512], F32, st=stA)
        drow = self.sbuf('drow', [1, 512], BF16, st=stA)
        drowf = self.sbuf('drowf', [1, 512], F32, st=stA)
        Ost = self.sbuf('Ost', [128, 2 * 512], BF16, st=stA)
        ysb = [self.sbuf(f'ysb{i}', [128, 512], F32, st=stA) for i in range(2)]
        yac = [self.sbuf(f'yac{i}', [128, 512], F32, st=stA) for i in range(2)]
        xsp = [self.sbuf(f'xsp{i}', [128, 512], F32, st=stA) for i in range(2)]
        x1f = [self.sbuf(f'x1f{i}', [128, 512], F32, st=stA) for i in range(2)]
        x1b = [self.sbuf(f'x1b{i}', [128, 512], BF16, st=stA) for i in range(2)]
        x1s = [self.sbuf(f'x1s{i}', [128, 512], BF16, st=stA) for i in range(2)]
        ylastlg = [None, None]
        yregion = {}
        yaclastv = [0, 0]
        x1_store_ls = [None, None]
        x1_sums_t = [0, 0]
        h0_first = [True]

        ngrp = {'Y': 1, 'Y3': 3}.get(self.stage, NGRP)
        for grp in range(ngrp):
            for c in range(NCH):
                self.dmaG(Wq_g[c][:, :], g['WqT'][c * 128:(c + 1) * 128, grp * 256:(grp + 1) * 256],
                          wait=([('V', h_done)] if grp == 0 and c == 0 else None))
                self.dmaG(Wk_g[c][:, :], g['WkT'][c * 128:(c + 1) * 128, grp * 256:(grp + 1) * 256])
                self.dmaG(Wv_g[c][:, :], g['WvT'][c * 128:(c + 1) * 128, grp * 256:(grp + 1) * 256])
            for p in range(2):
                self.dmaG(Wp_g[p][:, :],
                          g['Wpp'][grp * 256 + p * 128: grp * 256 + (p + 1) * 128, :])
            lgw = self.allw('LG')

            self.op('V', 'memset', Vg4[:, :, :, 64:65], 1.0, inc=('V', 1))
            vones = self.vfence()
            self.waitl('T', lgw)
            self.wait('T', 'V', max(h_done, vones))
            kvtgt = [pm[:, :], pOa[:, :], pOb[:, :], pb1[:, 0:512]]
            kv_ev = {}
            kv0v = self.vfence()
            kv0s = self.sfence()
            gi = [0]
            def kv_group(mms, evac):
                i = gi[0] % 4
                if gi[0] - 4 in kv_ev:
                    w = [('V', kv_ev[gi[0] - 4])]
                else:
                    w = [('V', kv0v), ('S', kv0s)]
                if w:
                    self.waitl('T', w)
                mms(kvtgt[i])
                tk = self.val('T')
                evac(kvtgt[i], tk)
                kv_ev[gi[0]] = self.vfence()
                gi[0] += 1
            for nc2 in range(NCH):
                for p in range(2):
                    def mms(tgt, nc2=nc2, p=p):
                        for c in range(NCH):
                            self.op('T', 'matmul', tgt, Wk_g[c][:, p * 128:(p + 1) * 128],
                                    h[c][:, nc2 * 512:(nc2 + 1) * 512],
                                    start=(c == 0), stop=(c == NCH - 1),
                                    inc=(('T', 1) if c == NCH - 1 else None))
                    def evac(tgt, tk, nc2=nc2, p=p):
                        self.op('V', 'tensor_copy', KT[p][:, nc2 * 512:(nc2 + 1) * 512],
                                tgt, wait=[('T', tk)], inc=('V', 1))
                    kv_group(mms, evac)
                for ms in range(4):
                    def mms(tgt, nc2=nc2, ms=ms):
                        for c in range(NCH):
                            self.op('T', 'matmul', tgt[0:128, 0:256],
                                    h[c][:, nc2 * 512 + ms * 128: nc2 * 512 + (ms + 1) * 128],
                                    Wv_g[c][:, :], start=(c == 0), stop=(c == NCH - 1),
                                    inc=(('T', 1) if c == NCH - 1 else None))
                    def evac(tgt, tk, nc2=nc2, ms=ms):
                        mcg = nc2 * 4 + ms
                        self.op('V', 'tensor_copy', Vg4[:, mcg, :, 0:64],
                                tgt[0:128, 0:256].rearrange("p (hh e) -> p hh e", e=64),
                                wait=[('T', tk)], inc=('V', 1))
                    kv_group(mms, evac)
            self.wait('T', 'V', kv_ev[gi[0] - 1])

            if self.stage == 'KV' and grp == 0:
                dbg = self.g['dbgKV']
                vlast = self.vfence()
                self.dmaS(dbg[:, 0:N], KT[0][:, :], wait=[('V', vlast)])
                self.dmaS(dbg[:, N:2 * N], KT[1][:, :])
                self.dmaS(dbg[:, 2 * N:], Vg[:, :])
                dW = self.g['dbgW']
                for c in range(NCH):
                    self.dmaS(dW[:, c * 256:(c + 1) * 256], Wk_g[c][:, :])
                    self.dmaS(dW[:, 2048 + c * 256:2048 + (c + 1) * 256], Wv_g[c][:, :])
                return
            self.canary(1 + grp * 3)
            for qb in range(NQB):
                if qb == 1:
                    self.canary(2 + grp * 3)
                for p in range(2):
                    for c in range(NCH):
                        self.op('T', 'matmul', pm[:, :], Wq_g[c][:, p * 128:(p + 1) * 128],
                                h[c][:, qb * 512:(qb + 1) * 512],
                                start=(c == 0), stop=(c == NCH - 1),
                                inc=(('T', 1) if c == NCH - 1 else None))
                    tq = self.val('T')
                    self.op('V', 'tensor_copy', Qg[:, p * 512:(p + 1) * 512], pm[:, :],
                            wait=[('T', tq)], inc=('V', 1))
                    self.wait('T', 'V', self.vfence())
                for p in range(2):
                    sexp = {}
                    tavm = {}
                    p0exp = self.val('S')
                    for mc in range(34):
                        if mc >= 2:
                            j = mc - 2
                            Pj = Pa if j % 2 == 0 else Pb
                            self.wait('T', 'S', sexp[j + 1] if j + 1 in sexp else sexp[j])
                            self.op('T', 'matmul', pOa[0:65, :],
                                    Vg4[:, j, 2 * p, :], Pj[:, 0:512],
                                    start=(j == 0), stop=(j == 31))
                            self.op('T', 'matmul', pOb[0:65, :],
                                    Vg4[:, j, 2 * p + 1, :], Pj[:, 512:1024],
                                    start=(j == 0), stop=(j == 31),
                                    inc=('T', 1))
                            tavm[j] = self.val('T')
                        if mc < 32:
                            Sp = pb1 if mc % 2 == 0 else pb3
                            Pj = Pa if mc % 2 == 0 else Pb
                            wS = [('S', sexp[mc - 2])] if mc >= 2 else [('S', p0exp)]
                            self.op('T', 'matmul', Sp[:, 0:512],
                                    KT[p][0:64, mc * 128:(mc + 1) * 128],
                                    Qg[0:64, p * 512:(p + 1) * 512],
                                    start=True, stop=True, tile_position=(0, 0),
                                    wait=wS)
                            self.op('T', 'matmul', Sp[:, 512:1024],
                                    KT[p][64:128, mc * 128:(mc + 1) * 128],
                                    Qg[64:128, p * 512:(p + 1) * 512],
                                    start=True, stop=True, tile_position=(64, 0),
                                    inc=('T', 1))
                            ts = self.val('T')
                            we = [('T', ts)]
                            if mc - 2 in tavm:
                                we.append(('T', tavm[mc - 2]))
                            self.op('S', 'activation', Pj[:, :], Sp[:, :], AF.Exp,
                                    scale=0.125, wait=we)
                            sexp[mc] = self.val('S')
                            if mc == 31:
                                sexp[32] = self.sfence()
                    tav = self.val('T')
                    for hh, pO in ((0, pOa), (1, pOb)):
                        self.op('V', 'reciprocal', drow[0:1, :], pO[64:65, :],
                                wait=[('T', tav)] if hh == 0 else None, inc=('V', 1))
                        vr1 = self.vfence()
                        self.op('T', 'matmul', pm[0:64, 0:512], self.ones1b[0:1, 0:64],
                                drow[0:1, :], start=True, stop=True,
                                wait=[('V', vr1)], inc=('T', 1))
                        tb1 = self.val('T')
                        self.op('V', 'tensor_copy', dn[:, :], pm[0:64, 0:512], wait=[('T', tb1)])
                        self.op('V', 'tensor_tensor',
                                Ost[hh * 64:(hh + 1) * 64, p * 512:(p + 1) * 512],
                                pO[0:64, :], dn[:, :], op=ALU.mult, inc=('V', 1))
                    self.wait('T', 'V', self.vfence())
                if self.stage == 'Y' and grp == 0 and qb == NQB - 1:
                    self._stopY = True
                if self.stage == 'X1' and grp == 3 and qb == 0:
                    ostsnap = self.sbuf('dbg_ostsnap', [128, 1024], BF16)
                    self.op('V', 'tensor_copy', ostsnap[:, :], Ost[:, :],
                            wait=[('V', self.val('V'))], inc=('V', 1))
                    self._ostsnap = ostsnap
                if self.stage == 'QOST3' and grp == 3 and qb == 0:
                    dbg = self.g['dbgQ']
                    vv = self.val('V')
                    self.dmaS(dbg[:, 0:1024], Qg[:, :], wait=[('V', vv)])
                    self.dmaS(dbg[:, 1024:2048], Pa[:, :])
                    self.dmaS(dbg[:, 2048:3072], Ost[:, :])
                    return
                if self.stage == 'QOST' and grp == 0 and qb == 0:
                    dbg = self.g['dbgQ']
                    vv = self.val('V')
                    self.dmaS(dbg[:, 0:1024], Qg[:, :], wait=[('V', vv)])
                    self.dmaS(dbg[:, 1024:2048], Pa[:, :])
                    self.dmaS(dbg[:, 2048:3072], Ost[:, :])
                    return
                for oc in range(8):
                    for p in range(2):
                        self.op('T', 'matmul', pb1[:, 0:512],
                                Wp_g[p][:, oc * 128:(oc + 1) * 128],
                                Ost[:, p * 512:(p + 1) * 512], start=(p == 0), stop=(p == 1),
                                inc=(('T', 1) if p == 1 else None))
                    ty = self.val('T')
                    if grp < 3:
                        b = oc % 2
                        wv = [('T', ty)]
                        if ylastlg[b]:
                            wv.append(ylastlg[b])
                        self.op('V', 'tensor_copy', ysb[b][:, :], pb1[:, 0:512],
                                wait=wv, inc=('V', 1))
                        vy = self.vfence()
                        wdma = [('V', vy)]
                        if (qb, oc) in yregion:
                            wdma.append(yregion[(qb, oc)])
                        ylastlg[b] = self.dmaG(
                            self.rr['yacc'][:, oc, qb * 512:(qb + 1) * 512],
                            ysb[b][:, :], wait=wdma, accum=(grp > 0))
                        yregion[(qb, oc)] = ylastlg[b]
                        self.wait('T', 'V', vy)
                    else:
                        b = oc % 2
                        wld = [yregion[(qb, oc)]]
                        if yaclastv[b]:
                            wld.append(('V', yaclastv[b]))
                        lh1 = self.dmaY(yac[b][:, :],
                                        self.rr['yacc'][:, oc, qb * 512:(qb + 1) * 512], wait=wld)
                        lh2 = self.dmaY(xsp[b][:, :],
                                        self.rr['xT'][:, oc, qb * 512:(qb + 1) * 512],
                                        wait=[('V', yaclastv[b])] if yaclastv[b] else None)
                        wv = [('T', ty), lh1, lh2]
                        if x1_store_ls[b]:
                            wv.append(x1_store_ls[b])
                        self.op('V', 'tensor_tensor', x1f[b][:, :], pb1[:, 0:512],
                                yac[b][:, :], op=ALU.add, wait=wv)
                        self.op('V', 'scalar_tensor_tensor', x1f[b][:, :], x1f[b][:, :],
                                self.bp_sb[:, oc:oc + 1], xsp[b][:, :],
                                op0=ALU.add, op1=ALU.add, inc=('V', 1))
                        vx1 = self.vfence()
                        yaclastv[b] = vx1
                        self.wait('T', 'V', vx1)
                        x1_store_ls[b] = self.dmaS(
                            self.rr['R1'][:, oc, qb * 512:(qb + 1) * 512],
                            x1f[b][:, :], wait=[('V', vx1)])
                        wv2 = [('T', x1_sums_t[b])] if x1_sums_t[b] else None
                        self.op('V', 'tensor_copy', x1b[b][:, :], x1f[b][:, :], wait=wv2)
                        self.op('V', 'tensor_tensor', x1s[b][:, :], x1b[b][:, :],
                                x1b[b][:, :], op=ALU.mult, inc=('V', 1))
                        vs1 = self.vfence()
                        self.wait('T', 'V', vs1)
                        for ms in range(4):
                            st = h0_first[0]
                            h0_first[0] = False
                            bend = (qb == NQB - 1 and oc == NCH - 1 and ms == 3)
                            self.op('T', 'matmul', pb0[:, 64 + qb * 4 + ms:65 + qb * 4 + ms],
                                    x1b[b][:, ms * 128:(ms + 1) * 128], self.onesb[:, :],
                                    start=st, stop=False, skip_group_check=True)
                            self.op('T', 'matmul', pb0[:, 96 + qb * 4 + ms:97 + qb * 4 + ms],
                                    x1s[b][:, ms * 128:(ms + 1) * 128], self.onesb[:, :],
                                    start=False, stop=bend, skip_group_check=True,
                                    inc=(('T', 1) if ms == 3 else None))
                        x1_sums_t[b] = self.val('T')
                        if self.stage == 'X1' and grp == 3 and qb == 0 and oc == 1:
                            dbg = self.g['dbgX']
                            ycp = self.sbuf('dbg_ycp', [128, 512], F32)
                            bcp = self.sbuf('dbg_bcp', [128, 8], F32)
                            self.op('V', 'tensor_copy', ycp[:, :], pb1[:, 0:512],
                                    wait=[('V', self.val('V'))])
                            self.op('V', 'tensor_copy', bcp[:, :], self.bp_sb[:, :],
                                    inc=('V', 1))
                            vv = self.val('V')
                            self.dmaS(dbg[:, 0:512], x1f[0][:, :], wait=[('V', vv)])
                            self.dmaS(dbg[:, 512:1024], ycp[:, :])
                            self.dmaS(dbg[:, 1024:1536], yac[0][:, :])
                            self.dmaS(dbg[:, 1536:1544], bcp[:, :])
                            wcp = self.sbuf('dbg_wcp', [128, 512], F32)
                            self.op('V', 'tensor_copy', wcp[:, 0:256], self._ostsnap[:, 0:256])
                            self.op('V', 'tensor_copy', wcp[:, 256:512], Ost[:, 0:256],
                                    inc=('V', 1))
                            self.dmaS(dbg[:, 1544:2048], wcp[:, 0:504], wait=[('V', self.val('V'))])
                            return
        self.canary(15)
        self.h0_rows_done = self._ln_rows('ln0', pb0, 64, 96, arow0, brow0,
                                          ('T', self.val('T')), st=stA)
        self.stageA_end = [('T', self.val('T')), ('V', self.val('V')),
                           ('S', self.val('S'))] + self.h0_rows_done
        if self.stage == 'CAN':
            self.dmaS(self.g['dbgC'], self.cansb[:, :], wait=[('V', self.val('V'))])

    def _ffn_layer(self, l):
        g = self.g
        pb0, pb1, pb3 = self.pb0, self.pb1, self.pb3
        pb5, pb6, pb7 = self.pb5, self.pb6, self.pb7
        if l == 0:
            self.stA.close()
            self.stA = None
            gate = list(self.stageA_end)
            Rin = self.rr['R1']
        else:
            self.stH0.close()
            self.stH0 = None
            gate = list(self.ffn_gate)
            Rin = self.rr['RZ1']
        rin_store = self.allw('LS')
        Rout = self.rr['RA0'] if l == 0 else self.rr['RA1']
        Rz = self.rr['RZ1'] if l == 0 else self.rr['outT']

        stR = contextlib.ExitStack()
        rsb = lambda nm, shp, dt: self.sbuf(f'L{l}{nm}', shp, dt, st=stR)
        a2row = rsb('a2row', [1, N], BF16)
        b2row = rsb('b2row', [1, N], BF16)
        b1t = rsb('b1t', [128, 44], F32)
        b2c = rsb('b2c', [128, 8], F32)
        lgc = rsb('lgc', [128, 8], F32)
        lbb = rsb('lbb', [128, 8], BF16)

        stF = contextlib.ExitStack()
        sb = lambda nm, shp, dt: self.sbuf(f'L{l}{nm}', shp, dt, st=stF)
        wqkv = [sb(f'wqkv{c}', [128, 1536], BF16) for c in range(NCH)]
        wo_sb = [sb(f'wo{p}', [128, 1024], BF16) for p in range(4)]
        PQ = sb('PQ', [128, 128], BF16)
        PK = sb('PK', [128, 128], BF16)
        qs_t = sb('qs', [128, 1], F32)
        ks_t = sb('ks', [128, 1], F32)
        self.dmaG(wqkv[0][:, :], g[f'wqkv{l}'][0:128, :], wait=gate)
        for c in range(1, NCH):
            self.dmaG(wqkv[c][:, :], g[f'wqkv{l}'][c * 128:(c + 1) * 128, :])
        for p in range(4):
            self.dmaG(wo_sb[p][:, :], g[f'wo{l}'][p * 128:(p + 1) * 128, :])
        self.dmaG(PQ[:, :], g[f'PQ{l}'])
        self.dmaG(PK[:, :], g[f'PK{l}'])
        self.dmaG(lbb[:, :], g[f'lbcol{l}'])
        tqs = self.dmaY(qs_t[:, :], g[f'qs_t{l}'], wait=gate)
        tks = self.dmaY(ks_t[:, :], g[f'ks_t{l}'])
        tlg = self.dmaY(lgc[:, :], g[f'lgcol{l}'])
        tb2c = self.dmaY(b2c[:, :], g[f'b2col{l}'])
        wload = self.allw('LG') + self.allw('LH')

        zwin = sb('zwin', [128, 8 * 513], F32)
        zw3 = zwin.rearrange("p (c n) -> p c n", n=513)
        hw = sb('hw', [128, 8 * 512], BF16)
        qraw = sb('qraw', [128, 4 * 512], BF16)
        kraw = sb('kraw', [128, 4 * 512], BF16)
        q2b = sb('q2b', [128, 512], BF16)
        k2b = sb('k2b', [128, 4 * 512], BF16)
        qf = sb('qf', [128, 4 * 512], BF16)
        kf = [sb(f'kf{i}', [128, 4 * 512], BF16) for i in range(2)]
        Vw = [sb(f'Vw{i}', [128, 4 * 8 * 65], BF16) for i in range(2)]
        Vw4 = [t.rearrange("p (m hh e) -> p m hh e", hh=8, e=65) for t in Vw]
        rk8 = [sb(f'rk8{i}', [128, 32], F32) for i in range(2)]
        rqrow = sb('rqrow', [2, 512], BF16)
        tmp1 = sb('tmp1', [128, 512], F32)
        tmp2 = sb('tmp2', [128, 512], F32)
        tmp3 = sb('tmp3', [128, 512], F32)
        tmp1k = sb('tmp1k', [128, 512], F32)
        tmp2k = sb('tmp2k', [128, 512], F32)
        tmp3k = sb('tmp3k', [128, 512], F32)
        TA1 = sb('TA1', [128, 512], F32)
        TA2 = sb('TA2', [128, 512], F32)
        TB1 = sb('TB1', [128, 512], F32)
        TB2 = sb('TB2', [128, 512], F32)
        A0w = sb('A0w', [128, 513], F32)
        B0w = sb('B0w', [128, 513], F32)
        Ph = [sb(f'Ph{i}', [128, 8 * 512], BF16) for i in range(2)]
        dnl = sb('dnl', [64, 512], F32)
        drl = sb('drl', [1, 512], BF16)
        drlf = sb('drlf', [1, 512], F32)
        Ow = sb('Ow', [128, 4 * 512], BF16)
        za = sb('za', [128, 8 * 513], F32)
        za3 = za.rearrange("p (c n) -> p c n", n=513)
        zhalo = sb('zhalo', [128, 8], F32)
        zab = sb('zab', [128, 8 * 513], BF16)
        za2 = sb('za2', [128, 8 * 513], BF16)
        scmb = sb('scmb', [128, 64], F32)

        def chunks(w):
            out = []
            for mc in range(8):
                if w == 0 and mc < 4:
                    continue
                out.append((mc, max(0, mc - 4) * 128, min(3, mc) * 128 + 128))
            return out

        zstore = []
        stats_first = [True]
        tstats_prev = 0
        tav_prev = 0
        vwin_prev = 0
        zast_prev = None
        for w in range(self.nwin):
            cur = w % 2
            prv = 1 - cur
            wl = list(rin_store)
            if vwin_prev:
                wl.append(('V', vwin_prev))
            if w == 0:
                tz = [self.dmaY(zw3[:, c, 1:513], Rin[:, c, 0:512],
                                wait=(wl + wload if c == 0 else wl)) for c in range(NCH)]
                self.op('V', 'memset', zw3[:, :, 0:1], 0.0, inc=('V', 1))
            else:
                tz = [self.dmaY(zw3[:, c, :], Rin[:, c, w * 512 - 1:(w + 1) * 512],
                                wait=wl) for c in range(NCH)]
            wt = [('V', vwin_prev)] if vwin_prev else None
            ttab = [self.dmaY(TA1[:, :], g['TAc'][:, w * 512:(w + 1) * 512], wait=wt),
                    self.dmaY(TA2[:, :], g['TAs'][:, w * 512:(w + 1) * 512], wait=wt),
                    self.dmaY(TB1[:, :], g['TBc'][:, w * 512:(w + 1) * 512], wait=wt),
                    self.dmaY(TB2[:, :], g['TBs'][:, w * 512:(w + 1) * 512], wait=wt)]

            if l == 0:
                vh0 = self.val('V')
                tw = [('V', vh0)]
                if w == 0:
                    self.op('T', 'matmul', pb1[:, 0:512], self.ones1b[:, :],
                            self.arow0[0:1, 0:512], start=True, stop=True,
                            wait=tw + gate)
                    self.op('T', 'matmul', pb3[:, 0:512], self.ones1b[:, :],
                            self.brow0[0:1, 0:512], start=True, stop=True, inc=('T', 1))
                    tb = self.val('T')
                    self.op('V', 'memset', A0w[:, 0:1], 0.0, wait=[('T', tb)])
                    self.op('V', 'memset', B0w[:, 0:1], 0.0)
                    self.op('V', 'tensor_copy', A0w[:, 1:513], pb1[:, 0:512])
                    self.op('V', 'tensor_copy', B0w[:, 1:513], pb3[:, 0:512], inc=('V', 1))
                else:
                    self.op('T', 'matmul', pb1[:, 0:512], self.ones1b[:, :],
                            self.arow0[0:1, w * 512 - 1:(w + 1) * 512 - 1],
                            start=True, stop=True, wait=tw)
                    self.op('T', 'matmul', pb1[:, 512:513], self.ones1b[:, :],
                            self.arow0[0:1, (w + 1) * 512 - 1:(w + 1) * 512],
                            start=True, stop=True)
                    self.op('T', 'matmul', pb3[:, 0:512], self.ones1b[:, :],
                            self.brow0[0:1, w * 512 - 1:(w + 1) * 512 - 1],
                            start=True, stop=True)
                    self.op('T', 'matmul', pb3[:, 512:513], self.ones1b[:, :],
                            self.brow0[0:1, (w + 1) * 512 - 1:(w + 1) * 512],
                            start=True, stop=True, inc=('T', 1))
                    tb = self.val('T')
                    self.op('V', 'tensor_copy', A0w[:, 0:512], pb1[:, 0:512],
                            wait=[('T', tb)])
                    self.op('V', 'tensor_copy', A0w[:, 512:513], pb1[:, 512:513])
                    self.op('V', 'tensor_copy', B0w[:, 0:512], pb3[:, 0:512])
                    self.op('V', 'tensor_copy', B0w[:, 512:513], pb3[:, 512:513],
                            inc=('V', 1))
                for c in range(NCH):
                    off = 0 if c >= 4 else 1
                    wz = tz if c == 0 else None
                    self.op('V', 'tensor_tensor', tmp1[:, :],
                            zw3[:, c, off:off + 512], A0w[:, off:off + 512],
                            op=ALU.mult, wait=wz)
                    self.op('V', 'tensor_tensor', hw[:, c * 512:(c + 1) * 512],
                            tmp1[:, :], B0w[:, off:off + 512], op=ALU.add,
                            inc=(('V', 1) if c == NCH - 1 else None))
            else:
                for c in range(NCH):
                    off = 0 if c >= 4 else 1
                    wz = tz if c == 0 else None
                    self.op('V', 'tensor_copy', hw[:, c * 512:(c + 1) * 512],
                            zw3[:, c, off:off + 512], wait=wz,
                            inc=(('V', 1) if c == NCH - 1 else None))
            vh = self.vfence()
            if self.sub < 2:
                vwin_prev = self.val('V')
                continue

            self.waitl('T', wload + [('V', vh)])
            self.op('V', 'memset', Vw4[cur][:, :, :, 64:65], 1.0, inc=('V', 1))
            vmset = self.vfence()
            bank_free = {}
            for pc in range(4):
                for part, bid, pbank, dst in ((0, 0, pb1, qraw), (1, 1, pb3, kraw)):
                    if bid in bank_free:
                        self.wait('T', 'V', bank_free[bid])
                    for c in range(NCH):
                        self.op('T', 'matmul', pbank[:, 0:512],
                                wqkv[c][:, part * 512 + pc * 128:part * 512 + (pc + 1) * 128],
                                hw[:, c * 512:(c + 1) * 512],
                                start=(c == 0), stop=(c == NCH - 1),
                                inc=(('T', 1) if c == NCH - 1 else None))
                    tqk = self.val('T')
                    self.op('V', 'tensor_copy', dst[:, pc * 512:(pc + 1) * 512],
                            pbank[:, 0:512], wait=[('T', tqk)], inc=('V', 1))
                    bank_free[bid] = self.vfence()
            if self.sub == 21:
                vwin_prev = self.val('V')
                continue
            self.wait('T', 'V', vmset)
            for ms in range(4):
                bid = 2 + ms % 2
                tgt = pb1[:, 512:1024] if ms % 2 == 0 else pb3[:, 512:1024]
                if bid in bank_free:
                    self.wait('T', 'V', bank_free[bid])
                for c in range(NCH):
                    self.op('T', 'matmul', tgt,
                            hw[:, c * 512 + ms * 128:c * 512 + (ms + 1) * 128],
                            wqkv[c][:, 1024:1536],
                            start=(c == 0), stop=(c == NCH - 1),
                            inc=(('T', 1) if c == NCH - 1 else None))
                tv = self.val('T')
                self.op('V', 'tensor_copy', Vw4[cur][:, ms, :, 0:64],
                        tgt.rearrange("p (hh e) -> p hh e", e=64),
                        wait=[('T', tv)], inc=('V', 1))
                bank_free[bid] = self.vfence()
            self.waitl('T', [('V', v) for v in bank_free.values()])

            if self.sub < 3:
                vwin_prev = self.val('V')
                continue
            KNORMS_ONLY = (self.sub == 31)
            for pc in range(4):
                self.op('V', 'tensor_tensor', k2b[:, pc * 512:(pc + 1) * 512],
                        kraw[:, pc * 512:(pc + 1) * 512],
                        kraw[:, pc * 512:(pc + 1) * 512], op=ALU.mult,
                        inc=(('V', 1) if pc == 3 else None))
            self.wait('T', 'V', self.vfence())
            for pc in range(4):
                for ms in range(4):
                    cc = (pc * 4 + ms) * 2
                    self.op('T', 'matmul', pb7[:, cc:cc + 2],
                            k2b[:, pc * 512 + ms * 128:pc * 512 + (ms + 1) * 128],
                            self.bd[:, :], start=True, stop=True,
                            skip_group_check=True,
                            inc=(('T', 1) if pc == 3 and ms == 3 else None))
            tkn = self.val('T')
            self.op('S', 'activation', rk8[cur][:, :], pb7[:, 0:32], AF.Ln,
                    bias=self.cns[:, 1:2], wait=[('T', tkn)])
            self.op('S', 'activation', rk8[cur][:, :], rk8[cur][:, :], AF.Exp,
                    scale=-0.5, bias=self.cns[:, 2:3], inc=('S', 1))
            self.wait('T', 'S', self.sfence())
            if KNORMS_ONLY:
                vwin_prev = self.val('V')
                continue

            rotq_t = [pb7[:, 0:512], pb3[:, 0:512]]
            rotk_t = [pb1[:, 0:512], pb3[:, 512:1024]]
            rq_t = [pb5[:, 0:512], pb6[:, 0:512]]
            qn_t = [pb5[0:2, 0:512], pb6[0:2, 0:512]]
            free_rq = {}
            free_rot = {}
            free_rotk = {}
            for pc in range(4):
                b = pc % 2
                wr = [('V', free_rot[b])] if b in free_rot else None
                self.op('T', 'matmul', rotq_t[b], PQ[:, :],
                        qraw[:, pc * 512:(pc + 1) * 512], start=True, stop=True,
                        wait=wr, inc=('T', 1))
                trot = self.val('T')
                self.op('V', 'tensor_tensor', q2b[:, :], qraw[:, pc * 512:(pc + 1) * 512],
                        qraw[:, pc * 512:(pc + 1) * 512], op=ALU.mult, inc=('V', 1))
                vq2 = self.vfence()
                wq = [('V', vq2)]
                if b in free_rq:
                    wq.append(('V', free_rq[b]))
                self.op('T', 'matmul', qn_t[b], self.bd[:, :], q2b[:, :],
                        start=True, stop=True, wait=wq, inc=('T', 1))
                tqn = self.val('T')
                self.op('S', 'activation', rqrow[:, :], qn_t[b], AF.Ln,
                        bias=self.cns[0:2, 1:2], wait=[('T', tqn)])
                self.op('S', 'activation', rqrow[:, :], rqrow[:, :], AF.Exp,
                        scale=-0.5, inc=('S', 1))
                sqn = self.sfence()
                self.op('T', 'matmul', rq_t[b], self.E2[:, :], rqrow[:, :],
                        start=True, stop=True, wait=[('S', sqn)], inc=('T', 1))
                trq = self.val('T')
                self.op('V', 'tensor_scalar', tmp1[:, :], qraw[:, pc * 512:(pc + 1) * 512],
                        qs_t[:, :], None, op0=ALU.mult, wait=[('T', trot), tqs] + ttab)
                self.op('V', 'tensor_tensor', tmp2[:, :], tmp1[:, :], TA1[:, :],
                        op=ALU.mult)
                self.op('V', 'tensor_tensor', tmp3[:, :], rotq_t[b], TA2[:, :],
                        op=ALU.mult)
                self.op('V', 'tensor_tensor', tmp2[:, :], tmp2[:, :], tmp3[:, :],
                        op=ALU.add)
                self.op('V', 'tensor_tensor', qf[:, pc * 512:(pc + 1) * 512],
                        tmp2[:, :], rq_t[b], op=ALU.mult,
                        wait=[('T', trq)], inc=('V', 1))
                free_rot[b] = free_rq[b] = self.vfence()
                wk = [('V', free_rotk[b])] if b in free_rotk else None
                self.op('T', 'matmul', rotk_t[b], PK[:, :],
                        kraw[:, pc * 512:(pc + 1) * 512], start=True, stop=True,
                        wait=wk, inc=('T', 1))
                trotk = self.val('T')
                self.op('V', 'tensor_scalar', tmp1k[:, :], kraw[:, pc * 512:(pc + 1) * 512],
                        ks_t[:, :], None, op0=ALU.mult, wait=[('T', trotk), tks])
                self.op('V', 'tensor_tensor', tmp2k[:, :], tmp1k[:, :], TB1[:, :],
                        op=ALU.mult)
                self.op('V', 'tensor_tensor', tmp3k[:, :], rotk_t[b], TB2[:, :],
                        op=ALU.mult)
                self.op('V', 'tensor_tensor', kf[cur][:, pc * 512:(pc + 1) * 512],
                        tmp2k[:, :], tmp3k[:, :], op=ALU.add, inc=('V', 1))
                free_rotk[b] = self.vfence()
            self.waitl('T', [('V', v) for v in
                             list(free_rot.values()) + list(free_rotk.values())])

            if self.sub < 40:
                vwin_prev = self.val('V')
                continue
            for pc in range(4):
                hA, hB = 2 * pc, 2 * pc + 1
                p0exp = self.val('S')
                first_exp = True
                sexp_l = {}
                for k, (mc, qlo, qhi) in enumerate(chunks(w)):
                    src = kf[prv] if mc < 4 else kf[cur]
                    vsrc = mc % 4
                    SA = pb5[:, :] if k % 2 == 0 else pb1[:, 0:512]
                    SB = pb6[:, :] if k % 2 == 0 else pb1[:, 512:1024]
                    wS = [('S', sexp_l[k - 2])] if k - 2 in sexp_l else [('S', p0exp)]
                    self.op('T', 'matmul', SA[:, qlo:qhi],
                            src[0:64, pc * 512 + vsrc * 128:pc * 512 + (vsrc + 1) * 128],
                            qf[0:64, pc * 512 + qlo:pc * 512 + qhi],
                            start=True, stop=True, tile_position=(0, 0), wait=wS)
                    self.op('T', 'matmul', SB[:, qlo:qhi],
                            src[64:128, pc * 512 + vsrc * 128:pc * 512 + (vsrc + 1) * 128],
                            qf[64:128, pc * 512 + qlo:pc * 512 + qhi],
                            start=True, stop=True, tile_position=(64, 0),
                            inc=('T', 1))
                    tS = self.val('T')
                    mwait = [('T', tS)]
                    if mc >= 4:
                        qs0 = (mc - 4) * 128
                        self.op('V', 'tensor_tensor', SA[:, qs0:qs0 + 128],
                                SA[:, qs0:qs0 + 128], self.M0_sb[:, :], op=ALU.add,
                                wait=mwait)
                        mwait = None
                        self.op('V', 'tensor_tensor', SB[:, qs0:qs0 + 128],
                                SB[:, qs0:qs0 + 128], self.M0_sb[:, :], op=ALU.add,
                                inc=('V', 1))
                    if mc <= 3:
                        qs0 = mc * 128
                        self.op('V', 'tensor_tensor', SA[:, qs0:qs0 + 128],
                                SA[:, qs0:qs0 + 128], self.M512_sb[:, :], op=ALU.add,
                                wait=mwait)
                        self.op('V', 'tensor_tensor', SB[:, qs0:qs0 + 128],
                                SB[:, qs0:qs0 + 128], self.M512_sb[:, :], op=ALU.add,
                                inc=('V', 1))
                    vmask = self.vfence()
                    if self.sub < 43:
                        continue
                    rksrc = rk8[prv] if mc < 4 else rk8[cur]
                    we = [('V', vmask)]
                    if first_exp:
                        we.append(('T', tav_prev))
                        first_exp = False
                    self.op('S', 'activation', Ph[0][:, mc * 512 + qlo:mc * 512 + qhi],
                            SA[:, qlo:qhi], AF.Exp,
                            scale=rksrc[:, (pc * 4 + vsrc) * 2:(pc * 4 + vsrc) * 2 + 1],
                            wait=we)
                    self.op('S', 'activation', Ph[1][:, mc * 512 + qlo:mc * 512 + qhi],
                            SB[:, qlo:qhi], AF.Exp,
                            scale=rksrc[:, (pc * 4 + vsrc) * 2 + 1:(pc * 4 + vsrc) * 2 + 2],
                            inc=('S', 1))
                    sexp_l[k] = self.val('S')
                if self.sub < 46:
                    continue
                self.wait('T', 'S', self.sfence())
                for qsub in range(4):
                    mcs = [mc for (mc, qlo, qhi) in chunks(w) if qlo <= qsub * 128 < qhi]
                    for ki, mc in enumerate(mcs):
                        vv4 = Vw4[prv] if mc < 4 else Vw4[cur]
                        st_ = (qsub == 0 and ki == 0)
                        sp_ = (qsub == 3 and ki == len(mcs) - 1)
                        self.op('T', 'matmul', pb1[0:65, qsub * 128:(qsub + 1) * 128],
                                vv4[:, mc % 4, hA, :],
                                Ph[0][:, mc * 512 + qsub * 128:mc * 512 + (qsub + 1) * 128],
                                start=st_, stop=sp_, skip_group_check=True)
                        self.op('T', 'matmul',
                                pb1[0:65, 512 + qsub * 128:512 + (qsub + 1) * 128],
                                vv4[:, mc % 4, hB, :],
                                Ph[1][:, mc * 512 + qsub * 128:mc * 512 + (qsub + 1) * 128],
                                start=st_, stop=sp_, skip_group_check=True,
                                inc=(('T', 1) if sp_ else None))
                tav = self.val('T')
                tav_prev = tav
                if self.sub < 47:
                    continue
                for hh, base in ((hA, 0), (hB, 512)):
                    self.op('V', 'reciprocal', drl[0:1, :], pb1[64:65, base:base + 512],
                            wait=[('T', tav)] if base == 0 else None, inc=('V', 1))
                    vr1 = self.vfence()
                    if self.sub < 48:
                        continue
                    self.op('T', 'matmul', pb7[0:64, 0:512], self.ones1b[0:1, 0:64],
                            drl[0:1, :], start=True, stop=True,
                            wait=[('V', vr1)], inc=('T', 1))
                    tb1 = self.val('T')
                    self.op('V', 'tensor_copy', dnl[:, :], pb7[0:64, 0:512],
                            wait=[('T', tb1)])
                    if self.sub < 49:
                        continue
                    self.op('V', 'tensor_tensor',
                            Ow[(hh % 2) * 64:(hh % 2) * 64 + 64, pc * 512:(pc + 1) * 512],
                            pb1[0:64, base:base + 512], dnl[:, :], op=ALU.mult,
                            inc=('V', 1))
                self.wait('T', 'V', self.vfence())

            if self.sub < 50:
                vwin_prev = self.val('V')
                continue
            for oc in range(NCH):
                pbank = pb3 if oc % 2 == 0 else pb1
                for pc in range(4):
                    self.op('T', 'matmul', pbank[:, 0:512],
                            wo_sb[pc][:, oc * 128:(oc + 1) * 128],
                            Ow[:, pc * 512:(pc + 1) * 512],
                            start=(pc == 0), stop=(pc == 3),
                            inc=(('T', 1) if pc == 3 else None))
                tyw = self.val('T')
                wv = [('T', tyw)]
                if oc == 0 and zast_prev is not None:
                    wv += zast_prev
                if l == 0:
                    self.op('V', 'tensor_tensor', tmp1[:, :],
                            zw3[:, oc, 1:513], A0w[:, 1:513], op=ALU.mult, wait=wv)
                    self.op('V', 'tensor_tensor', tmp1[:, :],
                            tmp1[:, :], B0w[:, 1:513], op=ALU.add)
                    self.op('V', 'tensor_tensor', za3[:, oc, 1:513],
                            pbank[:, 0:512], tmp1[:, :], op=ALU.add, inc=('V', 1))
                else:
                    self.op('V', 'tensor_tensor', za3[:, oc, 1:513],
                            pbank[:, 0:512], zw3[:, oc, 1:513], op=ALU.add,
                            wait=wv, inc=('V', 1))
                self.wait('T', 'V', self.vfence())
            if w == 0 or self.sub == 61:
                self.op('V', 'memset', za3[:, :, 0:1], 0.0, inc=('V', 1))
            else:
                self.op('V', 'tensor_copy', za3[:, :, 0:1],
                        zhalo.rearrange("p (c n) -> p c n", n=1)[:, :, :], inc=('V', 1))
            if self.sub != 61:
                self.op('V', 'tensor_copy', zhalo.rearrange("p (c n) -> p c n", n=1)[:, :, :],
                        za3[:, :, 512:513], inc=('V', 1))
            else:
                self.op('V', 'nop', inc=('V', 1))
            vza = self.vfence()
            zast_prev = [self.dmaS(Rout[:, c, w * 512:(w + 1) * 512],
                                   za3[:, c, 1:513],
                                   wait=[('V', vza)] if c == 0 else None)
                         for c in range(NCH)]
            zstore += zast_prev
            if self.sub == 62:
                tstats_prev = self.val('T')
                vwin_prev = self.val('V')
                continue
            wzb = [('T', tstats_prev)] if tstats_prev else None
            self.op('V', 'tensor_copy', zab[:, :], za[:, :], wait=wzb)
            self.op('V', 'tensor_tensor', za2[:, :], zab[:, :], zab[:, :],
                    op=ALU.mult, inc=('V', 1))
            vst = self.vfence()
            self.wait('T', 'V', vst)
            for qty, (src3, off, cb) in enumerate(
                    [(zab, 1, 64), (zab, 0, 96), (za2, 1, 128), (za2, 0, 160)]):
                s3 = src3.rearrange("p (c n) -> p c n", n=513)
                crange = range(0, 4) if qty in (0, 2) else range(4, 8)
                for ms in range(4):
                    for ci, c in enumerate(crange):
                        st_ = stats_first[0]
                        stats_first[0] = False
                        bend = (w == 7 and qty == 3 and ms == 3 and ci == 3)
                        lastq = (qty == 3 and ms == 3 and ci == 3)
                        self.op('T', 'matmul',
                                pb0[:, cb + w * 4 + ms:cb + w * 4 + ms + 1],
                                s3[:, c, off + ms * 128:off + (ms + 1) * 128],
                                self.onesb[:, :], start=st_, stop=bend,
                                skip_group_check=True,
                                inc=(('T', 1) if lastq else None))
            tstats_prev = self.val('T')
            vwin_prev = self.val('V')

        if self.sub < 5 or self.nwin < 8:
            stF.close()
            stR.close()
            return
        self.op('V', 'tensor_copy', scmb[:, 0:32], pb0[:, 96:128],
                wait=[('T', tstats_prev)])
        self.op('V', 'tensor_copy', scmb[:, 32:64], pb0[:, 160:192])
        self.op('V', 'tensor_tensor', scmb[:, 0:32], scmb[:, 0:32], pb0[:, 64:96],
                op=ALU.add)
        self.op('V', 'tensor_tensor', scmb[:, 32:64], scmb[:, 32:64], pb0[:, 128:160],
                op=ALU.add, inc=('V', 1))
        rows2 = self._ln_rows(f'ln2_{l}', scmb, 0, 32, a2row, b2row,
                              ('V', self.val('V')), st=stF)
        if self.stage == 'F0q':
            stF.close()
            stR.close()
            return
        stF.close()

        stD = contextlib.ExitStack()
        dsb = lambda nm, shp, dt: self.sbuf(f'L{l}d{nm}', shp, dt, st=stD)
        gate2 = [('T', self.val('T')), ('V', self.val('V')),
                 ('S', self.val('S'))] + rows2 + self.allw('LS')
        w1s = [dsb(f'w1_{c}', [128, 5460], BF16) for c in range(NCH)]
        w2rows = [128] * 21 + [42]
        w2s = [dsb(f'w2_{j}', [w2rows[j], 1024], BF16) for j in range(NUC)]
        tw1 = []
        for c in range(NCH):
            tw1.append(self.dmaG(w1s[c][:, :], g[f'w1_{l}'][c * 128:(c + 1) * 128, :],
                                 wait=gate2 if c == 0 else None))
        for j in range(NUC):
            self.dmaG(w2s[j][:, :], g[f'w2_{l}'][j * 128:j * 128 + w2rows[j], :])
        wload2 = self.allw('LG')
        if self.sub == 74:
            self.waitl('G', wload2)
            stD.close()
            stR.close()
            return
        for c in range(NCH):
            self.op('V', 'tensor_scalar', w1s[c][:, :], w1s[c][:, :],
                    lgc[:, c:c + 1], None, op0=ALU.mult,
                    wait=[tw1[c], tlg], inc=('V', 1))
        vw1 = self.vfence()
        self.wait('T', 'V', vw1)
        if self.sub == 73:
            stD.close()
            stR.close()
            return
        a_cols = [(j * 128, min(2730, (j + 1) * 128)) for j in range(NUC)]
        g_cols = [(2730 + a, 2730 + b) for (a, b) in a_cols]
        for j in range(2 * NUC):
            lo, hi = (a_cols + g_cols)[j]
            m = hi - lo
            for c in range(NCH):
                self.op('T', 'matmul', pb7[0:m, j:j + 1], w1s[c][:, lo:hi],
                        lbb[:, c:c + 1], start=(c == 0), stop=(c == NCH - 1),
                        skip_group_check=True,
                        inc=(('T', 1) if c == NCH - 1 and j == 2 * NUC - 1 else None))
        tlb = self.val('T')
        b1l = self.dmaY(b1t[:, :], g[f'b1col{l}'])
        self.op('V', 'tensor_tensor', b1t[:, :], b1t[:, :], pb7[:, 0:44],
                op=ALU.add, wait=[('T', tlb), b1l], inc=('V', 1))
        self.wait('T', 'V', self.vfence())

        if self.sub == 71:
            stD.close()
            stR.close()
            return
        zsp = [dsb(f'zsp{i}', [128, 8 * 257], F32) for i in range(2)]
        zsp3 = [t.rearrange("p (c n) -> p c n", n=257) for t in zsp]
        x1sp = [dsb(f'x1sp{i}', [128, 8 * 256], F32) for i in range(2)] if l == 1 else None
        h2 = dsb('h2', [128, 8 * 256], BF16)
        A2w = dsb('A2w', [128, 256], F32)
        B2w = dsb('B2w', [128, 256], F32)
        usb = [dsb(f'u{i}', [128, 256], BF16) for i in range(2)]
        gel = dsb('gel', [128, 256], F32)
        gel_b = dsb('gelb', [128, 256], F32)
        zot = [dsb(f'zot{i}', [128, 256], F32) for i in range(2)]
        tmpd = dsb('tmpd', [128, 256], F32)

        NSP = {70: 2, 72: 1}.get(self.sub, 16)
        vsp_done = [0, 0]
        out_prev = {0: None, 1: None}
        store_out = []
        for sp in range(NSP):
            cur = sp % 2
            wl = list(zstore)
            if vsp_done[cur]:
                wl.append(('V', vsp_done[cur]))
            if sp == 0:
                wl += gate2 + wload2
                tz = [self.dmaY(zsp3[cur][:, c, 1:257], Rout[:, c, 0:256],
                                wait=wl if c == 0 else None) for c in range(NCH)]
                self.op('V', 'memset', zsp3[cur][:, :, 0:1], 0.0, wait=tz, inc=('V', 1))
            else:
                tz = [self.dmaY(zsp3[cur][:, c, :],
                                Rout[:, c, sp * 256 - 1:(sp + 1) * 256],
                                wait=wl if c == 0 else None) for c in range(NCH)]
            if l == 1:
                wx = [('V', vsp_done[cur])] if vsp_done[cur] else []
                if sp == 0:
                    wx += gate2
                x1v = x1sp[cur].rearrange("p (c n) -> p c n", n=256)
                tx1 = [self.dmaY(x1v[:, c, :],
                                 self.rr['R1'][:, c, sp * 256:(sp + 1) * 256],
                                 wait=(wx or None) if c == 0 else None)
                       for c in range(NCH)]
            vprev = self.vfence()
            self.op('T', 'matmul', pb7[:, 0:256], self.ones1b[:, :],
                    a2row[0:1, sp * 256:(sp + 1) * 256], start=True, stop=True,
                    wait=[('V', vprev)] + (rows2 if sp == 0 else []))
            self.op('T', 'matmul', pb7[:, 256:512], self.ones1b[:, :],
                    b2row[0:1, sp * 256:(sp + 1) * 256], start=True, stop=True,
                    inc=('T', 1))
            tb2 = self.val('T')
            self.op('V', 'tensor_copy', A2w[:, :], pb7[:, 0:256], wait=[('T', tb2)])
            self.op('V', 'tensor_copy', B2w[:, :], pb7[:, 256:512], inc=('V', 1))
            self.wait('T', 'V', self.vfence())
            for c in range(NCH):
                off = 0 if c >= 4 else 1
                wz = tz if c == 0 else None
                self.op('V', 'tensor_tensor', tmpd[:, :], zsp3[cur][:, c, off:off + 256],
                        A2w[:, :], op=ALU.mult, wait=wz)
                self.op('V', 'tensor_tensor', h2[:, c * 256:(c + 1) * 256],
                        tmpd[:, :], B2w[:, :], op=ALU.add,
                        inc=(('V', 1) if c == NCH - 1 else None))
            vh2 = self.vfence()
            self.wait('T', 'V', vh2)
            gel2 = [gel, gel_b]
            vus = {}
            def emit_ag(j):
                pag = pb5 if j % 2 == 0 else pb6
                alo, ahi = a_cols[j]
                glo, ghi = g_cols[j]
                m = ahi - alo
                for c in range(NCH):
                    self.op('T', 'matmul', pag[0:m, 0:256], w1s[c][:, alo:ahi],
                            h2[:, c * 256:(c + 1) * 256],
                            start=(c == 0), stop=(c == NCH - 1),
                            skip_group_check=True)
                    self.op('T', 'matmul', pag[0:m, 256:512], w1s[c][:, glo:ghi],
                            h2[:, c * 256:(c + 1) * 256],
                            start=False, stop=(c == NCH - 1),
                            skip_group_check=True,
                            inc=(('T', 1) if c == NCH - 1 else None))
                tag = self.val('T')
                self.op('S', 'activation', gel2[j % 2][0:m, :], pag[0:m, 256:512], AF.Gelu,
                        bias=b1t[0:m, 22 + j:23 + j], wait=[('T', tag)], inc=('S', 1))
                sgel = self.sfence()
                self.op('V', 'scalar_tensor_tensor', usb[j % 2][0:m, :],
                        pag[0:m, 0:256], b1t[0:m, j:j + 1], gel2[j % 2][0:m, :],
                        op0=ALU.add, op1=ALU.mult, wait=[('S', sgel)], inc=('V', 1))
                vus[j] = self.vfence()
            def emit_w2(j):
                m = a_cols[j][1] - a_cols[j][0]
                self.wait('T', 'V', vus[j])
                for oc2 in range(NCH):
                    pob = pb1 if oc2 < 4 else pb3
                    colo = (oc2 % 4) * 256
                    self.op('T', 'matmul', pob[:, colo:colo + 256],
                            w2s[j][:, oc2 * 128:(oc2 + 1) * 128], usb[j % 2][0:m, :],
                            start=(j == 0 and oc2 % 2 == 0), stop=(j == NUC - 1),
                            skip_group_check=True,
                            inc=(('T', 1) if oc2 == NCH - 1 else None))
            emit_ag(0)
            for j in range(1, NUC):
                emit_ag(j)
                emit_w2(j - 1)
            emit_w2(NUC - 1)
            tw2d = self.val('T')
            for oc2 in range(NCH):
                pob = pb1 if oc2 < 4 else pb3
                colo = (oc2 % 4) * 256
                zb = zot[oc2 % 2]
                wv = [('T', tw2d)] if oc2 == 0 else []
                if oc2 == 0 and sp == 0:
                    wv.append(tb2c)
                if out_prev.get(oc2 % 2) is not None:
                    wv.append(out_prev[oc2 % 2])
                self.op('V', 'scalar_tensor_tensor', zb[:, :],
                        pob[:, colo:colo + 256], b2c[:, oc2:oc2 + 1],
                        zsp3[cur][:, oc2, 1:257], op0=ALU.add, op1=ALU.add,
                        wait=wv or None)
                if l == 1:
                    self.op('V', 'tensor_tensor', zb[:, :], zb[:, :],
                            x1sp[cur][:, oc2 * 256:(oc2 + 1) * 256], op=ALU.add,
                            wait=tx1 if oc2 == 0 else None)
                vzo = self.vfence()
                tkst = self.dmaS(Rz[:, oc2, sp * 256:(sp + 1) * 256], zb[:, :],
                                 wait=[('V', vzo)])
                out_prev[oc2 % 2] = tkst
                store_out.append(tkst)
            self.wait('T', 'V', self.vfence())
            vsp_done[cur] = self.val('V')
        self.ffn_gate = [('T', self.val('T')), ('V', self.val('V')),
                         ('S', self.val('S'))] + store_out
        stD.close()
        stR.close()

    def _finish(self):
        for att in ('stA', 'stH0'):
            st = getattr(self, att, None)
            if st is not None:
                st.close()
        toks = self.allw('LS') + self.allw('LG') + self.allw('LH')
        self.waitl('G', toks)


_PROG_CACHE = {}


def _build(stage, nwin=8, sub=50):
    mk = MK(stage, nwin, sub)
    nc = mk.build()
    return nc


def kernel(**inputs):
    sh, cores = host_prep(inputs)
    nc = _build(BUILD_STAGE)
    in_maps = []
    for b in range(8):
        m = dict(sh)
        m.update(cores[b])
        in_maps.append(m)
    res = run_bass_kernel_spmd(nc, in_maps, core_ids=list(range(8)))
    outs = []
    key = 'R1' if BUILD_STAGE == 'A' else 'outT'
    for r in res.results:
        outs.append(np.ascontiguousarray(r[key].T))
    return np.stack(outs, axis=0)


# version 08
# speedup vs baseline: 1.2116x; 1.2116x over previous
import contextlib
import numpy as np
import ml_dtypes
import concourse.bass as bass
import concourse.mybir as mybir
from concourse.bass_utils import run_bass_kernel_spmd
from concourse.alu_op_type import AluOpType as ALU

F32 = mybir.dt.float32
BF16 = mybir.dt.bfloat16
AF = mybir.ActivationFunctionType

N = 4096
D = 1024
NCH = 8
NQB = 8
E = 64
NGRP = 4
HID = 2730
NUC = 22
BUILD_STAGE = 'full'


def _rotary_tables():
    half = np.arange(0, 64, 2, dtype=np.float64)
    inv_freq = 1.0 / 10000.0 ** (half / 64.0)
    t = np.arange(N, dtype=np.float64)
    freqs = np.concatenate([t[:, None] * inv_freq[None, :]] * 2, axis=-1)
    base = (half + 0.4 * 64.0) / (1.4 * 64.0)
    power = (t - N // 2) / 256.0
    scale = np.concatenate([base[None, :] ** power[:, None]] * 2, axis=-1)
    cos = np.cos(freqs); sin = np.sin(freqs)
    sgn = np.concatenate([-np.ones(32), np.ones(32)])[None, :]
    def pack(tab):
        return np.ascontiguousarray(np.concatenate([tab.T, tab.T], axis=0), np.float32)
    return (pack(cos * scale), pack(sin * scale * sgn),
            pack(cos / scale), pack(sin / scale * sgn))


def _rot_perm(sv):
    P = np.zeros((128, 128), dtype=np.float32)
    for half in (0, 1):
        for ep in range(64):
            e = (ep + 32) % 64
            P[half * 64 + e, half * 64 + ep] = sv[e]
    return P


def _col_tiles(v, width):
    out = np.zeros((128, width), dtype=np.float32)
    nfull = len(v) // 128
    for c in range(nfull):
        out[:, c] = v[c * 128:(c + 1) * 128]
    rem = len(v) - nfull * 128
    if rem:
        out[:rem, nfull] = v[nfull * 128:]
    return np.ascontiguousarray(out)


def host_prep(ins):
    f = np.float32
    sh = {}
    Wq, Wk, Wv = ins['L0_attn_Wq'], ins['L0_attn_Wk'], ins['L0_attn_Wv']
    sh['WqT'] = np.ascontiguousarray(np.asarray(Wq, f).transpose(1, 0, 2).reshape(D, 1024))
    sh['WkT'] = np.ascontiguousarray(np.asarray(Wk, f).transpose(1, 0, 2).reshape(D, 1024))
    sh['WvT'] = np.ascontiguousarray(np.asarray(Wv, f).transpose(1, 0, 2).reshape(D, 1024))
    sh['Wpp'] = np.ascontiguousarray(
        np.asarray(ins['L0_attn_Wp'], f).reshape(64, 16, 1024).transpose(1, 0, 2).reshape(1024, 1024))
    sh['bp_col'] = np.ascontiguousarray(np.asarray(ins['L0_attn_bp'], f).reshape(8, 128).T)
    for l in range(2):
        sh[f'wqkv{l}'] = np.ascontiguousarray(np.asarray(ins['L0_ffn_wqkv'][l], f))
        sh[f'wo{l}'] = np.ascontiguousarray(np.asarray(ins['L0_ffn_wo'][l], f))
        sh[f'w1_{l}'] = np.ascontiguousarray(np.asarray(ins['L0_ffn_w1'][l], f))
        sh[f'w2_{l}'] = np.ascontiguousarray(np.asarray(ins['L0_ffn_w2'][l], f))
        sh[f'b1col{l}'] = _col_tiles(np.asarray(ins['L0_ffn_b1'][l], f), 44)
        sh[f'b2col{l}'] = np.ascontiguousarray(np.asarray(ins['L0_ffn_b2'][l], f).reshape(8, 128).T)
        sh[f'lgcol{l}'] = np.ascontiguousarray(np.asarray(ins['L0_ffn_ln_g'][l], f).reshape(8, 128).T)
        sh[f'lbcol{l}'] = np.ascontiguousarray(np.asarray(ins['L0_ffn_ln_b'][l], f).reshape(8, 128).T)
        qs = np.asarray(ins['L0_ffn_q_scale'][l], f)
        ks = np.asarray(ins['L0_ffn_k_scale'][l], f)
        sh[f'qs_t{l}'] = np.ascontiguousarray(np.tile(qs, 2)[:, None])
        sh[f'ks_t{l}'] = np.ascontiguousarray(np.tile(ks, 2)[:, None])
        sh[f'PQ{l}'] = _rot_perm(qs)
        sh[f'PK{l}'] = _rot_perm(ks)
    sh['TAc'], sh['TAs'], sh['TBc'], sh['TBs'] = _rotary_tables()
    pm = np.arange(128)[:, None]; pq = np.arange(128)[None, :]
    sh['M0'] = np.ascontiguousarray(np.where(pq >= pm, 0.0, -1e30).astype(f))
    sh['M512'] = np.ascontiguousarray(np.where(pq <= pm, 0.0, -1e30).astype(f))
    sh['identb'] = np.ascontiguousarray(np.eye(128, dtype=ml_dtypes.bfloat16))
    E2 = np.zeros((2, 128), dtype=ml_dtypes.bfloat16)
    E2[0, 0:64] = 1; E2[1, 64:128] = 1
    sh['E2h'] = E2
    x = ins['L0_attn_query']
    cores = [{'xT': np.ascontiguousarray(np.asarray(x[b], f).T)} for b in range(8)]
    return sh, cores


class MK:
    def __init__(self, stage='full', nwin=8, sub=50):
        self.nwin = nwin
        self.sub = sub
        self.stage = stage
        self.nc = bass.Bass("TRN2", target_bir_lowering=False, debug=False)
        self.es = contextlib.ExitStack()
        self.q = {'T': [], 'V': [], 'S': [], 'G': [], 'Y': []}
        self.cnt = {}

    def sbuf(self, name, shape, dtype, st=None):
        return (st or self.es).enter_context(self.nc.sbuf_tensor(name, shape, dtype))

    def psum(self, name, shape, dtype=F32):
        return self.es.enter_context(self.nc.psum_tensor(name, shape, dtype))

    def sem(self, name):
        h = self.es.enter_context(self.nc.semaphore(name))
        self.cnt[name] = 0
        self.semh = getattr(self, 'semh', {})
        self.semh[name] = h
        return h

    def emit(self, eng, fn):
        self.q[eng].append(fn)

    def bump(self, sem, k=1):
        self.cnt[sem] += k
        return self.cnt[sem]

    def val(self, sem):
        return self.cnt[sem]

    def op(self, eng, meth, *a, inc=None, wait=None, **kw):
        if inc is not None:
            incv = [(s, k) for (s, k) in ([inc] if isinstance(inc, tuple) else inc) if s]
        else:
            incv = []
        wait = list(wait) if wait else []
        if eng in ('V', 'S') and meth != 'dma_start':
            wait.append((eng, self.cnt[eng]))
            incv.append((eng, 1))
        merged = {}
        for sm, k in incv:
            merged[sm] = merged.get(sm, 0) + k
        incv = list(merged.items())
        for sm, k in incv:
            self.bump(sm, k)
        semh = self.semh
        def fn(e):
            for sm, v in wait:
                e.wait_ge(semh[sm], v)
            instr = getattr(e, meth)(*a, **kw)
            for sm, k in incv:
                instr.then_inc(semh[sm], k)
        self.emit(eng, fn)

    def wait(self, eng, sem, v):
        semh = self.semh
        self.emit(eng, lambda e: e.wait_ge(semh[sem], v))

    def vfence(self):
        self.op('V', 'tensor_copy', self.fsc[0:1, 1:2], self.fsc[0:1, 0:1])
        return self.cnt['V']

    def sfence(self):
        self.op('S', 'copy', self.fsc[0:1, 3:4], self.fsc[0:1, 2:3])
        return self.cnt['S']

    def waitl(self, eng, tokens):
        semh = self.semh
        toks = list(tokens)
        def fn(e):
            for sm, v in toks:
                e.wait_ge(semh[sm], v)
        self.emit(eng, fn)

    NLANES = 4

    def _dma(self, eng, queue, dst, src, wait=None, accum=False):
        st = self.lanes[queue]
        lane = st['rr']
        st['rr'] = (lane + 1) % self.NLANES
        name = f'{queue}{lane}'
        prev = self.cnt[name]
        wait = list(wait) if wait else []
        if prev > 0:
            wait.append((name, prev))
        kw = {'out': dst, 'in_': src}
        if accum:
            kw['accum_op'] = ALU.add
        self.op(eng, 'dma_start', inc=(name, 16), wait=wait, **kw)
        return (name, self.cnt[name])

    def dmaY(self, dst, src, wait=None):
        return self._dma('Y', 'LH', dst, src, wait)

    def dmaS(self, dst, src, wait=None):
        return self._dma('S', 'LS', dst, src, wait)

    def dmaG(self, dst, src, wait=None, accum=False):
        return self._dma('G', 'LG', dst, src, wait, accum)

    def allw(self, queue):
        return [(f'{queue}{i}', self.cnt[f'{queue}{i}'])
                for i in range(self.NLANES) if self.cnt[f'{queue}{i}'] > 0]

    def build(self):
        nc = self.nc
        with self.es:
            self._declare_dram()
            self._globals()
            blk = self.es.enter_context(nc.Block())
            self._stage_a()
            if self.stage in ('F0', 'F0q'):
                self._ffn_layer(0)
            elif self.stage == 'full':
                self._ffn_layer(0)
                self._ffn_layer(1)
            self._finish()
            with nc.allow_low_precision(reason="bf16 matmul operands by design"):
                for tag, attr in [('T', 'tensor'), ('V', 'vector'), ('S', 'scalar'),
                                  ('G', 'gpsimd'), ('Y', 'sync')]:
                    fns = self.q[tag]
                    def body(eng, fns=fns):
                        for fn in fns:
                            fn(eng)
                    getattr(blk, attr)(body)
        return nc

    def _declare_dram(self):
        nc = self.nc
        g = self.g = {}
        def d(name, shape, dtype=F32, kind='ExternalInput'):
            g[name] = nc.dram_tensor(name, shape, dtype, kind=kind).ap()
        d('xT', [D, N])
        for k, shp in [('WqT', [D, 1024]), ('WkT', [D, 1024]), ('WvT', [D, 1024]),
                       ('Wpp', [1024, 1024]), ('bp_col', [128, 8]),
                       ('TAc', [128, N]), ('TAs', [128, N]), ('TBc', [128, N]),
                       ('TBs', [128, N]), ('M0', [128, 128]), ('M512', [128, 128])]:
            d(k, shp)
        d('identb', [128, 128], BF16)
        d('E2h', [2, 128], BF16)
        for l in range(2):
            for k, shp in [(f'wqkv{l}', [D, 1536]), (f'wo{l}', [512, 1024]),
                           (f'w1_{l}', [D, 5460]), (f'w2_{l}', [HID, 1024]),
                           (f'b1col{l}', [128, 44]), (f'b2col{l}', [128, 8]),
                           (f'lgcol{l}', [128, 8]), (f'lbcol{l}', [128, 8]),
                           (f'qs_t{l}', [128, 1]), (f'ks_t{l}', [128, 1]),
                           (f'PQ{l}', [128, 128]), (f'PK{l}', [128, 128])]:
                d(k, shp)
        d('yacc', [D, N], kind='ExternalOutput' if self.stage in ('Y', 'Y3') else 'Internal')
        d('rowsc', [2, 4096], kind='Internal')
        d('R1', [D, N], kind='ExternalOutput' if self.stage == 'A' else 'Internal')
        if self.stage == 'A0':
            d('dbgA0', [128, 64 + N], F32, 'ExternalOutput')
        if self.stage == 'A1':
            d('dbgA1', [128, 2 * N], BF16, 'ExternalOutput')
        if self.stage == 'KV':
            d('dbgKV', [128, 2 * N + 32 * 4 * 65], BF16, 'ExternalOutput')
            d('dbgW', [128, 8 * 256 * 2], BF16, 'ExternalOutput')
        if self.stage == 'X1':
            d('dbgX', [128, 4 * 512], F32, 'ExternalOutput')
        if self.stage == 'CAN':
            d('dbgC', [128, 32 * 16], F32, 'ExternalOutput')
        if self.stage in ('QOST', 'QOST3'):
            d('dbgQ', [128, 2 * 512 + 1024 + 2 * 1024], BF16, 'ExternalOutput')
        d('RA0', [D, N], kind='ExternalOutput' if self.stage == 'F0q' else 'Internal')
        d('RZ1', [D, N], kind='ExternalOutput' if self.stage == 'F0' else 'Internal')
        d('RA1', [D, N], kind='Internal')
        if self.stage != 'A':
            d('outT', [D, N], F32, 'ExternalOutput')
        self.rr = {k: g[k].rearrange("(c p) n -> p c n", p=128)
                   for k in ['xT', 'yacc', 'R1', 'RA0', 'RZ1', 'RA1']
                   + (['outT'] if self.stage != 'A' else [])}

    def _globals(self):
        for s in ['T', 'V', 'S']:
            self.sem(s)
        self.lanes = {}
        for q in ['LH', 'LS', 'LG']:
            self.lanes[q] = {'rr': 0}
            for i in range(self.NLANES):
                self.sem(f'{q}{i}')
        self.onesb = self.sbuf('onesb', [128, 1], BF16)
        self.ones1b = self.sbuf('ones1b', [1, 128], BF16)
        self.bd = self.sbuf('bd', [128, 2], BF16)
        self.E2 = self.sbuf('E2', [2, 128], BF16)
        self.identb_sb = self.sbuf('identb_sb', [128, 128], BF16)
        self.bp_sb = self.sbuf('bp_sb', [128, 8], F32)
        self.M0_sb = self.sbuf('M0_sb', [128, 128], F32)
        self.M512_sb = self.sbuf('M512_sb', [128, 128], F32)
        self.cns = self.sbuf('cns', [128, 4], F32)
        self.fsc = self.sbuf('fsc', [1, 4], F32)
        if self.stage == 'CAN':
            self.cansb = self.sbuf('cansb', [128, 32 * 16], F32)

    def canary(self, idx):
        if self.stage != 'CAN' or idx >= 16:
            return
        if not hasattr(self, '_can_init'):
            self.op('V', 'memset', self.cansb[:, :], -7.0)
            self._can_init = True
        c = self.cansb[:, idx * 32:idx * 32 + 32]
        self.op('V', 'tensor_copy', c[:, 0:8], self.bp_sb[:, :], wait=[('V', self.val('V'))])
        self.op('V', 'tensor_copy', c[:, 8:9], self.onesb[:, :])
        self.op('V', 'tensor_copy', c[:, 9:10], self.ones1b[0:1, :].rearrange("o (a b) -> o a b", b=1)[:, 0, :] if False else self.cns[:, 0:1])
        self.op('V', 'tensor_copy', c[:, 10:26], self.identb_sb[:, 0:16], inc=('V', 1))

    def _load_consts(self):
        g = self.g
        for dst, src in [(self.bp_sb, g['bp_col']), (self.M0_sb, g['M0']),
                         (self.M512_sb, g['M512']), (self.identb_sb, g['identb']),
                         (self.E2, g['E2h'])]:
            self.dmaY(dst[:, :], src)
        w = self.allw('LH')
        for t, v in [(self.onesb, 1.0), (self.ones1b, 1.0)]:
            self.op('V', 'memset', t[:, :], v)
        self.op('V', 'memset', self.bd[:, :], 0.0)
        self.op('V', 'memset', self.bd[0:64, 0:1], 1.0)
        self.op('V', 'memset', self.bd[64:128, 1:2], 1.0)
        self.op('V', 'memset', self.fsc[:, :], 0.0)
        self.op('V', 'memset', self.cns[:, 0:1], 1e-5)
        self.op('V', 'memset', self.cns[:, 1:2], 1e-24)
        self.op('V', 'memset', self.cns[:, 2:3], 2.079441542, inc=('V', 1))
        self.consts_v = self.val('V')
        self.consts_lh = w

    def _ln_rows(self, tag, psums, cS, cS2, arow, brow, wait_t, st=None):
        mc = self.sbuf(f'{tag}_mc', [128, 32], F32, st=st)
        vr = self.sbuf(f'{tag}_vr', [128, 32], F32, st=st)
        rc = self.sbuf(f'{tag}_rc', [128, 32], F32, st=st)
        ab = self.sbuf(f'{tag}_ab', [128, 64], BF16, st=st)
        tr = self.sbuf(f'{tag}_tr', [32, 256], BF16, st=st)
        ptr = self.ptr_bf
        self.op('V', 'tensor_scalar', mc[:, :], psums[:, cS:cS + 32], 1.0 / D, None,
                op0=ALU.mult, wait=[wait_t])
        self.op('V', 'tensor_scalar', vr[:, :], psums[:, cS2:cS2 + 32], 1.0 / D, None, op0=ALU.mult)
        self.op('V', 'tensor_tensor', rc[:, :], mc[:, :], mc[:, :], op=ALU.mult)
        self.op('V', 'tensor_tensor', vr[:, :], vr[:, :], rc[:, :], op=ALU.subtract,
                inc=('V', 1))
        v1 = self.vfence()
        self.op('S', 'activation', vr[:, :], vr[:, :], AF.Ln, bias=self.cns[:, 0:1], wait=[('V', v1)])
        self.op('S', 'activation', rc[:, :], vr[:, :], AF.Exp, scale=-0.5, inc=('S', 1))
        s1 = self.sfence()
        self.op('V', 'tensor_copy', ab[:, 0:32], rc[:, :], wait=[('S', s1)])
        self.op('V', 'scalar_tensor_tensor', ab[:, 32:64], mc[:, :], -1.0, rc[:, :],
                op0=ALU.mult, op1=ALU.mult, inc=('V', 1))
        v2 = self.vfence()
        self.op('T', 'transpose', ptr[0:32, 0:128], ab[:, 0:32], self.identb_sb[:, :],
                wait=[('V', v2)])
        self.op('T', 'transpose', ptr[0:32, 128:256], ab[:, 32:64], self.identb_sb[:, :],
                inc=('T', 1))
        t1 = self.val('T')
        self.op('V', 'tensor_copy', tr[:, 0:128], ptr[0:32, 0:128], wait=[('T', t1)])
        self.op('V', 'tensor_copy', tr[:, 128:256], ptr[0:32, 128:256], inc=('V', 1))
        v3 = self.vfence()
        rs = self.g['rowsc']
        rsa = rs[0:1, :].rearrange("o (r n) -> (o r) n", n=128)
        rsb = rs[1:2, :].rearrange("o (r n) -> (o r) n", n=128)
        ta = self.dmaG(rsa, tr[:, 0:128], wait=[('V', v3)])
        tb = self.dmaG(rsb, tr[:, 128:256])
        t3 = self.dmaG(arow[0:1, :], rs[0:1, :], wait=[ta])
        t4 = self.dmaG(brow[0:1, :], rs[1:2, :], wait=[tb])
        return [t3, t4]

    def _stage_a(self):
        g = self.g
        self._load_consts()
        self.pb0 = self.psum('pb0', [128, 512])
        self.pb1 = self.psum('pb1', [128, 1024])
        self.pb3 = self.psum('pb3', [128, 1024])
        self.pb5 = self.psum('pb5', [128, 512])
        self.pb6 = self.psum('pb6', [128, 512])
        self.pb7 = self.psum('pb7', [128, 512])
        self.ptr_bf = self.pb7.bitcast(BF16)
        pb0, pb1, pb3 = self.pb0, self.pb1, self.pb3
        pOa, pOb, pm = self.pb5, self.pb6, self.pb7

        self.stH0 = contextlib.ExitStack()
        arow0 = self.sbuf('arow0', [1, N], BF16, st=self.stH0)
        brow0 = self.sbuf('brow0', [1, N], BF16, st=self.stH0)
        self.arow0, self.brow0 = arow0, brow0
        stA = self.stA = contextlib.ExitStack()
        h = [self.sbuf(f'h{c}', [128, N], BF16, st=stA) for c in range(NCH)]
        sub = contextlib.ExitStack()
        xspan = [self.sbuf(f'xspan{i}', [128, 8 * 512], F32, st=sub) for i in range(2)]
        xb = [self.sbuf(f'xbspan{i}', [128, 8 * 512], BF16, st=sub) for i in range(2)]
        x2 = [self.sbuf(f'x2span{i}', [128, 8 * 512], BF16, st=sub) for i in range(2)]
        arow = self.sbuf('arowA', [1, N], BF16, st=sub)
        brow = self.sbuf('browA', [1, N], BF16, st=sub)

        vcopy_done = [0] * 8
        tsum_done = [0] * 8
        for s in range(8):
            b = s % 2
            wl = [('V', vcopy_done[s - 2])] if s >= 2 else None
            lh = self.dmaY(xspan[b].rearrange("p (c n) -> p c n", n=512)[:, :, :],
                           self.rr['xT'][:, :, s * 512:(s + 1) * 512], wait=wl)
            wv = ([lh] if s > 0 else self.allw('LH')) + \
                 ([('T', tsum_done[s - 2])] if s >= 2 else [])
            self.op('V', 'tensor_copy', xb[b][:, :], xspan[b][:, :], wait=wv)
            self.op('V', 'tensor_tensor', x2[b][:, :], xb[b][:, :], xb[b][:, :],
                    op=ALU.mult, inc=('V', 1))
            vcopy_done[s] = self.vfence()
            self.wait('T', 'V', vcopy_done[s])
            for c in range(NCH):
                for ms in range(4):
                    st = (s == 0 and c == 0 and ms == 0)
                    last = (c == NCH - 1 and ms == 3)
                    burst_end = (s == 7 and last)
                    self.op('T', 'matmul', pb0[:, s * 4 + ms:s * 4 + ms + 1],
                            xb[b][:, c * 512 + ms * 128: c * 512 + (ms + 1) * 128],
                            self.onesb[:, :], start=st, stop=False, skip_group_check=True)
                    self.op('T', 'matmul', pb0[:, 32 + s * 4 + ms:33 + s * 4 + ms],
                            x2[b][:, c * 512 + ms * 128: c * 512 + (ms + 1) * 128],
                            self.onesb[:, :], start=False, stop=burst_end, skip_group_check=True,
                            inc=(('T', 1) if last else None))
            tsum_done[s] = self.val('T')
        rows_done = self._ln_rows('lnA', pb0, 0, 32, arow, brow, ('T', self.val('T')), st=sub)
        if self.stage == 'A0':
            dbg = self.g['dbgA0']
            sums_sb = self.sbuf('dbg_sums', [128, 64], F32, st=sub)
            self.op('V', 'tensor_copy', sums_sb[:, :], pb0[:, 0:64],
                    wait=[rows_done], inc=('V', 1))
            vv = self.val('V')
            self.dmaS(dbg[0:128, 0:64], sums_sb[:, :], wait=[('V', vv)])
            self.dmaS(dbg[0:1, 64:64 + N // 2].bitcast(BF16), arow[0:1, :])
            self.dmaS(dbg[1:2, 64:64 + N // 2].bitcast(BF16), brow[0:1, :])
            sub.close()
            return

        hbuild_done = [0] * 8
        for s in range(8):
            b = s % 2
            wl = [('V', hbuild_done[s - 2])] if s >= 2 else [('V', vcopy_done[7])]
            lh = self.dmaY(xspan[b].rearrange("p (c n) -> p c n", n=512)[:, :, :],
                           self.rr['xT'][:, :, s * 512:(s + 1) * 512], wait=wl)
            wt = [('V', hbuild_done[s - 1] if s >= 1 else vcopy_done[7])]
            if s == 0:
                wt += rows_done
            self.op('T', 'matmul', pb1[:, 0:512], self.ones1b[:, :],
                    arow[0:1, s * 512:(s + 1) * 512], start=True, stop=True, wait=wt)
            self.op('T', 'matmul', pb1[:, 512:1024], self.ones1b[:, :],
                    brow[0:1, s * 512:(s + 1) * 512], start=True, stop=True, inc=('T', 1))
            tb = self.val('T')
            self.waitl('V', [lh])
            self.wait('V', 'T', tb)
            for c in range(NCH):
                self.op('V', 'tensor_tensor', xb[b][:, 0:512],
                        xspan[b][:, c * 512:(c + 1) * 512], pb1[:, 0:512], op=ALU.mult)
                self.op('V', 'tensor_tensor', h[c][:, s * 512:(s + 1) * 512],
                        xb[b][:, 0:512], pb1[:, 512:1024], op=ALU.add,
                        inc=(('V', 1) if c == NCH - 1 else None))
            hbuild_done[s] = self.vfence()
        h_done = self.vfence()
        self.canary(0)
        h_done = self.val('V')
        if self.stage == 'A1':
            dbg = self.g['dbgA1']
            self.dmaS(dbg[:, 0:N], h[0][:, :], wait=[('V', h_done)])
            self.dmaS(dbg[:, N:2 * N], h[5][:, :])
            sub.close()
            return
        sub.close()

        KT = [self.sbuf(f'KTg{p}', [128, N], BF16, st=stA) for p in range(2)]
        Vg = self.sbuf('Vg', [128, 32 * 4 * 65], BF16, st=stA)
        Vg4 = Vg.rearrange("p (m h e) -> p m h e", h=4, e=65)
        Wq_g = [self.sbuf(f'Wqg{c}', [128, 256], BF16, st=stA) for c in range(NCH)]
        Wk_g = [self.sbuf(f'Wkg{c}', [128, 256], BF16, st=stA) for c in range(NCH)]
        Wv_g = [self.sbuf(f'Wvg{c}', [128, 256], BF16, st=stA) for c in range(NCH)]
        Wp_g = [self.sbuf(f'Wpg{p}', [128, 1024], BF16, st=stA) for p in range(2)]
        Qg = self.sbuf('Qg', [128, 2 * 512], BF16, st=stA)
        Pa = self.sbuf('Pa', [128, 1024], BF16, st=stA)
        Pb = self.sbuf('Pb', [128, 1024], BF16, st=stA)
        dn = self.sbuf('dn', [64, 512], F32, st=stA)
        drow = self.sbuf('drow', [1, 512], BF16, st=stA)
        drowf = self.sbuf('drowf', [1, 512], F32, st=stA)
        Ost = self.sbuf('Ost', [128, 2 * 512], BF16, st=stA)
        ysb = [self.sbuf(f'ysb{i}', [128, 512], F32, st=stA) for i in range(2)]
        yac = [self.sbuf(f'yac{i}', [128, 512], F32, st=stA) for i in range(2)]
        xsp = [self.sbuf(f'xsp{i}', [128, 512], F32, st=stA) for i in range(2)]
        x1f = [self.sbuf(f'x1f{i}', [128, 512], F32, st=stA) for i in range(2)]
        x1b = [self.sbuf(f'x1b{i}', [128, 512], BF16, st=stA) for i in range(2)]
        x1s = [self.sbuf(f'x1s{i}', [128, 512], BF16, st=stA) for i in range(2)]
        ylastlg = [None, None]
        yregion = {}
        yaclastv = [0, 0]
        x1_store_ls = [None, None]
        x1_sums_t = [0, 0]
        h0_first = [True]

        ngrp = {'Y': 1, 'Y3': 3}.get(self.stage, NGRP)
        for grp in range(ngrp):
            for c in range(NCH):
                self.dmaG(Wq_g[c][:, :], g['WqT'][c * 128:(c + 1) * 128, grp * 256:(grp + 1) * 256],
                          wait=([('V', h_done)] if grp == 0 and c == 0 else None))
                self.dmaG(Wk_g[c][:, :], g['WkT'][c * 128:(c + 1) * 128, grp * 256:(grp + 1) * 256])
                self.dmaG(Wv_g[c][:, :], g['WvT'][c * 128:(c + 1) * 128, grp * 256:(grp + 1) * 256])
            for p in range(2):
                self.dmaG(Wp_g[p][:, :],
                          g['Wpp'][grp * 256 + p * 128: grp * 256 + (p + 1) * 128, :])
            lgw = self.allw('LG')

            self.op('V', 'memset', Vg4[:, :, :, 64:65], 1.0, inc=('V', 1))
            vones = self.vfence()
            self.waitl('T', lgw)
            self.wait('T', 'V', max(h_done, vones))
            kvtgt = [pm[:, :], pOa[:, :], pOb[:, :], pb1[:, 0:512]]
            kv_ev = {}
            kv0v = self.vfence()
            kv0s = self.sfence()
            gi = [0]
            def kv_group(mms, evac):
                i = gi[0] % 4
                if gi[0] - 4 in kv_ev:
                    w = [('V', kv_ev[gi[0] - 4])]
                else:
                    w = [('V', kv0v), ('S', kv0s)]
                if w:
                    self.waitl('T', w)
                mms(kvtgt[i])
                tk = self.val('T')
                evac(kvtgt[i], tk)
                kv_ev[gi[0]] = self.vfence()
                gi[0] += 1
            for nc2 in range(NCH):
                for p in range(2):
                    def mms(tgt, nc2=nc2, p=p):
                        for c in range(NCH):
                            self.op('T', 'matmul', tgt, Wk_g[c][:, p * 128:(p + 1) * 128],
                                    h[c][:, nc2 * 512:(nc2 + 1) * 512],
                                    start=(c == 0), stop=(c == NCH - 1),
                                    inc=(('T', 1) if c == NCH - 1 else None))
                    def evac(tgt, tk, nc2=nc2, p=p):
                        self.op('V', 'tensor_copy', KT[p][:, nc2 * 512:(nc2 + 1) * 512],
                                tgt, wait=[('T', tk)], inc=('V', 1))
                    kv_group(mms, evac)
                for ms in range(4):
                    def mms(tgt, nc2=nc2, ms=ms):
                        for c in range(NCH):
                            self.op('T', 'matmul', tgt[0:128, 0:256],
                                    h[c][:, nc2 * 512 + ms * 128: nc2 * 512 + (ms + 1) * 128],
                                    Wv_g[c][:, :], start=(c == 0), stop=(c == NCH - 1),
                                    inc=(('T', 1) if c == NCH - 1 else None))
                    def evac(tgt, tk, nc2=nc2, ms=ms):
                        mcg = nc2 * 4 + ms
                        self.op('V', 'tensor_copy', Vg4[:, mcg, :, 0:64],
                                tgt[0:128, 0:256].rearrange("p (hh e) -> p hh e", e=64),
                                wait=[('T', tk)], inc=('V', 1))
                    kv_group(mms, evac)
            self.wait('T', 'V', kv_ev[gi[0] - 1])

            if self.stage == 'KV' and grp == 0:
                dbg = self.g['dbgKV']
                vlast = self.vfence()
                self.dmaS(dbg[:, 0:N], KT[0][:, :], wait=[('V', vlast)])
                self.dmaS(dbg[:, N:2 * N], KT[1][:, :])
                self.dmaS(dbg[:, 2 * N:], Vg[:, :])
                dW = self.g['dbgW']
                for c in range(NCH):
                    self.dmaS(dW[:, c * 256:(c + 1) * 256], Wk_g[c][:, :])
                    self.dmaS(dW[:, 2048 + c * 256:2048 + (c + 1) * 256], Wv_g[c][:, :])
                return
            self.canary(1 + grp * 3)
            for qb in range(NQB):
                if qb == 1:
                    self.canary(2 + grp * 3)
                for p in range(2):
                    for c in range(NCH):
                        self.op('T', 'matmul', pm[:, :], Wq_g[c][:, p * 128:(p + 1) * 128],
                                h[c][:, qb * 512:(qb + 1) * 512],
                                start=(c == 0), stop=(c == NCH - 1),
                                inc=(('T', 1) if c == NCH - 1 else None))
                    tq = self.val('T')
                    self.op('V', 'tensor_copy', Qg[:, p * 512:(p + 1) * 512], pm[:, :],
                            wait=[('T', tq)], inc=('V', 1))
                    self.wait('T', 'V', self.vfence())
                for p in range(2):
                    sexp = {}
                    tavm = {}
                    p0exp = self.val('S')
                    for mc in range(33):
                        if mc < 32:
                            Sp = pb1 if mc % 2 == 0 else pb3
                            Pj = Pa if mc % 2 == 0 else Pb
                            wS = [('S', sexp[mc - 2])] if mc >= 2 else [('S', p0exp)]
                            self.op('T', 'matmul', Sp[:, 0:512],
                                    KT[p][0:64, mc * 128:(mc + 1) * 128],
                                    Qg[0:64, p * 512:(p + 1) * 512],
                                    start=True, stop=True, tile_position=(0, 0),
                                    wait=wS)
                            self.op('T', 'matmul', Sp[:, 512:1024],
                                    KT[p][64:128, mc * 128:(mc + 1) * 128],
                                    Qg[64:128, p * 512:(p + 1) * 512],
                                    start=True, stop=True, tile_position=(64, 0),
                                    inc=('T', 1))
                            ts = self.val('T')
                            we = [('T', ts)]
                            if mc - 2 in tavm:
                                we.append(('T', tavm[mc - 2]))
                            self.op('S', 'activation', Pj[:, :], Sp[:, :], AF.Exp,
                                    scale=0.125, wait=we)
                            sexp[mc] = self.sfence()
                        if mc >= 1:
                            j = mc - 1
                            Pj = Pa if j % 2 == 0 else Pb
                            self.wait('T', 'S', sexp[j])
                            self.op('T', 'matmul', pOa[0:65, :],
                                    Vg4[:, j, 2 * p, :], Pj[:, 0:512],
                                    start=(j == 0), stop=(j == 31))
                            self.op('T', 'matmul', pOb[0:65, :],
                                    Vg4[:, j, 2 * p + 1, :], Pj[:, 512:1024],
                                    start=(j == 0), stop=(j == 31),
                                    inc=('T', 1))
                            tavm[j] = self.val('T')
                    tav = self.val('T')
                    for hh, pO in ((0, pOa), (1, pOb)):
                        self.op('V', 'reciprocal', drow[0:1, :], pO[64:65, :],
                                wait=[('T', tav)] if hh == 0 else None, inc=('V', 1))
                        vr1 = self.vfence()
                        self.op('T', 'matmul', pm[0:64, 0:512], self.ones1b[0:1, 0:64],
                                drow[0:1, :], start=True, stop=True,
                                wait=[('V', vr1)], inc=('T', 1))
                        tb1 = self.val('T')
                        self.op('V', 'tensor_copy', dn[:, :], pm[0:64, 0:512], wait=[('T', tb1)])
                        self.op('V', 'tensor_tensor',
                                Ost[hh * 64:(hh + 1) * 64, p * 512:(p + 1) * 512],
                                pO[0:64, :], dn[:, :], op=ALU.mult, inc=('V', 1))
                    self.wait('T', 'V', self.vfence())
                if self.stage == 'Y' and grp == 0 and qb == NQB - 1:
                    self._stopY = True
                if self.stage == 'X1' and grp == 3 and qb == 0:
                    ostsnap = self.sbuf('dbg_ostsnap', [128, 1024], BF16)
                    self.op('V', 'tensor_copy', ostsnap[:, :], Ost[:, :],
                            wait=[('V', self.val('V'))], inc=('V', 1))
                    self._ostsnap = ostsnap
                if self.stage == 'QOST3' and grp == 3 and qb == 0:
                    dbg = self.g['dbgQ']
                    vv = self.val('V')
                    self.dmaS(dbg[:, 0:1024], Qg[:, :], wait=[('V', vv)])
                    self.dmaS(dbg[:, 1024:2048], Pa[:, :])
                    self.dmaS(dbg[:, 2048:3072], Ost[:, :])
                    return
                if self.stage == 'QOST' and grp == 0 and qb == 0:
                    dbg = self.g['dbgQ']
                    vv = self.val('V')
                    self.dmaS(dbg[:, 0:1024], Qg[:, :], wait=[('V', vv)])
                    self.dmaS(dbg[:, 1024:2048], Pa[:, :])
                    self.dmaS(dbg[:, 2048:3072], Ost[:, :])
                    return
                for oc in range(8):
                    for p in range(2):
                        self.op('T', 'matmul', pb1[:, 0:512],
                                Wp_g[p][:, oc * 128:(oc + 1) * 128],
                                Ost[:, p * 512:(p + 1) * 512], start=(p == 0), stop=(p == 1),
                                inc=(('T', 1) if p == 1 else None))
                    ty = self.val('T')
                    if grp < 3:
                        b = oc % 2
                        wv = [('T', ty)]
                        if ylastlg[b]:
                            wv.append(ylastlg[b])
                        self.op('V', 'tensor_copy', ysb[b][:, :], pb1[:, 0:512],
                                wait=wv, inc=('V', 1))
                        vy = self.vfence()
                        wdma = [('V', vy)]
                        if (qb, oc) in yregion:
                            wdma.append(yregion[(qb, oc)])
                        ylastlg[b] = self.dmaG(
                            self.rr['yacc'][:, oc, qb * 512:(qb + 1) * 512],
                            ysb[b][:, :], wait=wdma, accum=(grp > 0))
                        yregion[(qb, oc)] = ylastlg[b]
                        self.wait('T', 'V', vy)
                    else:
                        b = oc % 2
                        wld = [yregion[(qb, oc)]]
                        if yaclastv[b]:
                            wld.append(('V', yaclastv[b]))
                        lh1 = self.dmaY(yac[b][:, :],
                                        self.rr['yacc'][:, oc, qb * 512:(qb + 1) * 512], wait=wld)
                        lh2 = self.dmaY(xsp[b][:, :],
                                        self.rr['xT'][:, oc, qb * 512:(qb + 1) * 512],
                                        wait=[('V', yaclastv[b])] if yaclastv[b] else None)
                        wv = [('T', ty), lh1, lh2]
                        if x1_store_ls[b]:
                            wv.append(x1_store_ls[b])
                        self.op('V', 'tensor_tensor', x1f[b][:, :], pb1[:, 0:512],
                                yac[b][:, :], op=ALU.add, wait=wv)
                        self.op('V', 'scalar_tensor_tensor', x1f[b][:, :], x1f[b][:, :],
                                self.bp_sb[:, oc:oc + 1], xsp[b][:, :],
                                op0=ALU.add, op1=ALU.add, inc=('V', 1))
                        vx1 = self.vfence()
                        yaclastv[b] = vx1
                        self.wait('T', 'V', vx1)
                        x1_store_ls[b] = self.dmaS(
                            self.rr['R1'][:, oc, qb * 512:(qb + 1) * 512],
                            x1f[b][:, :], wait=[('V', vx1)])
                        wv2 = [('T', x1_sums_t[b])] if x1_sums_t[b] else None
                        self.op('V', 'tensor_copy', x1b[b][:, :], x1f[b][:, :], wait=wv2)
                        self.op('V', 'tensor_tensor', x1s[b][:, :], x1b[b][:, :],
                                x1b[b][:, :], op=ALU.mult, inc=('V', 1))
                        vs1 = self.vfence()
                        self.wait('T', 'V', vs1)
                        for ms in range(4):
                            st = h0_first[0]
                            h0_first[0] = False
                            bend = (qb == NQB - 1 and oc == NCH - 1 and ms == 3)
                            self.op('T', 'matmul', pb0[:, 64 + qb * 4 + ms:65 + qb * 4 + ms],
                                    x1b[b][:, ms * 128:(ms + 1) * 128], self.onesb[:, :],
                                    start=st, stop=False, skip_group_check=True)
                            self.op('T', 'matmul', pb0[:, 96 + qb * 4 + ms:97 + qb * 4 + ms],
                                    x1s[b][:, ms * 128:(ms + 1) * 128], self.onesb[:, :],
                                    start=False, stop=bend, skip_group_check=True,
                                    inc=(('T', 1) if ms == 3 else None))
                        x1_sums_t[b] = self.val('T')
                        if self.stage == 'X1' and grp == 3 and qb == 0 and oc == 1:
                            dbg = self.g['dbgX']
                            ycp = self.sbuf('dbg_ycp', [128, 512], F32)
                            bcp = self.sbuf('dbg_bcp', [128, 8], F32)
                            self.op('V', 'tensor_copy', ycp[:, :], pb1[:, 0:512],
                                    wait=[('V', self.val('V'))])
                            self.op('V', 'tensor_copy', bcp[:, :], self.bp_sb[:, :],
                                    inc=('V', 1))
                            vv = self.val('V')
                            self.dmaS(dbg[:, 0:512], x1f[0][:, :], wait=[('V', vv)])
                            self.dmaS(dbg[:, 512:1024], ycp[:, :])
                            self.dmaS(dbg[:, 1024:1536], yac[0][:, :])
                            self.dmaS(dbg[:, 1536:1544], bcp[:, :])
                            wcp = self.sbuf('dbg_wcp', [128, 512], F32)
                            self.op('V', 'tensor_copy', wcp[:, 0:256], self._ostsnap[:, 0:256])
                            self.op('V', 'tensor_copy', wcp[:, 256:512], Ost[:, 0:256],
                                    inc=('V', 1))
                            self.dmaS(dbg[:, 1544:2048], wcp[:, 0:504], wait=[('V', self.val('V'))])
                            return
        self.canary(15)
        self.h0_rows_done = self._ln_rows('ln0', pb0, 64, 96, arow0, brow0,
                                          ('T', self.val('T')), st=stA)
        self.stageA_end = [('T', self.val('T')), ('V', self.val('V')),
                           ('S', self.val('S'))] + self.h0_rows_done
        if self.stage == 'CAN':
            self.dmaS(self.g['dbgC'], self.cansb[:, :], wait=[('V', self.val('V'))])

    def _ffn_layer(self, l):
        g = self.g
        pb0, pb1, pb3 = self.pb0, self.pb1, self.pb3
        pb5, pb6, pb7 = self.pb5, self.pb6, self.pb7
        if l == 0:
            self.stA.close()
            self.stA = None
            gate = list(self.stageA_end)
            Rin = self.rr['R1']
        else:
            self.stH0.close()
            self.stH0 = None
            gate = list(self.ffn_gate)
            Rin = self.rr['RZ1']
        rin_store = self.allw('LS')
        Rout = self.rr['RA0'] if l == 0 else self.rr['RA1']
        Rz = self.rr['RZ1'] if l == 0 else self.rr['outT']

        stR = contextlib.ExitStack()
        rsb = lambda nm, shp, dt: self.sbuf(f'L{l}{nm}', shp, dt, st=stR)
        a2row = rsb('a2row', [1, N], BF16)
        b2row = rsb('b2row', [1, N], BF16)
        b1t = rsb('b1t', [128, 44], F32)
        b2c = rsb('b2c', [128, 8], F32)
        lgc = rsb('lgc', [128, 8], F32)
        lbb = rsb('lbb', [128, 8], BF16)

        stF = contextlib.ExitStack()
        sb = lambda nm, shp, dt: self.sbuf(f'L{l}{nm}', shp, dt, st=stF)
        wqkv = [sb(f'wqkv{c}', [128, 1536], BF16) for c in range(NCH)]
        wo_sb = [sb(f'wo{p}', [128, 1024], BF16) for p in range(4)]
        PQ = sb('PQ', [128, 128], BF16)
        PK = sb('PK', [128, 128], BF16)
        qs_t = sb('qs', [128, 1], F32)
        ks_t = sb('ks', [128, 1], F32)
        self.dmaG(wqkv[0][:, :], g[f'wqkv{l}'][0:128, :], wait=gate)
        for c in range(1, NCH):
            self.dmaG(wqkv[c][:, :], g[f'wqkv{l}'][c * 128:(c + 1) * 128, :])
        for p in range(4):
            self.dmaG(wo_sb[p][:, :], g[f'wo{l}'][p * 128:(p + 1) * 128, :])
        self.dmaG(PQ[:, :], g[f'PQ{l}'])
        self.dmaG(PK[:, :], g[f'PK{l}'])
        self.dmaG(lbb[:, :], g[f'lbcol{l}'])
        tqs = self.dmaY(qs_t[:, :], g[f'qs_t{l}'], wait=gate)
        tks = self.dmaY(ks_t[:, :], g[f'ks_t{l}'])
        tlg = self.dmaY(lgc[:, :], g[f'lgcol{l}'])
        tb2c = self.dmaY(b2c[:, :], g[f'b2col{l}'])
        wload = self.allw('LG') + self.allw('LH')

        zwin = sb('zwin', [128, 8 * 513], F32)
        zw3 = zwin.rearrange("p (c n) -> p c n", n=513)
        hw = sb('hw', [128, 8 * 512], BF16)
        qraw = sb('qraw', [128, 4 * 512], BF16)
        kraw = sb('kraw', [128, 4 * 512], BF16)
        q2b = sb('q2b', [128, 512], BF16)
        k2b = sb('k2b', [128, 4 * 512], BF16)
        qf = sb('qf', [128, 4 * 512], BF16)
        kf = [sb(f'kf{i}', [128, 4 * 512], BF16) for i in range(2)]
        Vw = [sb(f'Vw{i}', [128, 4 * 8 * 65], BF16) for i in range(2)]
        Vw4 = [t.rearrange("p (m hh e) -> p m hh e", hh=8, e=65) for t in Vw]
        rk8 = [sb(f'rk8{i}', [128, 32], F32) for i in range(2)]
        rqrow = sb('rqrow', [2, 512], BF16)
        tmp1 = sb('tmp1', [128, 512], F32)
        tmp2 = sb('tmp2', [128, 512], F32)
        tmp3 = sb('tmp3', [128, 512], F32)
        tmp1k = sb('tmp1k', [128, 512], F32)
        tmp2k = sb('tmp2k', [128, 512], F32)
        tmp3k = sb('tmp3k', [128, 512], F32)
        TA1 = sb('TA1', [128, 512], F32)
        TA2 = sb('TA2', [128, 512], F32)
        TB1 = sb('TB1', [128, 512], F32)
        TB2 = sb('TB2', [128, 512], F32)
        A0w = sb('A0w', [128, 513], F32)
        B0w = sb('B0w', [128, 513], F32)
        Ph = [sb(f'Ph{i}', [128, 8 * 512], BF16) for i in range(2)]
        dnl = sb('dnl', [64, 512], F32)
        drl = sb('drl', [1, 512], BF16)
        drlf = sb('drlf', [1, 512], F32)
        Ow = sb('Ow', [128, 4 * 512], BF16)
        za = sb('za', [128, 8 * 513], F32)
        za3 = za.rearrange("p (c n) -> p c n", n=513)
        zhalo = sb('zhalo', [128, 8], F32)
        zab = sb('zab', [128, 8 * 513], BF16)
        za2 = sb('za2', [128, 8 * 513], BF16)
        scmb = sb('scmb', [128, 64], F32)

        def chunks(w):
            out = []
            for mc in range(8):
                if w == 0 and mc < 4:
                    continue
                out.append((mc, max(0, mc - 4) * 128, min(3, mc) * 128 + 128))
            return out

        zstore = []
        stats_first = [True]
        tstats_prev = 0
        tav_prev = 0
        vwin_prev = 0
        zast_prev = None
        for w in range(self.nwin):
            cur = w % 2
            prv = 1 - cur
            wl = list(rin_store)
            if vwin_prev:
                wl.append(('V', vwin_prev))
            if w == 0:
                tz = [self.dmaY(zw3[:, c, 1:513], Rin[:, c, 0:512],
                                wait=(wl + wload if c == 0 else wl)) for c in range(NCH)]
                self.op('V', 'memset', zw3[:, :, 0:1], 0.0, inc=('V', 1))
            else:
                tz = [self.dmaY(zw3[:, c, :], Rin[:, c, w * 512 - 1:(w + 1) * 512],
                                wait=wl) for c in range(NCH)]
            wt = [('V', vwin_prev)] if vwin_prev else None
            ttab = [self.dmaY(TA1[:, :], g['TAc'][:, w * 512:(w + 1) * 512], wait=wt),
                    self.dmaY(TA2[:, :], g['TAs'][:, w * 512:(w + 1) * 512], wait=wt),
                    self.dmaY(TB1[:, :], g['TBc'][:, w * 512:(w + 1) * 512], wait=wt),
                    self.dmaY(TB2[:, :], g['TBs'][:, w * 512:(w + 1) * 512], wait=wt)]

            if l == 0:
                vh0 = self.val('V')
                tw = [('V', vh0)]
                if w == 0:
                    self.op('T', 'matmul', pb1[:, 0:512], self.ones1b[:, :],
                            self.arow0[0:1, 0:512], start=True, stop=True,
                            wait=tw + gate)
                    self.op('T', 'matmul', pb3[:, 0:512], self.ones1b[:, :],
                            self.brow0[0:1, 0:512], start=True, stop=True, inc=('T', 1))
                    tb = self.val('T')
                    self.op('V', 'memset', A0w[:, 0:1], 0.0, wait=[('T', tb)])
                    self.op('V', 'memset', B0w[:, 0:1], 0.0)
                    self.op('V', 'tensor_copy', A0w[:, 1:513], pb1[:, 0:512])
                    self.op('V', 'tensor_copy', B0w[:, 1:513], pb3[:, 0:512], inc=('V', 1))
                else:
                    self.op('T', 'matmul', pb1[:, 0:512], self.ones1b[:, :],
                            self.arow0[0:1, w * 512 - 1:(w + 1) * 512 - 1],
                            start=True, stop=True, wait=tw)
                    self.op('T', 'matmul', pb1[:, 512:513], self.ones1b[:, :],
                            self.arow0[0:1, (w + 1) * 512 - 1:(w + 1) * 512],
                            start=True, stop=True)
                    self.op('T', 'matmul', pb3[:, 0:512], self.ones1b[:, :],
                            self.brow0[0:1, w * 512 - 1:(w + 1) * 512 - 1],
                            start=True, stop=True)
                    self.op('T', 'matmul', pb3[:, 512:513], self.ones1b[:, :],
                            self.brow0[0:1, (w + 1) * 512 - 1:(w + 1) * 512],
                            start=True, stop=True, inc=('T', 1))
                    tb = self.val('T')
                    self.op('V', 'tensor_copy', A0w[:, 0:512], pb1[:, 0:512],
                            wait=[('T', tb)])
                    self.op('V', 'tensor_copy', A0w[:, 512:513], pb1[:, 512:513])
                    self.op('V', 'tensor_copy', B0w[:, 0:512], pb3[:, 0:512])
                    self.op('V', 'tensor_copy', B0w[:, 512:513], pb3[:, 512:513],
                            inc=('V', 1))
                for c in range(NCH):
                    off = 0 if c >= 4 else 1
                    wz = tz if c == 0 else None
                    self.op('V', 'tensor_tensor', tmp1[:, :],
                            zw3[:, c, off:off + 512], A0w[:, off:off + 512],
                            op=ALU.mult, wait=wz)
                    self.op('V', 'tensor_tensor', hw[:, c * 512:(c + 1) * 512],
                            tmp1[:, :], B0w[:, off:off + 512], op=ALU.add,
                            inc=(('V', 1) if c == NCH - 1 else None))
            else:
                for c in range(NCH):
                    off = 0 if c >= 4 else 1
                    wz = tz if c == 0 else None
                    self.op('V', 'tensor_copy', hw[:, c * 512:(c + 1) * 512],
                            zw3[:, c, off:off + 512], wait=wz,
                            inc=(('V', 1) if c == NCH - 1 else None))
            vh = self.vfence()
            if self.sub < 2:
                vwin_prev = self.val('V')
                continue

            self.waitl('T', wload + [('V', vh)])
            self.op('V', 'memset', Vw4[cur][:, :, :, 64:65], 1.0, inc=('V', 1))
            vmset = self.vfence()
            bank_free = {}
            for pc in range(4):
                for part, bid, pbank, dst in ((0, 0, pb1, qraw), (1, 1, pb3, kraw)):
                    if bid in bank_free:
                        self.wait('T', 'V', bank_free[bid])
                    for c in range(NCH):
                        self.op('T', 'matmul', pbank[:, 0:512],
                                wqkv[c][:, part * 512 + pc * 128:part * 512 + (pc + 1) * 128],
                                hw[:, c * 512:(c + 1) * 512],
                                start=(c == 0), stop=(c == NCH - 1),
                                inc=(('T', 1) if c == NCH - 1 else None))
                    tqk = self.val('T')
                    self.op('V', 'tensor_copy', dst[:, pc * 512:(pc + 1) * 512],
                            pbank[:, 0:512], wait=[('T', tqk)], inc=('V', 1))
                    bank_free[bid] = self.vfence()
            if self.sub == 21:
                vwin_prev = self.val('V')
                continue
            self.wait('T', 'V', vmset)
            for ms in range(4):
                bid = 2 + ms % 2
                tgt = pb1[:, 512:1024] if ms % 2 == 0 else pb3[:, 512:1024]
                if bid in bank_free:
                    self.wait('T', 'V', bank_free[bid])
                for c in range(NCH):
                    self.op('T', 'matmul', tgt,
                            hw[:, c * 512 + ms * 128:c * 512 + (ms + 1) * 128],
                            wqkv[c][:, 1024:1536],
                            start=(c == 0), stop=(c == NCH - 1),
                            inc=(('T', 1) if c == NCH - 1 else None))
                tv = self.val('T')
                self.op('V', 'tensor_copy', Vw4[cur][:, ms, :, 0:64],
                        tgt.rearrange("p (hh e) -> p hh e", e=64),
                        wait=[('T', tv)], inc=('V', 1))
                bank_free[bid] = self.vfence()
            self.waitl('T', [('V', v) for v in bank_free.values()])

            if self.sub < 3:
                vwin_prev = self.val('V')
                continue
            KNORMS_ONLY = (self.sub == 31)
            for pc in range(4):
                self.op('V', 'tensor_tensor', k2b[:, pc * 512:(pc + 1) * 512],
                        kraw[:, pc * 512:(pc + 1) * 512],
                        kraw[:, pc * 512:(pc + 1) * 512], op=ALU.mult,
                        inc=(('V', 1) if pc == 3 else None))
            self.wait('T', 'V', self.vfence())
            for pc in range(4):
                for ms in range(4):
                    cc = (pc * 4 + ms) * 2
                    self.op('T', 'matmul', pb7[:, cc:cc + 2],
                            k2b[:, pc * 512 + ms * 128:pc * 512 + (ms + 1) * 128],
                            self.bd[:, :], start=True, stop=True,
                            skip_group_check=True,
                            inc=(('T', 1) if pc == 3 and ms == 3 else None))
            tkn = self.val('T')
            self.op('S', 'activation', rk8[cur][:, :], pb7[:, 0:32], AF.Ln,
                    bias=self.cns[:, 1:2], wait=[('T', tkn)])
            self.op('S', 'activation', rk8[cur][:, :], rk8[cur][:, :], AF.Exp,
                    scale=-0.5, bias=self.cns[:, 2:3], inc=('S', 1))
            self.wait('T', 'S', self.sfence())
            if KNORMS_ONLY:
                vwin_prev = self.val('V')
                continue

            rotq_t = [pb7[:, 0:512], pb3[:, 0:512]]
            rotk_t = [pb1[:, 0:512], pb3[:, 512:1024]]
            rq_t = [pb5[:, 0:512], pb6[:, 0:512]]
            qn_t = [pb5[0:2, 0:512], pb6[0:2, 0:512]]
            free_rq = {}
            free_rot = {}
            free_rotk = {}
            for pc in range(4):
                b = pc % 2
                wr = [('V', free_rot[b])] if b in free_rot else None
                self.op('T', 'matmul', rotq_t[b], PQ[:, :],
                        qraw[:, pc * 512:(pc + 1) * 512], start=True, stop=True,
                        wait=wr, inc=('T', 1))
                trot = self.val('T')
                self.op('V', 'tensor_tensor', q2b[:, :], qraw[:, pc * 512:(pc + 1) * 512],
                        qraw[:, pc * 512:(pc + 1) * 512], op=ALU.mult, inc=('V', 1))
                vq2 = self.vfence()
                wq = [('V', vq2)]
                if b in free_rq:
                    wq.append(('V', free_rq[b]))
                self.op('T', 'matmul', qn_t[b], self.bd[:, :], q2b[:, :],
                        start=True, stop=True, wait=wq, inc=('T', 1))
                tqn = self.val('T')
                self.op('S', 'activation', rqrow[:, :], qn_t[b], AF.Ln,
                        bias=self.cns[0:2, 1:2], wait=[('T', tqn)])
                self.op('S', 'activation', rqrow[:, :], rqrow[:, :], AF.Exp,
                        scale=-0.5, inc=('S', 1))
                sqn = self.sfence()
                self.op('T', 'matmul', rq_t[b], self.E2[:, :], rqrow[:, :],
                        start=True, stop=True, wait=[('S', sqn)], inc=('T', 1))
                trq = self.val('T')
                self.op('V', 'tensor_scalar', tmp1[:, :], qraw[:, pc * 512:(pc + 1) * 512],
                        qs_t[:, :], None, op0=ALU.mult, wait=[('T', trot), tqs] + ttab)
                self.op('V', 'tensor_tensor', tmp2[:, :], tmp1[:, :], TA1[:, :],
                        op=ALU.mult)
                self.op('V', 'tensor_tensor', tmp3[:, :], rotq_t[b], TA2[:, :],
                        op=ALU.mult)
                self.op('V', 'tensor_tensor', tmp2[:, :], tmp2[:, :], tmp3[:, :],
                        op=ALU.add)
                self.op('V', 'tensor_tensor', qf[:, pc * 512:(pc + 1) * 512],
                        tmp2[:, :], rq_t[b], op=ALU.mult,
                        wait=[('T', trq)], inc=('V', 1))
                free_rot[b] = free_rq[b] = self.vfence()
                wk = [('V', free_rotk[b])] if b in free_rotk else None
                self.op('T', 'matmul', rotk_t[b], PK[:, :],
                        kraw[:, pc * 512:(pc + 1) * 512], start=True, stop=True,
                        wait=wk, inc=('T', 1))
                trotk = self.val('T')
                self.op('V', 'tensor_scalar', tmp1k[:, :], kraw[:, pc * 512:(pc + 1) * 512],
                        ks_t[:, :], None, op0=ALU.mult, wait=[('T', trotk), tks])
                self.op('V', 'tensor_tensor', tmp2k[:, :], tmp1k[:, :], TB1[:, :],
                        op=ALU.mult)
                self.op('V', 'tensor_tensor', tmp3k[:, :], rotk_t[b], TB2[:, :],
                        op=ALU.mult)
                self.op('V', 'tensor_tensor', kf[cur][:, pc * 512:(pc + 1) * 512],
                        tmp2k[:, :], tmp3k[:, :], op=ALU.add, inc=('V', 1))
                free_rotk[b] = self.vfence()
            self.waitl('T', [('V', v) for v in
                             list(free_rot.values()) + list(free_rotk.values())])

            if self.sub < 40:
                vwin_prev = self.val('V')
                continue
            for pc in range(4):
                hA, hB = 2 * pc, 2 * pc + 1
                p0exp = self.val('S')
                first_exp = True
                sexp_l = {}
                for k, (mc, qlo, qhi) in enumerate(chunks(w)):
                    src = kf[prv] if mc < 4 else kf[cur]
                    vsrc = mc % 4
                    SA = pb5[:, :] if k % 2 == 0 else pb1[:, 0:512]
                    SB = pb6[:, :] if k % 2 == 0 else pb1[:, 512:1024]
                    wS = [('S', sexp_l[k - 2])] if k - 2 in sexp_l else [('S', p0exp)]
                    self.op('T', 'matmul', SA[:, qlo:qhi],
                            src[0:64, pc * 512 + vsrc * 128:pc * 512 + (vsrc + 1) * 128],
                            qf[0:64, pc * 512 + qlo:pc * 512 + qhi],
                            start=True, stop=True, tile_position=(0, 0), wait=wS)
                    self.op('T', 'matmul', SB[:, qlo:qhi],
                            src[64:128, pc * 512 + vsrc * 128:pc * 512 + (vsrc + 1) * 128],
                            qf[64:128, pc * 512 + qlo:pc * 512 + qhi],
                            start=True, stop=True, tile_position=(64, 0),
                            inc=('T', 1))
                    tS = self.val('T')
                    mwait = [('T', tS)]
                    if mc >= 4:
                        qs0 = (mc - 4) * 128
                        self.op('V', 'tensor_tensor', SA[:, qs0:qs0 + 128],
                                SA[:, qs0:qs0 + 128], self.M0_sb[:, :], op=ALU.add,
                                wait=mwait)
                        mwait = None
                        self.op('V', 'tensor_tensor', SB[:, qs0:qs0 + 128],
                                SB[:, qs0:qs0 + 128], self.M0_sb[:, :], op=ALU.add,
                                inc=('V', 1))
                    if mc <= 3:
                        qs0 = mc * 128
                        self.op('V', 'tensor_tensor', SA[:, qs0:qs0 + 128],
                                SA[:, qs0:qs0 + 128], self.M512_sb[:, :], op=ALU.add,
                                wait=mwait)
                        self.op('V', 'tensor_tensor', SB[:, qs0:qs0 + 128],
                                SB[:, qs0:qs0 + 128], self.M512_sb[:, :], op=ALU.add,
                                inc=('V', 1))
                    vmask = self.vfence()
                    if self.sub < 43:
                        continue
                    rksrc = rk8[prv] if mc < 4 else rk8[cur]
                    we = [('V', vmask)]
                    if first_exp:
                        we.append(('T', tav_prev))
                        first_exp = False
                    self.op('S', 'activation', Ph[0][:, mc * 512 + qlo:mc * 512 + qhi],
                            SA[:, qlo:qhi], AF.Exp,
                            scale=rksrc[:, (pc * 4 + vsrc) * 2:(pc * 4 + vsrc) * 2 + 1],
                            wait=we)
                    self.op('S', 'activation', Ph[1][:, mc * 512 + qlo:mc * 512 + qhi],
                            SB[:, qlo:qhi], AF.Exp,
                            scale=rksrc[:, (pc * 4 + vsrc) * 2 + 1:(pc * 4 + vsrc) * 2 + 2],
                            inc=('S', 1))
                    sexp_l[k] = self.val('S')
                if self.sub < 46:
                    continue
                self.wait('T', 'S', self.sfence())
                for qsub in range(4):
                    mcs = [mc for (mc, qlo, qhi) in chunks(w) if qlo <= qsub * 128 < qhi]
                    for ki, mc in enumerate(mcs):
                        vv4 = Vw4[prv] if mc < 4 else Vw4[cur]
                        st_ = (qsub == 0 and ki == 0)
                        sp_ = (qsub == 3 and ki == len(mcs) - 1)
                        self.op('T', 'matmul', pb1[0:65, qsub * 128:(qsub + 1) * 128],
                                vv4[:, mc % 4, hA, :],
                                Ph[0][:, mc * 512 + qsub * 128:mc * 512 + (qsub + 1) * 128],
                                start=st_, stop=sp_, skip_group_check=True)
                        self.op('T', 'matmul',
                                pb1[0:65, 512 + qsub * 128:512 + (qsub + 1) * 128],
                                vv4[:, mc % 4, hB, :],
                                Ph[1][:, mc * 512 + qsub * 128:mc * 512 + (qsub + 1) * 128],
                                start=st_, stop=sp_, skip_group_check=True,
                                inc=(('T', 1) if sp_ else None))
                tav = self.val('T')
                tav_prev = tav
                if self.sub < 47:
                    continue
                for hh, base in ((hA, 0), (hB, 512)):
                    self.op('V', 'reciprocal', drl[0:1, :], pb1[64:65, base:base + 512],
                            wait=[('T', tav)] if base == 0 else None, inc=('V', 1))
                    vr1 = self.vfence()
                    if self.sub < 48:
                        continue
                    self.op('T', 'matmul', pb7[0:64, 0:512], self.ones1b[0:1, 0:64],
                            drl[0:1, :], start=True, stop=True,
                            wait=[('V', vr1)], inc=('T', 1))
                    tb1 = self.val('T')
                    self.op('V', 'tensor_copy', dnl[:, :], pb7[0:64, 0:512],
                            wait=[('T', tb1)])
                    if self.sub < 49:
                        continue
                    self.op('V', 'tensor_tensor',
                            Ow[(hh % 2) * 64:(hh % 2) * 64 + 64, pc * 512:(pc + 1) * 512],
                            pb1[0:64, base:base + 512], dnl[:, :], op=ALU.mult,
                            inc=('V', 1))
                self.wait('T', 'V', self.vfence())

            if self.sub < 50:
                vwin_prev = self.val('V')
                continue
            for oc in range(NCH):
                pbank = pb3 if oc % 2 == 0 else pb1
                for pc in range(4):
                    self.op('T', 'matmul', pbank[:, 0:512],
                            wo_sb[pc][:, oc * 128:(oc + 1) * 128],
                            Ow[:, pc * 512:(pc + 1) * 512],
                            start=(pc == 0), stop=(pc == 3),
                            inc=(('T', 1) if pc == 3 else None))
                tyw = self.val('T')
                wv = [('T', tyw)]
                if oc == 0 and zast_prev is not None:
                    wv += zast_prev
                if l == 0:
                    self.op('V', 'tensor_tensor', tmp1[:, :],
                            zw3[:, oc, 1:513], A0w[:, 1:513], op=ALU.mult, wait=wv)
                    self.op('V', 'tensor_tensor', tmp1[:, :],
                            tmp1[:, :], B0w[:, 1:513], op=ALU.add)
                    self.op('V', 'tensor_tensor', za3[:, oc, 1:513],
                            pbank[:, 0:512], tmp1[:, :], op=ALU.add, inc=('V', 1))
                else:
                    self.op('V', 'tensor_tensor', za3[:, oc, 1:513],
                            pbank[:, 0:512], zw3[:, oc, 1:513], op=ALU.add,
                            wait=wv, inc=('V', 1))
                self.wait('T', 'V', self.vfence())
            if w == 0 or self.sub == 61:
                self.op('V', 'memset', za3[:, :, 0:1], 0.0, inc=('V', 1))
            else:
                self.op('V', 'tensor_copy', za3[:, :, 0:1],
                        zhalo.rearrange("p (c n) -> p c n", n=1)[:, :, :], inc=('V', 1))
            if self.sub != 61:
                self.op('V', 'tensor_copy', zhalo.rearrange("p (c n) -> p c n", n=1)[:, :, :],
                        za3[:, :, 512:513], inc=('V', 1))
            else:
                self.op('V', 'nop', inc=('V', 1))
            vza = self.vfence()
            zast_prev = [self.dmaS(Rout[:, c, w * 512:(w + 1) * 512],
                                   za3[:, c, 1:513],
                                   wait=[('V', vza)] if c == 0 else None)
                         for c in range(NCH)]
            zstore += zast_prev
            if self.sub == 62:
                tstats_prev = self.val('T')
                vwin_prev = self.val('V')
                continue
            wzb = [('T', tstats_prev)] if tstats_prev else None
            self.op('V', 'tensor_copy', zab[:, :], za[:, :], wait=wzb)
            self.op('V', 'tensor_tensor', za2[:, :], zab[:, :], zab[:, :],
                    op=ALU.mult, inc=('V', 1))
            vst = self.vfence()
            self.wait('T', 'V', vst)
            for qty, (src3, off, cb) in enumerate(
                    [(zab, 1, 64), (zab, 0, 96), (za2, 1, 128), (za2, 0, 160)]):
                s3 = src3.rearrange("p (c n) -> p c n", n=513)
                crange = range(0, 4) if qty in (0, 2) else range(4, 8)
                for ms in range(4):
                    for ci, c in enumerate(crange):
                        st_ = stats_first[0]
                        stats_first[0] = False
                        bend = (w == 7 and qty == 3 and ms == 3 and ci == 3)
                        lastq = (qty == 3 and ms == 3 and ci == 3)
                        self.op('T', 'matmul',
                                pb0[:, cb + w * 4 + ms:cb + w * 4 + ms + 1],
                                s3[:, c, off + ms * 128:off + (ms + 1) * 128],
                                self.onesb[:, :], start=st_, stop=bend,
                                skip_group_check=True,
                                inc=(('T', 1) if lastq else None))
            tstats_prev = self.val('T')
            vwin_prev = self.val('V')

        if self.sub < 5 or self.nwin < 8:
            stF.close()
            stR.close()
            return
        self.op('V', 'tensor_copy', scmb[:, 0:32], pb0[:, 96:128],
                wait=[('T', tstats_prev)])
        self.op('V', 'tensor_copy', scmb[:, 32:64], pb0[:, 160:192])
        self.op('V', 'tensor_tensor', scmb[:, 0:32], scmb[:, 0:32], pb0[:, 64:96],
                op=ALU.add)
        self.op('V', 'tensor_tensor', scmb[:, 32:64], scmb[:, 32:64], pb0[:, 128:160],
                op=ALU.add, inc=('V', 1))
        rows2 = self._ln_rows(f'ln2_{l}', scmb, 0, 32, a2row, b2row,
                              ('V', self.val('V')), st=stF)
        if self.stage == 'F0q':
            stF.close()
            stR.close()
            return
        stF.close()

        stD = contextlib.ExitStack()
        dsb = lambda nm, shp, dt: self.sbuf(f'L{l}d{nm}', shp, dt, st=stD)
        gate2 = [('T', self.val('T')), ('V', self.val('V')),
                 ('S', self.val('S'))] + rows2 + self.allw('LS')
        w1s = [dsb(f'w1_{c}', [128, 5460], BF16) for c in range(NCH)]
        w2rows = [128] * 21 + [42]
        w2s = [dsb(f'w2_{j}', [w2rows[j], 1024], BF16) for j in range(NUC)]
        tw1 = []
        for c in range(NCH):
            tw1.append(self.dmaG(w1s[c][:, :], g[f'w1_{l}'][c * 128:(c + 1) * 128, :],
                                 wait=gate2 if c == 0 else None))
        for j in range(NUC):
            self.dmaG(w2s[j][:, :], g[f'w2_{l}'][j * 128:j * 128 + w2rows[j], :])
        wload2 = self.allw('LG')
        if self.sub == 74:
            self.waitl('G', wload2)
            stD.close()
            stR.close()
            return
        for c in range(NCH):
            self.op('V', 'tensor_scalar', w1s[c][:, :], w1s[c][:, :],
                    lgc[:, c:c + 1], None, op0=ALU.mult,
                    wait=[tw1[c], tlg], inc=('V', 1))
        vw1 = self.vfence()
        self.wait('T', 'V', vw1)
        if self.sub == 73:
            stD.close()
            stR.close()
            return
        a_cols = [(j * 128, min(2730, (j + 1) * 128)) for j in range(NUC)]
        g_cols = [(2730 + a, 2730 + b) for (a, b) in a_cols]
        for j in range(2 * NUC):
            lo, hi = (a_cols + g_cols)[j]
            m = hi - lo
            for c in range(NCH):
                self.op('T', 'matmul', pb7[0:m, j:j + 1], w1s[c][:, lo:hi],
                        lbb[:, c:c + 1], start=(c == 0), stop=(c == NCH - 1),
                        skip_group_check=True,
                        inc=(('T', 1) if c == NCH - 1 and j == 2 * NUC - 1 else None))
        tlb = self.val('T')
        b1l = self.dmaY(b1t[:, :], g[f'b1col{l}'])
        self.op('V', 'tensor_tensor', b1t[:, :], b1t[:, :], pb7[:, 0:44],
                op=ALU.add, wait=[('T', tlb), b1l], inc=('V', 1))
        self.wait('T', 'V', self.vfence())

        if self.sub == 71:
            stD.close()
            stR.close()
            return
        zsp = [dsb(f'zsp{i}', [128, 8 * 257], F32) for i in range(2)]
        zsp3 = [t.rearrange("p (c n) -> p c n", n=257) for t in zsp]
        x1sp = [dsb(f'x1sp{i}', [128, 8 * 256], F32) for i in range(2)] if l == 1 else None
        h2 = dsb('h2', [128, 8 * 256], BF16)
        A2w = dsb('A2w', [128, 256], F32)
        B2w = dsb('B2w', [128, 256], F32)
        usb = [dsb(f'u{i}', [128, 256], BF16) for i in range(2)]
        gel = dsb('gel', [128, 256], F32)
        gel_b = dsb('gelb', [128, 256], F32)
        zot = [dsb(f'zot{i}', [128, 256], F32) for i in range(2)]
        tmpd = dsb('tmpd', [128, 256], F32)

        NSP = {70: 2, 72: 1}.get(self.sub, 16)
        vsp_done = [0, 0]
        out_prev = {0: None, 1: None}
        store_out = []
        for sp in range(NSP):
            cur = sp % 2
            wl = list(zstore)
            if vsp_done[cur]:
                wl.append(('V', vsp_done[cur]))
            if sp == 0:
                wl += gate2 + wload2
                tz = [self.dmaY(zsp3[cur][:, c, 1:257], Rout[:, c, 0:256],
                                wait=wl if c == 0 else None) for c in range(NCH)]
                self.op('V', 'memset', zsp3[cur][:, :, 0:1], 0.0, wait=tz, inc=('V', 1))
            else:
                tz = [self.dmaY(zsp3[cur][:, c, :],
                                Rout[:, c, sp * 256 - 1:(sp + 1) * 256],
                                wait=wl if c == 0 else None) for c in range(NCH)]
            if l == 1:
                wx = [('V', vsp_done[cur])] if vsp_done[cur] else []
                if sp == 0:
                    wx += gate2
                x1v = x1sp[cur].rearrange("p (c n) -> p c n", n=256)
                tx1 = [self.dmaY(x1v[:, c, :],
                                 self.rr['R1'][:, c, sp * 256:(sp + 1) * 256],
                                 wait=(wx or None) if c == 0 else None)
                       for c in range(NCH)]
            vprev = self.vfence()
            self.op('T', 'matmul', pb7[:, 0:256], self.ones1b[:, :],
                    a2row[0:1, sp * 256:(sp + 1) * 256], start=True, stop=True,
                    wait=[('V', vprev)] + (rows2 if sp == 0 else []))
            self.op('T', 'matmul', pb7[:, 256:512], self.ones1b[:, :],
                    b2row[0:1, sp * 256:(sp + 1) * 256], start=True, stop=True,
                    inc=('T', 1))
            tb2 = self.val('T')
            self.op('V', 'tensor_copy', A2w[:, :], pb7[:, 0:256], wait=[('T', tb2)])
            self.op('V', 'tensor_copy', B2w[:, :], pb7[:, 256:512], inc=('V', 1))
            self.wait('T', 'V', self.vfence())
            for c in range(NCH):
                off = 0 if c >= 4 else 1
                wz = tz if c == 0 else None
                self.op('V', 'tensor_tensor', tmpd[:, :], zsp3[cur][:, c, off:off + 256],
                        A2w[:, :], op=ALU.mult, wait=wz)
                self.op('V', 'tensor_tensor', h2[:, c * 256:(c + 1) * 256],
                        tmpd[:, :], B2w[:, :], op=ALU.add,
                        inc=(('V', 1) if c == NCH - 1 else None))
            vh2 = self.vfence()
            self.wait('T', 'V', vh2)
            gel2 = [gel, gel_b]
            vus = {}
            def emit_ag(j):
                pag = pb5 if j % 2 == 0 else pb6
                alo, ahi = a_cols[j]
                glo, ghi = g_cols[j]
                m = ahi - alo
                for c in range(NCH):
                    self.op('T', 'matmul', pag[0:m, 0:256], w1s[c][:, alo:ahi],
                            h2[:, c * 256:(c + 1) * 256],
                            start=(c == 0), stop=(c == NCH - 1),
                            skip_group_check=True)
                    self.op('T', 'matmul', pag[0:m, 256:512], w1s[c][:, glo:ghi],
                            h2[:, c * 256:(c + 1) * 256],
                            start=False, stop=(c == NCH - 1),
                            skip_group_check=True,
                            inc=(('T', 1) if c == NCH - 1 else None))
                tag = self.val('T')
                self.op('S', 'activation', gel2[j % 2][0:m, :], pag[0:m, 256:512], AF.Gelu,
                        bias=b1t[0:m, 22 + j:23 + j], wait=[('T', tag)], inc=('S', 1))
                sgel = self.sfence()
                self.op('V', 'scalar_tensor_tensor', usb[j % 2][0:m, :],
                        pag[0:m, 0:256], b1t[0:m, j:j + 1], gel2[j % 2][0:m, :],
                        op0=ALU.add, op1=ALU.mult, wait=[('S', sgel)], inc=('V', 1))
                vus[j] = self.vfence()
            def emit_w2(j):
                m = a_cols[j][1] - a_cols[j][0]
                self.wait('T', 'V', vus[j])
                for oc2 in range(NCH):
                    pob = pb1 if oc2 < 4 else pb3
                    colo = (oc2 % 4) * 256
                    self.op('T', 'matmul', pob[:, colo:colo + 256],
                            w2s[j][:, oc2 * 128:(oc2 + 1) * 128], usb[j % 2][0:m, :],
                            start=(j == 0 and oc2 % 2 == 0), stop=(j == NUC - 1),
                            skip_group_check=True,
                            inc=(('T', 1) if oc2 == NCH - 1 else None))
            emit_ag(0)
            for j in range(1, NUC):
                emit_ag(j)
                emit_w2(j - 1)
            emit_w2(NUC - 1)
            tw2d = self.val('T')
            for oc2 in range(NCH):
                pob = pb1 if oc2 < 4 else pb3
                colo = (oc2 % 4) * 256
                zb = zot[oc2 % 2]
                wv = [('T', tw2d)] if oc2 == 0 else []
                if oc2 == 0 and sp == 0:
                    wv.append(tb2c)
                if out_prev.get(oc2 % 2) is not None:
                    wv.append(out_prev[oc2 % 2])
                self.op('V', 'scalar_tensor_tensor', zb[:, :],
                        pob[:, colo:colo + 256], b2c[:, oc2:oc2 + 1],
                        zsp3[cur][:, oc2, 1:257], op0=ALU.add, op1=ALU.add,
                        wait=wv or None)
                if l == 1:
                    self.op('V', 'tensor_tensor', zb[:, :], zb[:, :],
                            x1sp[cur][:, oc2 * 256:(oc2 + 1) * 256], op=ALU.add,
                            wait=tx1 if oc2 == 0 else None)
                vzo = self.vfence()
                tkst = self.dmaS(Rz[:, oc2, sp * 256:(sp + 1) * 256], zb[:, :],
                                 wait=[('V', vzo)])
                out_prev[oc2 % 2] = tkst
                store_out.append(tkst)
            self.wait('T', 'V', self.vfence())
            vsp_done[cur] = self.val('V')
        self.ffn_gate = [('T', self.val('T')), ('V', self.val('V')),
                         ('S', self.val('S'))] + store_out
        stD.close()
        stR.close()

    def _finish(self):
        for att in ('stA', 'stH0'):
            st = getattr(self, att, None)
            if st is not None:
                st.close()
        toks = self.allw('LS') + self.allw('LG') + self.allw('LH')
        self.waitl('G', toks)


_PROG_CACHE = {}


def _build(stage, nwin=8, sub=50):
    mk = MK(stage, nwin, sub)
    nc = mk.build()
    return nc


def kernel(**inputs):
    sh, cores = host_prep(inputs)
    nc = _build(BUILD_STAGE)
    in_maps = []
    for b in range(8):
        m = dict(sh)
        m.update(cores[b])
        in_maps.append(m)
    res = run_bass_kernel_spmd(nc, in_maps, core_ids=list(range(8)))
    outs = []
    key = 'R1' if BUILD_STAGE == 'A' else 'outT'
    for r in res.results:
        outs.append(np.ascontiguousarray(r[key].T))
    return np.stack(outs, axis=0)


# version 09 (final)
# speedup vs baseline: 1.2962x; 1.0699x over previous
import contextlib
import numpy as np
import ml_dtypes
import concourse.bass as bass
import concourse.mybir as mybir
from concourse.bass_utils import run_bass_kernel_spmd
from concourse.alu_op_type import AluOpType as ALU

F32 = mybir.dt.float32
BF16 = mybir.dt.bfloat16
AF = mybir.ActivationFunctionType

N = 4096
D = 1024
NCH = 8
NQB = 8
E = 64
NGRP = 4
HID = 2730
NUC = 22
BUILD_STAGE = 'full'


def _rotary_tables():
    half = np.arange(0, 64, 2, dtype=np.float64)
    inv_freq = 1.0 / 10000.0 ** (half / 64.0)
    t = np.arange(N, dtype=np.float64)
    freqs = np.concatenate([t[:, None] * inv_freq[None, :]] * 2, axis=-1)
    base = (half + 0.4 * 64.0) / (1.4 * 64.0)
    power = (t - N // 2) / 256.0
    scale = np.concatenate([base[None, :] ** power[:, None]] * 2, axis=-1)
    cos = np.cos(freqs); sin = np.sin(freqs)
    sgn = np.concatenate([-np.ones(32), np.ones(32)])[None, :]
    def pack(tab):
        return np.ascontiguousarray(np.concatenate([tab.T, tab.T], axis=0), np.float32)
    return (pack(cos * scale), pack(sin * scale * sgn),
            pack(cos / scale), pack(sin / scale * sgn))


def _rot_perm(sv):
    P = np.zeros((128, 128), dtype=np.float32)
    for half in (0, 1):
        for ep in range(64):
            e = (ep + 32) % 64
            P[half * 64 + e, half * 64 + ep] = sv[e]
    return P


def _col_tiles(v, width):
    out = np.zeros((128, width), dtype=np.float32)
    nfull = len(v) // 128
    for c in range(nfull):
        out[:, c] = v[c * 128:(c + 1) * 128]
    rem = len(v) - nfull * 128
    if rem:
        out[:rem, nfull] = v[nfull * 128:]
    return np.ascontiguousarray(out)


def host_prep(ins):
    f = np.float32
    sh = {}
    Wq, Wk, Wv = ins['L0_attn_Wq'], ins['L0_attn_Wk'], ins['L0_attn_Wv']
    sh['WqT'] = np.ascontiguousarray(np.asarray(Wq, f).transpose(1, 0, 2).reshape(D, 1024))
    sh['WkT'] = np.ascontiguousarray(np.asarray(Wk, f).transpose(1, 0, 2).reshape(D, 1024))
    sh['WvT'] = np.ascontiguousarray(np.asarray(Wv, f).transpose(1, 0, 2).reshape(D, 1024))
    sh['Wpp'] = np.ascontiguousarray(
        np.asarray(ins['L0_attn_Wp'], f).reshape(64, 16, 1024).transpose(1, 0, 2).reshape(1024, 1024))
    sh['bp_col'] = np.ascontiguousarray(np.asarray(ins['L0_attn_bp'], f).reshape(8, 128).T)
    for l in range(2):
        sh[f'wqkv{l}'] = np.ascontiguousarray(np.asarray(ins['L0_ffn_wqkv'][l], f))
        sh[f'wo{l}'] = np.ascontiguousarray(np.asarray(ins['L0_ffn_wo'][l], f))
        sh[f'w1_{l}'] = np.ascontiguousarray(np.asarray(ins['L0_ffn_w1'][l], f))
        sh[f'w2_{l}'] = np.ascontiguousarray(np.asarray(ins['L0_ffn_w2'][l], f))
        sh[f'b1col{l}'] = _col_tiles(np.asarray(ins['L0_ffn_b1'][l], f), 44)
        sh[f'b2col{l}'] = np.ascontiguousarray(np.asarray(ins['L0_ffn_b2'][l], f).reshape(8, 128).T)
        sh[f'lgcol{l}'] = np.ascontiguousarray(np.asarray(ins['L0_ffn_ln_g'][l], f).reshape(8, 128).T)
        sh[f'lbcol{l}'] = np.ascontiguousarray(np.asarray(ins['L0_ffn_ln_b'][l], f).reshape(8, 128).T)
        qs = np.asarray(ins['L0_ffn_q_scale'][l], f)
        ks = np.asarray(ins['L0_ffn_k_scale'][l], f)
        sh[f'qs_t{l}'] = np.ascontiguousarray(np.tile(qs, 2)[:, None])
        sh[f'ks_t{l}'] = np.ascontiguousarray(np.tile(ks, 2)[:, None])
        sh[f'PQ{l}'] = _rot_perm(qs)
        sh[f'PK{l}'] = _rot_perm(ks)
    sh['TAc'], sh['TAs'], sh['TBc'], sh['TBs'] = _rotary_tables()
    pm = np.arange(128)[:, None]; pq = np.arange(128)[None, :]
    sh['M0'] = np.ascontiguousarray(np.where(pq >= pm, 0.0, -1e30).astype(f))
    sh['M512'] = np.ascontiguousarray(np.where(pq <= pm, 0.0, -1e30).astype(f))
    sh['identb'] = np.ascontiguousarray(np.eye(128, dtype=ml_dtypes.bfloat16))
    E2 = np.zeros((2, 128), dtype=ml_dtypes.bfloat16)
    E2[0, 0:64] = 1; E2[1, 64:128] = 1
    sh['E2h'] = E2
    x = ins['L0_attn_query']
    cores = [{'xT': np.ascontiguousarray(np.asarray(x[b], f).T)} for b in range(8)]
    return sh, cores


class MK:
    def __init__(self, stage='full', nwin=8, sub=50):
        self.nwin = nwin
        self.sub = sub
        self.stage = stage
        self.nc = bass.Bass("TRN2", target_bir_lowering=False, debug=False)
        self.es = contextlib.ExitStack()
        self.q = {'T': [], 'V': [], 'S': [], 'G': [], 'Y': []}
        self.cnt = {}

    def sbuf(self, name, shape, dtype, st=None):
        return (st or self.es).enter_context(self.nc.sbuf_tensor(name, shape, dtype))

    def psum(self, name, shape, dtype=F32):
        return self.es.enter_context(self.nc.psum_tensor(name, shape, dtype))

    def sem(self, name):
        h = self.es.enter_context(self.nc.semaphore(name))
        self.cnt[name] = 0
        self.semh = getattr(self, 'semh', {})
        self.semh[name] = h
        return h

    def emit(self, eng, fn):
        self.q[eng].append(fn)

    def bump(self, sem, k=1):
        self.cnt[sem] += k
        return self.cnt[sem]

    def val(self, sem):
        return self.cnt[sem]

    def op(self, eng, meth, *a, inc=None, wait=None, **kw):
        if inc is not None:
            incv = [(s, k) for (s, k) in ([inc] if isinstance(inc, tuple) else inc) if s]
        else:
            incv = []
        wait = list(wait) if wait else []
        if eng in ('V', 'S') and meth != 'dma_start':
            wait.append((eng, self.cnt[eng]))
            incv.append((eng, 1))
        merged = {}
        for sm, k in incv:
            merged[sm] = merged.get(sm, 0) + k
        incv = list(merged.items())
        for sm, k in incv:
            self.bump(sm, k)
        semh = self.semh
        def fn(e):
            for sm, v in wait:
                e.wait_ge(semh[sm], v)
            instr = getattr(e, meth)(*a, **kw)
            for sm, k in incv:
                instr.then_inc(semh[sm], k)
        self.emit(eng, fn)

    def wait(self, eng, sem, v):
        semh = self.semh
        self.emit(eng, lambda e: e.wait_ge(semh[sem], v))

    def vfence(self):
        self.op('V', 'tensor_copy', self.fsc[0:1, 1:2], self.fsc[0:1, 0:1])
        return self.cnt['V']

    def sfence(self):
        self.op('S', 'copy', self.fsc[0:1, 3:4], self.fsc[0:1, 2:3])
        return self.cnt['S']

    def waitl(self, eng, tokens):
        semh = self.semh
        toks = list(tokens)
        def fn(e):
            for sm, v in toks:
                e.wait_ge(semh[sm], v)
        self.emit(eng, fn)

    NLANES = 4

    def _dma(self, eng, queue, dst, src, wait=None, accum=False):
        st = self.lanes[queue]
        lane = st['rr']
        st['rr'] = (lane + 1) % self.NLANES
        name = f'{queue}{lane}'
        prev = self.cnt[name]
        wait = list(wait) if wait else []
        if prev > 0:
            wait.append((name, prev))
        kw = {'out': dst, 'in_': src}
        if accum:
            kw['accum_op'] = ALU.add
        self.op(eng, 'dma_start', inc=(name, 16), wait=wait, **kw)
        return (name, self.cnt[name])

    def dmaY(self, dst, src, wait=None):
        return self._dma('Y', 'LH', dst, src, wait)

    def dmaS(self, dst, src, wait=None):
        return self._dma('S', 'LS', dst, src, wait)

    def dmaG(self, dst, src, wait=None, accum=False):
        return self._dma('G', 'LG', dst, src, wait, accum)

    def allw(self, queue):
        return [(f'{queue}{i}', self.cnt[f'{queue}{i}'])
                for i in range(self.NLANES) if self.cnt[f'{queue}{i}'] > 0]

    def build(self):
        nc = self.nc
        with self.es:
            self._declare_dram()
            self._globals()
            blk = self.es.enter_context(nc.Block())
            self._stage_a()
            if self.stage in ('F0', 'F0q'):
                self._ffn_layer(0)
            elif self.stage == 'full':
                self._ffn_layer(0)
                self._ffn_layer(1)
            self._finish()
            with nc.allow_low_precision(reason="bf16 matmul operands by design"):
                for tag, attr in [('T', 'tensor'), ('V', 'vector'), ('S', 'scalar'),
                                  ('G', 'gpsimd'), ('Y', 'sync')]:
                    fns = self.q[tag]
                    def body(eng, fns=fns):
                        for fn in fns:
                            fn(eng)
                    getattr(blk, attr)(body)
        return nc

    def _declare_dram(self):
        nc = self.nc
        g = self.g = {}
        def d(name, shape, dtype=F32, kind='ExternalInput'):
            g[name] = nc.dram_tensor(name, shape, dtype, kind=kind).ap()
        d('xT', [D, N])
        for k, shp in [('WqT', [D, 1024]), ('WkT', [D, 1024]), ('WvT', [D, 1024]),
                       ('Wpp', [1024, 1024]), ('bp_col', [128, 8]),
                       ('TAc', [128, N]), ('TAs', [128, N]), ('TBc', [128, N]),
                       ('TBs', [128, N]), ('M0', [128, 128]), ('M512', [128, 128])]:
            d(k, shp)
        d('identb', [128, 128], BF16)
        d('E2h', [2, 128], BF16)
        for l in range(2):
            for k, shp in [(f'wqkv{l}', [D, 1536]), (f'wo{l}', [512, 1024]),
                           (f'w1_{l}', [D, 5460]), (f'w2_{l}', [HID, 1024]),
                           (f'b1col{l}', [128, 44]), (f'b2col{l}', [128, 8]),
                           (f'lgcol{l}', [128, 8]), (f'lbcol{l}', [128, 8]),
                           (f'qs_t{l}', [128, 1]), (f'ks_t{l}', [128, 1]),
                           (f'PQ{l}', [128, 128]), (f'PK{l}', [128, 128])]:
                d(k, shp)
        d('yacc', [D, N], kind='ExternalOutput' if self.stage in ('Y', 'Y3') else 'Internal')
        d('rowsc', [2, 4096], kind='Internal')
        d('R1', [D, N], kind='ExternalOutput' if self.stage == 'A' else 'Internal')
        if self.stage == 'A0':
            d('dbgA0', [128, 64 + N], F32, 'ExternalOutput')
        if self.stage == 'A1':
            d('dbgA1', [128, 2 * N], BF16, 'ExternalOutput')
        if self.stage == 'KV':
            d('dbgKV', [128, 2 * N + 32 * 4 * 65], BF16, 'ExternalOutput')
            d('dbgW', [128, 8 * 256 * 2], BF16, 'ExternalOutput')
        if self.stage == 'X1':
            d('dbgX', [128, 4 * 512], F32, 'ExternalOutput')
        if self.stage == 'CAN':
            d('dbgC', [128, 32 * 16], F32, 'ExternalOutput')
        if self.stage in ('QOST', 'QOST3'):
            d('dbgQ', [128, 2 * 512 + 1024 + 2 * 1024], BF16, 'ExternalOutput')
        d('RA0', [D, N], kind='ExternalOutput' if self.stage == 'F0q' else 'Internal')
        d('RZ1', [D, N], kind='ExternalOutput' if self.stage == 'F0' else 'Internal')
        d('RA1', [D, N], kind='Internal')
        if self.stage != 'A':
            d('outT', [D, N], F32, 'ExternalOutput')
        self.rr = {k: g[k].rearrange("(c p) n -> p c n", p=128)
                   for k in ['xT', 'yacc', 'R1', 'RA0', 'RZ1', 'RA1']
                   + (['outT'] if self.stage != 'A' else [])}

    def _globals(self):
        for s in ['T', 'V', 'S']:
            self.sem(s)
        self.lanes = {}
        for q in ['LH', 'LS', 'LG']:
            self.lanes[q] = {'rr': 0}
            for i in range(self.NLANES):
                self.sem(f'{q}{i}')
        self.onesb = self.sbuf('onesb', [128, 1], BF16)
        self.ones1b = self.sbuf('ones1b', [1, 128], BF16)
        self.bd = self.sbuf('bd', [128, 2], BF16)
        self.E2 = self.sbuf('E2', [2, 128], BF16)
        self.identb_sb = self.sbuf('identb_sb', [128, 128], BF16)
        self.bp_sb = self.sbuf('bp_sb', [128, 8], F32)
        self.M0_sb = self.sbuf('M0_sb', [128, 128], F32)
        self.M512_sb = self.sbuf('M512_sb', [128, 128], F32)
        self.cns = self.sbuf('cns', [128, 4], F32)
        self.fsc = self.sbuf('fsc', [1, 4], F32)
        if self.stage == 'CAN':
            self.cansb = self.sbuf('cansb', [128, 32 * 16], F32)

    def canary(self, idx):
        if self.stage != 'CAN' or idx >= 16:
            return
        if not hasattr(self, '_can_init'):
            self.op('V', 'memset', self.cansb[:, :], -7.0)
            self._can_init = True
        c = self.cansb[:, idx * 32:idx * 32 + 32]
        self.op('V', 'tensor_copy', c[:, 0:8], self.bp_sb[:, :], wait=[('V', self.val('V'))])
        self.op('V', 'tensor_copy', c[:, 8:9], self.onesb[:, :])
        self.op('V', 'tensor_copy', c[:, 9:10], self.ones1b[0:1, :].rearrange("o (a b) -> o a b", b=1)[:, 0, :] if False else self.cns[:, 0:1])
        self.op('V', 'tensor_copy', c[:, 10:26], self.identb_sb[:, 0:16], inc=('V', 1))

    def _load_consts(self):
        g = self.g
        for dst, src in [(self.bp_sb, g['bp_col']), (self.M0_sb, g['M0']),
                         (self.M512_sb, g['M512']), (self.identb_sb, g['identb']),
                         (self.E2, g['E2h'])]:
            self.dmaY(dst[:, :], src)
        w = self.allw('LH')
        for t, v in [(self.onesb, 1.0), (self.ones1b, 1.0)]:
            self.op('V', 'memset', t[:, :], v)
        self.op('V', 'memset', self.bd[:, :], 0.0)
        self.op('V', 'memset', self.bd[0:64, 0:1], 1.0)
        self.op('V', 'memset', self.bd[64:128, 1:2], 1.0)
        self.op('V', 'memset', self.fsc[:, :], 0.0)
        self.op('V', 'memset', self.cns[:, 0:1], 1e-5)
        self.op('V', 'memset', self.cns[:, 1:2], 1e-24)
        self.op('V', 'memset', self.cns[:, 2:3], 2.079441542, inc=('V', 1))
        self.consts_v = self.val('V')
        self.consts_lh = w

    def _ln_rows(self, tag, psums, cS, cS2, arow, brow, wait_t, st=None):
        mc = self.sbuf(f'{tag}_mc', [128, 32], F32, st=st)
        vr = self.sbuf(f'{tag}_vr', [128, 32], F32, st=st)
        rc = self.sbuf(f'{tag}_rc', [128, 32], F32, st=st)
        ab = self.sbuf(f'{tag}_ab', [128, 64], BF16, st=st)
        tr = self.sbuf(f'{tag}_tr', [32, 256], BF16, st=st)
        ptr = self.ptr_bf
        self.op('V', 'tensor_scalar', mc[:, :], psums[:, cS:cS + 32], 1.0 / D, None,
                op0=ALU.mult, wait=[wait_t])
        self.op('V', 'tensor_scalar', vr[:, :], psums[:, cS2:cS2 + 32], 1.0 / D, None, op0=ALU.mult)
        self.op('V', 'tensor_tensor', rc[:, :], mc[:, :], mc[:, :], op=ALU.mult)
        self.op('V', 'tensor_tensor', vr[:, :], vr[:, :], rc[:, :], op=ALU.subtract,
                inc=('V', 1))
        v1 = self.vfence()
        self.op('S', 'activation', vr[:, :], vr[:, :], AF.Ln, bias=self.cns[:, 0:1], wait=[('V', v1)])
        self.op('S', 'activation', rc[:, :], vr[:, :], AF.Exp, scale=-0.5, inc=('S', 1))
        s1 = self.sfence()
        self.op('V', 'tensor_copy', ab[:, 0:32], rc[:, :], wait=[('S', s1)])
        self.op('V', 'scalar_tensor_tensor', ab[:, 32:64], mc[:, :], -1.0, rc[:, :],
                op0=ALU.mult, op1=ALU.mult, inc=('V', 1))
        v2 = self.vfence()
        self.op('T', 'transpose', ptr[0:32, 0:128], ab[:, 0:32], self.identb_sb[:, :],
                wait=[('V', v2)])
        self.op('T', 'transpose', ptr[0:32, 128:256], ab[:, 32:64], self.identb_sb[:, :],
                inc=('T', 1))
        t1 = self.val('T')
        self.op('V', 'tensor_copy', tr[:, 0:128], ptr[0:32, 0:128], wait=[('T', t1)])
        self.op('V', 'tensor_copy', tr[:, 128:256], ptr[0:32, 128:256], inc=('V', 1))
        v3 = self.vfence()
        rs = self.g['rowsc']
        rsa = rs[0:1, :].rearrange("o (r n) -> (o r) n", n=128)
        rsb = rs[1:2, :].rearrange("o (r n) -> (o r) n", n=128)
        ta = self.dmaG(rsa, tr[:, 0:128], wait=[('V', v3)])
        tb = self.dmaG(rsb, tr[:, 128:256])
        t3 = self.dmaG(arow[0:1, :], rs[0:1, :], wait=[ta])
        t4 = self.dmaG(brow[0:1, :], rs[1:2, :], wait=[tb])
        return [t3, t4]

    def _stage_a(self):
        g = self.g
        self._load_consts()
        self.pb0 = self.psum('pb0', [128, 512])
        self.pb1 = self.psum('pb1', [128, 1024])
        self.pb3 = self.psum('pb3', [128, 1024])
        self.pb5 = self.psum('pb5', [128, 512])
        self.pb6 = self.psum('pb6', [128, 512])
        self.pb7 = self.psum('pb7', [128, 512])
        self.ptr_bf = self.pb7.bitcast(BF16)
        pb0, pb1, pb3 = self.pb0, self.pb1, self.pb3
        pOa, pOb, pm = self.pb5, self.pb6, self.pb7

        self.stH0 = contextlib.ExitStack()
        arow0 = self.sbuf('arow0', [1, N], BF16, st=self.stH0)
        brow0 = self.sbuf('brow0', [1, N], BF16, st=self.stH0)
        self.arow0, self.brow0 = arow0, brow0
        stA = self.stA = contextlib.ExitStack()
        h = [self.sbuf(f'h{c}', [128, N], BF16, st=stA) for c in range(NCH)]
        sub = contextlib.ExitStack()
        xspan = [self.sbuf(f'xspan{i}', [128, 8 * 512], F32, st=sub) for i in range(2)]
        xb = [self.sbuf(f'xbspan{i}', [128, 8 * 512], BF16, st=sub) for i in range(2)]
        x2 = [self.sbuf(f'x2span{i}', [128, 8 * 512], BF16, st=sub) for i in range(2)]
        arow = self.sbuf('arowA', [1, N], BF16, st=sub)
        brow = self.sbuf('browA', [1, N], BF16, st=sub)

        vcopy_done = [0] * 8
        tsum_done = [0] * 8
        for s in range(8):
            b = s % 2
            wl = [('V', vcopy_done[s - 2])] if s >= 2 else None
            lh = self.dmaY(xspan[b].rearrange("p (c n) -> p c n", n=512)[:, :, :],
                           self.rr['xT'][:, :, s * 512:(s + 1) * 512], wait=wl)
            wv = ([lh] if s > 0 else self.allw('LH')) + \
                 ([('T', tsum_done[s - 2])] if s >= 2 else [])
            self.op('V', 'tensor_copy', xb[b][:, :], xspan[b][:, :], wait=wv)
            self.op('V', 'tensor_tensor', x2[b][:, :], xb[b][:, :], xb[b][:, :],
                    op=ALU.mult, inc=('V', 1))
            vcopy_done[s] = self.vfence()
            self.wait('T', 'V', vcopy_done[s])
            for c in range(NCH):
                for ms in range(4):
                    st = (s == 0 and c == 0 and ms == 0)
                    last = (c == NCH - 1 and ms == 3)
                    burst_end = (s == 7 and last)
                    self.op('T', 'matmul', pb0[:, s * 4 + ms:s * 4 + ms + 1],
                            xb[b][:, c * 512 + ms * 128: c * 512 + (ms + 1) * 128],
                            self.onesb[:, :], start=st, stop=False, skip_group_check=True)
                    self.op('T', 'matmul', pb0[:, 32 + s * 4 + ms:33 + s * 4 + ms],
                            x2[b][:, c * 512 + ms * 128: c * 512 + (ms + 1) * 128],
                            self.onesb[:, :], start=False, stop=burst_end, skip_group_check=True,
                            inc=(('T', 1) if last else None))
            tsum_done[s] = self.val('T')
        rows_done = self._ln_rows('lnA', pb0, 0, 32, arow, brow, ('T', self.val('T')), st=sub)
        if self.stage == 'A0':
            dbg = self.g['dbgA0']
            sums_sb = self.sbuf('dbg_sums', [128, 64], F32, st=sub)
            self.op('V', 'tensor_copy', sums_sb[:, :], pb0[:, 0:64],
                    wait=[rows_done], inc=('V', 1))
            vv = self.val('V')
            self.dmaS(dbg[0:128, 0:64], sums_sb[:, :], wait=[('V', vv)])
            self.dmaS(dbg[0:1, 64:64 + N // 2].bitcast(BF16), arow[0:1, :])
            self.dmaS(dbg[1:2, 64:64 + N // 2].bitcast(BF16), brow[0:1, :])
            sub.close()
            return

        hbuild_done = [0] * 8
        for s in range(8):
            b = s % 2
            wl = [('V', hbuild_done[s - 2])] if s >= 2 else [('V', vcopy_done[7])]
            lh = self.dmaY(xspan[b].rearrange("p (c n) -> p c n", n=512)[:, :, :],
                           self.rr['xT'][:, :, s * 512:(s + 1) * 512], wait=wl)
            wt = [('V', hbuild_done[s - 1] if s >= 1 else vcopy_done[7])]
            if s == 0:
                wt += rows_done
            self.op('T', 'matmul', pb1[:, 0:512], self.ones1b[:, :],
                    arow[0:1, s * 512:(s + 1) * 512], start=True, stop=True, wait=wt)
            self.op('T', 'matmul', pb1[:, 512:1024], self.ones1b[:, :],
                    brow[0:1, s * 512:(s + 1) * 512], start=True, stop=True, inc=('T', 1))
            tb = self.val('T')
            self.waitl('V', [lh])
            self.wait('V', 'T', tb)
            for c in range(NCH):
                self.op('V', 'tensor_tensor', xb[b][:, 0:512],
                        xspan[b][:, c * 512:(c + 1) * 512], pb1[:, 0:512], op=ALU.mult)
                self.op('V', 'tensor_tensor', h[c][:, s * 512:(s + 1) * 512],
                        xb[b][:, 0:512], pb1[:, 512:1024], op=ALU.add,
                        inc=(('V', 1) if c == NCH - 1 else None))
            hbuild_done[s] = self.vfence()
        h_done = self.vfence()
        self.canary(0)
        h_done = self.val('V')
        if self.stage == 'A1':
            dbg = self.g['dbgA1']
            self.dmaS(dbg[:, 0:N], h[0][:, :], wait=[('V', h_done)])
            self.dmaS(dbg[:, N:2 * N], h[5][:, :])
            sub.close()
            return
        sub.close()

        KT = [self.sbuf(f'KTg{p}', [128, N], BF16, st=stA) for p in range(2)]
        Vg = self.sbuf('Vg', [128, 32 * 4 * 65], BF16, st=stA)
        Vg4 = Vg.rearrange("p (m h e) -> p m h e", h=4, e=65)
        Wq_g = [self.sbuf(f'Wqg{c}', [128, 256], BF16, st=stA) for c in range(NCH)]
        Wk_g = [self.sbuf(f'Wkg{c}', [128, 256], BF16, st=stA) for c in range(NCH)]
        Wv_g = [self.sbuf(f'Wvg{c}', [128, 256], BF16, st=stA) for c in range(NCH)]
        Wp_g = [self.sbuf(f'Wpg{p}', [128, 1024], BF16, st=stA) for p in range(2)]
        Qg = self.sbuf('Qg', [128, 2 * 512], BF16, st=stA)
        Pa = self.sbuf('Pa', [128, 1024], BF16, st=stA)
        Pb = self.sbuf('Pb', [128, 1024], BF16, st=stA)
        dn = self.sbuf('dn', [64, 512], F32, st=stA)
        drow = self.sbuf('drow', [1, 512], BF16, st=stA)
        drowf = self.sbuf('drowf', [1, 512], F32, st=stA)
        Ost = self.sbuf('Ost', [128, 2 * 512], BF16, st=stA)
        ysb = [self.sbuf(f'ysb{i}', [128, 512], F32, st=stA) for i in range(2)]
        yac = [self.sbuf(f'yac{i}', [128, 512], F32, st=stA) for i in range(2)]
        xsp = [self.sbuf(f'xsp{i}', [128, 512], F32, st=stA) for i in range(2)]
        x1f = [self.sbuf(f'x1f{i}', [128, 512], F32, st=stA) for i in range(2)]
        x1b = [self.sbuf(f'x1b{i}', [128, 512], BF16, st=stA) for i in range(2)]
        x1s = [self.sbuf(f'x1s{i}', [128, 512], BF16, st=stA) for i in range(2)]
        ylastlg = [None, None]
        yregion = {}
        yaclastv = [0, 0]
        x1_store_ls = [None, None]
        x1_sums_t = [0, 0]
        h0_first = [True]

        ngrp = {'Y': 1, 'Y3': 3}.get(self.stage, NGRP)
        for grp in range(ngrp):
            for c in range(NCH):
                self.dmaG(Wq_g[c][:, :], g['WqT'][c * 128:(c + 1) * 128, grp * 256:(grp + 1) * 256],
                          wait=([('V', h_done)] if grp == 0 and c == 0 else None))
                self.dmaG(Wk_g[c][:, :], g['WkT'][c * 128:(c + 1) * 128, grp * 256:(grp + 1) * 256])
                self.dmaG(Wv_g[c][:, :], g['WvT'][c * 128:(c + 1) * 128, grp * 256:(grp + 1) * 256])
            for p in range(2):
                self.dmaG(Wp_g[p][:, :],
                          g['Wpp'][grp * 256 + p * 128: grp * 256 + (p + 1) * 128, :])
            lgw = self.allw('LG')

            self.op('V', 'memset', Vg4[:, :, :, 64:65], 1.0, inc=('V', 1))
            vones = self.vfence()
            self.waitl('T', lgw)
            self.wait('T', 'V', max(h_done, vones))
            kvtgt = [pm[:, :], pOa[:, :], pOb[:, :], pb1[:, 0:512]]
            kv_ev = {}
            kv0v = self.vfence()
            kv0s = self.sfence()
            gi = [0]
            def kv_group(mms, evac):
                i = gi[0] % 4
                if gi[0] - 4 in kv_ev:
                    w = [('V', kv_ev[gi[0] - 4])]
                else:
                    w = [('V', kv0v), ('S', kv0s)]
                if w:
                    self.waitl('T', w)
                mms(kvtgt[i])
                tk = self.val('T')
                evac(kvtgt[i], tk)
                kv_ev[gi[0]] = self.vfence()
                gi[0] += 1
            for nc2 in range(NCH):
                for p in range(2):
                    def mms(tgt, nc2=nc2, p=p):
                        for c in range(NCH):
                            self.op('T', 'matmul', tgt, Wk_g[c][:, p * 128:(p + 1) * 128],
                                    h[c][:, nc2 * 512:(nc2 + 1) * 512],
                                    start=(c == 0), stop=(c == NCH - 1),
                                    inc=(('T', 1) if c == NCH - 1 else None))
                    def evac(tgt, tk, nc2=nc2, p=p):
                        self.op('V', 'tensor_copy', KT[p][:, nc2 * 512:(nc2 + 1) * 512],
                                tgt, wait=[('T', tk)], inc=('V', 1))
                    kv_group(mms, evac)
                for ms in range(4):
                    def mms(tgt, nc2=nc2, ms=ms):
                        for c in range(NCH):
                            self.op('T', 'matmul', tgt[0:128, 0:256],
                                    h[c][:, nc2 * 512 + ms * 128: nc2 * 512 + (ms + 1) * 128],
                                    Wv_g[c][:, :], start=(c == 0), stop=(c == NCH - 1),
                                    inc=(('T', 1) if c == NCH - 1 else None))
                    def evac(tgt, tk, nc2=nc2, ms=ms):
                        mcg = nc2 * 4 + ms
                        self.op('V', 'tensor_copy', Vg4[:, mcg, :, 0:64],
                                tgt[0:128, 0:256].rearrange("p (hh e) -> p hh e", e=64),
                                wait=[('T', tk)], inc=('V', 1))
                    kv_group(mms, evac)
            self.wait('T', 'V', kv_ev[gi[0] - 1])

            if self.stage == 'KV' and grp == 0:
                dbg = self.g['dbgKV']
                vlast = self.vfence()
                self.dmaS(dbg[:, 0:N], KT[0][:, :], wait=[('V', vlast)])
                self.dmaS(dbg[:, N:2 * N], KT[1][:, :])
                self.dmaS(dbg[:, 2 * N:], Vg[:, :])
                dW = self.g['dbgW']
                for c in range(NCH):
                    self.dmaS(dW[:, c * 256:(c + 1) * 256], Wk_g[c][:, :])
                    self.dmaS(dW[:, 2048 + c * 256:2048 + (c + 1) * 256], Wv_g[c][:, :])
                return
            self.canary(1 + grp * 3)
            for qb in range(NQB):
                if qb == 1:
                    self.canary(2 + grp * 3)
                for p in range(2):
                    for c in range(NCH):
                        self.op('T', 'matmul', pm[:, :], Wq_g[c][:, p * 128:(p + 1) * 128],
                                h[c][:, qb * 512:(qb + 1) * 512],
                                start=(c == 0), stop=(c == NCH - 1),
                                inc=(('T', 1) if c == NCH - 1 else None))
                    tq = self.val('T')
                    self.op('V', 'tensor_copy', Qg[:, p * 512:(p + 1) * 512], pm[:, :],
                            wait=[('T', tq)], inc=('V', 1))
                    self.wait('T', 'V', self.vfence())
                for p in range(2):
                    sexp = {}
                    tavm = {}
                    p0exp = self.val('S')
                    for mc in range(33):
                        if mc < 32:
                            Sp = pb1 if mc % 2 == 0 else pb3
                            Pj = Pa if mc % 2 == 0 else Pb
                            wS = [('S', sexp[mc - 2])] if mc >= 2 else [('S', p0exp)]
                            self.op('T', 'matmul', Sp[:, 0:512],
                                    KT[p][0:64, mc * 128:(mc + 1) * 128],
                                    Qg[0:64, p * 512:(p + 1) * 512],
                                    start=True, stop=True, tile_position=(0, 0),
                                    wait=wS)
                            self.op('T', 'matmul', Sp[:, 512:1024],
                                    KT[p][64:128, mc * 128:(mc + 1) * 128],
                                    Qg[64:128, p * 512:(p + 1) * 512],
                                    start=True, stop=True, tile_position=(64, 0),
                                    inc=('T', 1))
                            ts = self.val('T')
                            we = [('T', ts)]
                            if mc - 2 in tavm:
                                we.append(('T', tavm[mc - 2]))
                            self.op('S', 'activation', Pj[:, :], Sp[:, :], AF.Exp,
                                    scale=0.125, wait=we)
                            sexp[mc] = self.val('S')
                        if mc >= 1:
                            j = mc - 1
                            Pj = Pa if j % 2 == 0 else Pb
                            self.wait('T', 'S', sexp[j])
                            self.op('T', 'matmul', pOa[0:65, :],
                                    Vg4[:, j, 2 * p, :], Pj[:, 0:512],
                                    start=(j == 0), stop=(j == 31))
                            self.op('T', 'matmul', pOb[0:65, :],
                                    Vg4[:, j, 2 * p + 1, :], Pj[:, 512:1024],
                                    start=(j == 0), stop=(j == 31),
                                    inc=('T', 1))
                            tavm[j] = self.val('T')
                    tav = self.val('T')
                    for hh, pO in ((0, pOa), (1, pOb)):
                        self.op('V', 'reciprocal', drow[0:1, :], pO[64:65, :],
                                wait=[('T', tav)] if hh == 0 else None, inc=('V', 1))
                        vr1 = self.vfence()
                        self.op('T', 'matmul', pm[0:64, 0:512], self.ones1b[0:1, 0:64],
                                drow[0:1, :], start=True, stop=True,
                                wait=[('V', vr1)], inc=('T', 1))
                        tb1 = self.val('T')
                        self.op('V', 'tensor_copy', dn[:, :], pm[0:64, 0:512], wait=[('T', tb1)])
                        self.op('V', 'tensor_tensor',
                                Ost[hh * 64:(hh + 1) * 64, p * 512:(p + 1) * 512],
                                pO[0:64, :], dn[:, :], op=ALU.mult, inc=('V', 1))
                    self.wait('T', 'V', self.vfence())
                if self.stage == 'Y' and grp == 0 and qb == NQB - 1:
                    self._stopY = True
                if self.stage == 'X1' and grp == 3 and qb == 0:
                    ostsnap = self.sbuf('dbg_ostsnap', [128, 1024], BF16)
                    self.op('V', 'tensor_copy', ostsnap[:, :], Ost[:, :],
                            wait=[('V', self.val('V'))], inc=('V', 1))
                    self._ostsnap = ostsnap
                if self.stage == 'QOST3' and grp == 3 and qb == 0:
                    dbg = self.g['dbgQ']
                    vv = self.val('V')
                    self.dmaS(dbg[:, 0:1024], Qg[:, :], wait=[('V', vv)])
                    self.dmaS(dbg[:, 1024:2048], Pa[:, :])
                    self.dmaS(dbg[:, 2048:3072], Ost[:, :])
                    return
                if self.stage == 'QOST' and grp == 0 and qb == 0:
                    dbg = self.g['dbgQ']
                    vv = self.val('V')
                    self.dmaS(dbg[:, 0:1024], Qg[:, :], wait=[('V', vv)])
                    self.dmaS(dbg[:, 1024:2048], Pa[:, :])
                    self.dmaS(dbg[:, 2048:3072], Ost[:, :])
                    return
                for oc in range(8):
                    for p in range(2):
                        self.op('T', 'matmul', pb1[:, 0:512],
                                Wp_g[p][:, oc * 128:(oc + 1) * 128],
                                Ost[:, p * 512:(p + 1) * 512], start=(p == 0), stop=(p == 1),
                                inc=(('T', 1) if p == 1 else None))
                    ty = self.val('T')
                    if grp < 3:
                        b = oc % 2
                        wv = [('T', ty)]
                        if ylastlg[b]:
                            wv.append(ylastlg[b])
                        self.op('V', 'tensor_copy', ysb[b][:, :], pb1[:, 0:512],
                                wait=wv, inc=('V', 1))
                        vy = self.vfence()
                        wdma = [('V', vy)]
                        if (qb, oc) in yregion:
                            wdma.append(yregion[(qb, oc)])
                        ylastlg[b] = self.dmaG(
                            self.rr['yacc'][:, oc, qb * 512:(qb + 1) * 512],
                            ysb[b][:, :], wait=wdma, accum=(grp > 0))
                        yregion[(qb, oc)] = ylastlg[b]
                        self.wait('T', 'V', vy)
                    else:
                        b = oc % 2
                        wld = [yregion[(qb, oc)]]
                        if yaclastv[b]:
                            wld.append(('V', yaclastv[b]))
                        lh1 = self.dmaY(yac[b][:, :],
                                        self.rr['yacc'][:, oc, qb * 512:(qb + 1) * 512], wait=wld)
                        lh2 = self.dmaY(xsp[b][:, :],
                                        self.rr['xT'][:, oc, qb * 512:(qb + 1) * 512],
                                        wait=[('V', yaclastv[b])] if yaclastv[b] else None)
                        wv = [('T', ty), lh1, lh2]
                        if x1_store_ls[b]:
                            wv.append(x1_store_ls[b])
                        self.op('V', 'tensor_tensor', x1f[b][:, :], pb1[:, 0:512],
                                yac[b][:, :], op=ALU.add, wait=wv)
                        self.op('V', 'scalar_tensor_tensor', x1f[b][:, :], x1f[b][:, :],
                                self.bp_sb[:, oc:oc + 1], xsp[b][:, :],
                                op0=ALU.add, op1=ALU.add, inc=('V', 1))
                        vx1 = self.vfence()
                        yaclastv[b] = vx1
                        self.wait('T', 'V', vx1)
                        x1_store_ls[b] = self.dmaS(
                            self.rr['R1'][:, oc, qb * 512:(qb + 1) * 512],
                            x1f[b][:, :], wait=[('V', vx1)])
                        wv2 = [('T', x1_sums_t[b])] if x1_sums_t[b] else None
                        self.op('V', 'tensor_copy', x1b[b][:, :], x1f[b][:, :], wait=wv2)
                        self.op('V', 'tensor_tensor', x1s[b][:, :], x1b[b][:, :],
                                x1b[b][:, :], op=ALU.mult, inc=('V', 1))
                        vs1 = self.vfence()
                        self.wait('T', 'V', vs1)
                        for ms in range(4):
                            st = h0_first[0]
                            h0_first[0] = False
                            bend = (qb == NQB - 1 and oc == NCH - 1 and ms == 3)
                            self.op('T', 'matmul', pb0[:, 64 + qb * 4 + ms:65 + qb * 4 + ms],
                                    x1b[b][:, ms * 128:(ms + 1) * 128], self.onesb[:, :],
                                    start=st, stop=False, skip_group_check=True)
                            self.op('T', 'matmul', pb0[:, 96 + qb * 4 + ms:97 + qb * 4 + ms],
                                    x1s[b][:, ms * 128:(ms + 1) * 128], self.onesb[:, :],
                                    start=False, stop=bend, skip_group_check=True,
                                    inc=(('T', 1) if ms == 3 else None))
                        x1_sums_t[b] = self.val('T')
                        if self.stage == 'X1' and grp == 3 and qb == 0 and oc == 1:
                            dbg = self.g['dbgX']
                            ycp = self.sbuf('dbg_ycp', [128, 512], F32)
                            bcp = self.sbuf('dbg_bcp', [128, 8], F32)
                            self.op('V', 'tensor_copy', ycp[:, :], pb1[:, 0:512],
                                    wait=[('V', self.val('V'))])
                            self.op('V', 'tensor_copy', bcp[:, :], self.bp_sb[:, :],
                                    inc=('V', 1))
                            vv = self.val('V')
                            self.dmaS(dbg[:, 0:512], x1f[0][:, :], wait=[('V', vv)])
                            self.dmaS(dbg[:, 512:1024], ycp[:, :])
                            self.dmaS(dbg[:, 1024:1536], yac[0][:, :])
                            self.dmaS(dbg[:, 1536:1544], bcp[:, :])
                            wcp = self.sbuf('dbg_wcp', [128, 512], F32)
                            self.op('V', 'tensor_copy', wcp[:, 0:256], self._ostsnap[:, 0:256])
                            self.op('V', 'tensor_copy', wcp[:, 256:512], Ost[:, 0:256],
                                    inc=('V', 1))
                            self.dmaS(dbg[:, 1544:2048], wcp[:, 0:504], wait=[('V', self.val('V'))])
                            return
        self.canary(15)
        self.h0_rows_done = self._ln_rows('ln0', pb0, 64, 96, arow0, brow0,
                                          ('T', self.val('T')), st=stA)
        self.stageA_end = [('T', self.val('T')), ('V', self.val('V')),
                           ('S', self.val('S'))] + self.h0_rows_done
        if self.stage == 'CAN':
            self.dmaS(self.g['dbgC'], self.cansb[:, :], wait=[('V', self.val('V'))])

    def _ffn_layer(self, l):
        g = self.g
        pb0, pb1, pb3 = self.pb0, self.pb1, self.pb3
        pb5, pb6, pb7 = self.pb5, self.pb6, self.pb7
        if l == 0:
            self.stA.close()
            self.stA = None
            gate = list(self.stageA_end)
            Rin = self.rr['R1']
        else:
            self.stH0.close()
            self.stH0 = None
            gate = list(self.ffn_gate)
            Rin = self.rr['RZ1']
        rin_store = self.allw('LS')
        Rout = self.rr['RA0'] if l == 0 else self.rr['RA1']
        Rz = self.rr['RZ1'] if l == 0 else self.rr['outT']

        stR = contextlib.ExitStack()
        rsb = lambda nm, shp, dt: self.sbuf(f'L{l}{nm}', shp, dt, st=stR)
        a2row = rsb('a2row', [1, N], BF16)
        b2row = rsb('b2row', [1, N], BF16)
        b1t = rsb('b1t', [128, 44], F32)
        b2c = rsb('b2c', [128, 8], F32)
        lgc = rsb('lgc', [128, 8], F32)
        lbb = rsb('lbb', [128, 8], BF16)

        stF = contextlib.ExitStack()
        sb = lambda nm, shp, dt: self.sbuf(f'L{l}{nm}', shp, dt, st=stF)
        wqkv = [sb(f'wqkv{c}', [128, 1536], BF16) for c in range(NCH)]
        wo_sb = [sb(f'wo{p}', [128, 1024], BF16) for p in range(4)]
        PQ = sb('PQ', [128, 128], BF16)
        PK = sb('PK', [128, 128], BF16)
        qs_t = sb('qs', [128, 1], F32)
        ks_t = sb('ks', [128, 1], F32)
        self.dmaG(wqkv[0][:, :], g[f'wqkv{l}'][0:128, :], wait=gate)
        for c in range(1, NCH):
            self.dmaG(wqkv[c][:, :], g[f'wqkv{l}'][c * 128:(c + 1) * 128, :])
        for p in range(4):
            self.dmaG(wo_sb[p][:, :], g[f'wo{l}'][p * 128:(p + 1) * 128, :])
        self.dmaG(PQ[:, :], g[f'PQ{l}'])
        self.dmaG(PK[:, :], g[f'PK{l}'])
        self.dmaG(lbb[:, :], g[f'lbcol{l}'])
        tqs = self.dmaY(qs_t[:, :], g[f'qs_t{l}'], wait=gate)
        tks = self.dmaY(ks_t[:, :], g[f'ks_t{l}'])
        tlg = self.dmaY(lgc[:, :], g[f'lgcol{l}'])
        tb2c = self.dmaY(b2c[:, :], g[f'b2col{l}'])
        wload = self.allw('LG') + self.allw('LH')

        zwin = sb('zwin', [128, 8 * 513], F32)
        zw3 = zwin.rearrange("p (c n) -> p c n", n=513)
        hw = sb('hw', [128, 8 * 512], BF16)
        qraw = sb('qraw', [128, 4 * 512], BF16)
        kraw = sb('kraw', [128, 4 * 512], BF16)
        q2b = sb('q2b', [128, 512], BF16)
        k2b = sb('k2b', [128, 4 * 512], BF16)
        qf = sb('qf', [128, 4 * 512], BF16)
        kf = [sb(f'kf{i}', [128, 4 * 512], BF16) for i in range(2)]
        Vw = [sb(f'Vw{i}', [128, 4 * 8 * 65], BF16) for i in range(2)]
        Vw4 = [t.rearrange("p (m hh e) -> p m hh e", hh=8, e=65) for t in Vw]
        rk8 = [sb(f'rk8{i}', [128, 32], F32) for i in range(2)]
        rqrow = sb('rqrow', [2, 512], BF16)
        tmp1 = sb('tmp1', [128, 512], F32)
        tmp2 = sb('tmp2', [128, 512], F32)
        tmp3 = sb('tmp3', [128, 512], F32)
        tmp1k = sb('tmp1k', [128, 512], F32)
        tmp2k = sb('tmp2k', [128, 512], F32)
        tmp3k = sb('tmp3k', [128, 512], F32)
        TA1 = sb('TA1', [128, 512], F32)
        TA2 = sb('TA2', [128, 512], F32)
        TB1 = sb('TB1', [128, 512], F32)
        TB2 = sb('TB2', [128, 512], F32)
        A0w = sb('A0w', [128, 513], F32)
        B0w = sb('B0w', [128, 513], F32)
        Ph = [sb(f'Ph{i}', [128, 8 * 512], BF16) for i in range(2)]
        dnl = sb('dnl', [64, 512], F32)
        drl = sb('drl', [1, 512], BF16)
        drlf = sb('drlf', [1, 512], F32)
        Ow = sb('Ow', [128, 4 * 512], BF16)
        za = sb('za', [128, 8 * 513], F32)
        za3 = za.rearrange("p (c n) -> p c n", n=513)
        zhalo = sb('zhalo', [128, 8], F32)
        zab = sb('zab', [128, 8 * 513], BF16)
        za2 = sb('za2', [128, 8 * 513], BF16)
        scmb = sb('scmb', [128, 64], F32)

        def chunks(w):
            out = []
            for mc in range(8):
                if w == 0 and mc < 4:
                    continue
                out.append((mc, max(0, mc - 4) * 128, min(3, mc) * 128 + 128))
            return out

        zstore = []
        stats_first = [True]
        tstats_prev = 0
        tav_prev = 0
        vwin_prev = 0
        zast_prev = None
        for w in range(self.nwin):
            cur = w % 2
            prv = 1 - cur
            wl = list(rin_store)
            if vwin_prev:
                wl.append(('V', vwin_prev))
            if w == 0:
                tz = [self.dmaY(zw3[:, c, 1:513], Rin[:, c, 0:512],
                                wait=(wl + wload if c == 0 else wl)) for c in range(NCH)]
                self.op('V', 'memset', zw3[:, :, 0:1], 0.0, inc=('V', 1))
            else:
                tz = [self.dmaY(zw3[:, c, :], Rin[:, c, w * 512 - 1:(w + 1) * 512],
                                wait=wl) for c in range(NCH)]
            wt = [('V', vwin_prev)] if vwin_prev else None
            ttab = [self.dmaY(TA1[:, :], g['TAc'][:, w * 512:(w + 1) * 512], wait=wt),
                    self.dmaY(TA2[:, :], g['TAs'][:, w * 512:(w + 1) * 512], wait=wt),
                    self.dmaY(TB1[:, :], g['TBc'][:, w * 512:(w + 1) * 512], wait=wt),
                    self.dmaY(TB2[:, :], g['TBs'][:, w * 512:(w + 1) * 512], wait=wt)]

            if l == 0:
                vh0 = self.val('V')
                tw = [('V', vh0)]
                if w == 0:
                    self.op('T', 'matmul', pb1[:, 0:512], self.ones1b[:, :],
                            self.arow0[0:1, 0:512], start=True, stop=True,
                            wait=tw + gate)
                    self.op('T', 'matmul', pb3[:, 0:512], self.ones1b[:, :],
                            self.brow0[0:1, 0:512], start=True, stop=True, inc=('T', 1))
                    tb = self.val('T')
                    self.op('V', 'memset', A0w[:, 0:1], 0.0, wait=[('T', tb)])
                    self.op('V', 'memset', B0w[:, 0:1], 0.0)
                    self.op('V', 'tensor_copy', A0w[:, 1:513], pb1[:, 0:512])
                    self.op('V', 'tensor_copy', B0w[:, 1:513], pb3[:, 0:512], inc=('V', 1))
                else:
                    self.op('T', 'matmul', pb1[:, 0:512], self.ones1b[:, :],
                            self.arow0[0:1, w * 512 - 1:(w + 1) * 512 - 1],
                            start=True, stop=True, wait=tw)
                    self.op('T', 'matmul', pb1[:, 512:513], self.ones1b[:, :],
                            self.arow0[0:1, (w + 1) * 512 - 1:(w + 1) * 512],
                            start=True, stop=True)
                    self.op('T', 'matmul', pb3[:, 0:512], self.ones1b[:, :],
                            self.brow0[0:1, w * 512 - 1:(w + 1) * 512 - 1],
                            start=True, stop=True)
                    self.op('T', 'matmul', pb3[:, 512:513], self.ones1b[:, :],
                            self.brow0[0:1, (w + 1) * 512 - 1:(w + 1) * 512],
                            start=True, stop=True, inc=('T', 1))
                    tb = self.val('T')
                    self.op('V', 'tensor_copy', A0w[:, 0:512], pb1[:, 0:512],
                            wait=[('T', tb)])
                    self.op('V', 'tensor_copy', A0w[:, 512:513], pb1[:, 512:513])
                    self.op('V', 'tensor_copy', B0w[:, 0:512], pb3[:, 0:512])
                    self.op('V', 'tensor_copy', B0w[:, 512:513], pb3[:, 512:513],
                            inc=('V', 1))
                for c in range(NCH):
                    off = 0 if c >= 4 else 1
                    wz = tz if c == 0 else None
                    self.op('V', 'tensor_tensor', tmp1[:, :],
                            zw3[:, c, off:off + 512], A0w[:, off:off + 512],
                            op=ALU.mult, wait=wz)
                    self.op('V', 'tensor_tensor', hw[:, c * 512:(c + 1) * 512],
                            tmp1[:, :], B0w[:, off:off + 512], op=ALU.add,
                            inc=(('V', 1) if c == NCH - 1 else None))
            else:
                for c in range(NCH):
                    off = 0 if c >= 4 else 1
                    wz = tz if c == 0 else None
                    self.op('V', 'tensor_copy', hw[:, c * 512:(c + 1) * 512],
                            zw3[:, c, off:off + 512], wait=wz,
                            inc=(('V', 1) if c == NCH - 1 else None))
            vh = self.vfence()
            if self.sub < 2:
                vwin_prev = self.val('V')
                continue

            self.waitl('T', wload + [('V', vh)])
            self.op('V', 'memset', Vw4[cur][:, :, :, 64:65], 1.0, inc=('V', 1))
            vmset = self.vfence()
            bank_free = {}
            for pc in range(4):
                for part, bid, pbank, dst in ((0, 0, pb1, qraw), (1, 1, pb3, kraw)):
                    if bid in bank_free:
                        self.wait('T', 'V', bank_free[bid])
                    for c in range(NCH):
                        self.op('T', 'matmul', pbank[:, 0:512],
                                wqkv[c][:, part * 512 + pc * 128:part * 512 + (pc + 1) * 128],
                                hw[:, c * 512:(c + 1) * 512],
                                start=(c == 0), stop=(c == NCH - 1),
                                inc=(('T', 1) if c == NCH - 1 else None))
                    tqk = self.val('T')
                    self.op('V', 'tensor_copy', dst[:, pc * 512:(pc + 1) * 512],
                            pbank[:, 0:512], wait=[('T', tqk)], inc=('V', 1))
                    bank_free[bid] = self.vfence()
            if self.sub == 21:
                vwin_prev = self.val('V')
                continue
            self.wait('T', 'V', vmset)
            for ms in range(4):
                bid = 2 + ms % 2
                tgt = pb1[:, 512:1024] if ms % 2 == 0 else pb3[:, 512:1024]
                if bid in bank_free:
                    self.wait('T', 'V', bank_free[bid])
                for c in range(NCH):
                    self.op('T', 'matmul', tgt,
                            hw[:, c * 512 + ms * 128:c * 512 + (ms + 1) * 128],
                            wqkv[c][:, 1024:1536],
                            start=(c == 0), stop=(c == NCH - 1),
                            inc=(('T', 1) if c == NCH - 1 else None))
                tv = self.val('T')
                self.op('V', 'tensor_copy', Vw4[cur][:, ms, :, 0:64],
                        tgt.rearrange("p (hh e) -> p hh e", e=64),
                        wait=[('T', tv)], inc=('V', 1))
                bank_free[bid] = self.vfence()
            self.waitl('T', [('V', v) for v in bank_free.values()])

            if self.sub < 3:
                vwin_prev = self.val('V')
                continue
            KNORMS_ONLY = (self.sub == 31)
            for pc in range(4):
                self.op('V', 'tensor_tensor', k2b[:, pc * 512:(pc + 1) * 512],
                        kraw[:, pc * 512:(pc + 1) * 512],
                        kraw[:, pc * 512:(pc + 1) * 512], op=ALU.mult,
                        inc=(('V', 1) if pc == 3 else None))
            self.wait('T', 'V', self.vfence())
            for pc in range(4):
                for ms in range(4):
                    cc = (pc * 4 + ms) * 2
                    self.op('T', 'matmul', pb7[:, cc:cc + 2],
                            k2b[:, pc * 512 + ms * 128:pc * 512 + (ms + 1) * 128],
                            self.bd[:, :], start=True, stop=True,
                            skip_group_check=True,
                            inc=(('T', 1) if pc == 3 and ms == 3 else None))
            tkn = self.val('T')
            self.op('S', 'activation', rk8[cur][:, :], pb7[:, 0:32], AF.Ln,
                    bias=self.cns[:, 1:2], wait=[('T', tkn)])
            self.op('S', 'activation', rk8[cur][:, :], rk8[cur][:, :], AF.Exp,
                    scale=-0.5, bias=self.cns[:, 2:3], inc=('S', 1))
            self.wait('T', 'S', self.sfence())
            if KNORMS_ONLY:
                vwin_prev = self.val('V')
                continue

            rotq_t = [pb7[:, 0:512], pb3[:, 0:512]]
            rotk_t = [pb1[:, 0:512], pb3[:, 512:1024]]
            rq_t = [pb5[:, 0:512], pb6[:, 0:512]]
            qn_t = [pb5[0:2, 0:512], pb6[0:2, 0:512]]
            free_rq = {}
            free_rot = {}
            free_rotk = {}
            for pc in range(4):
                b = pc % 2
                wr = [('V', free_rot[b])] if b in free_rot else None
                self.op('T', 'matmul', rotq_t[b], PQ[:, :],
                        qraw[:, pc * 512:(pc + 1) * 512], start=True, stop=True,
                        wait=wr, inc=('T', 1))
                trot = self.val('T')
                self.op('V', 'tensor_tensor', q2b[:, :], qraw[:, pc * 512:(pc + 1) * 512],
                        qraw[:, pc * 512:(pc + 1) * 512], op=ALU.mult, inc=('V', 1))
                vq2 = self.vfence()
                wq = [('V', vq2)]
                if b in free_rq:
                    wq.append(('V', free_rq[b]))
                self.op('T', 'matmul', qn_t[b], self.bd[:, :], q2b[:, :],
                        start=True, stop=True, wait=wq, inc=('T', 1))
                tqn = self.val('T')
                self.op('S', 'activation', rqrow[:, :], qn_t[b], AF.Ln,
                        bias=self.cns[0:2, 1:2], wait=[('T', tqn)])
                self.op('S', 'activation', rqrow[:, :], rqrow[:, :], AF.Exp,
                        scale=-0.5, inc=('S', 1))
                sqn = self.sfence()
                self.op('T', 'matmul', rq_t[b], self.E2[:, :], rqrow[:, :],
                        start=True, stop=True, wait=[('S', sqn)], inc=('T', 1))
                trq = self.val('T')
                self.op('V', 'tensor_scalar', tmp1[:, :], qraw[:, pc * 512:(pc + 1) * 512],
                        qs_t[:, :], None, op0=ALU.mult, wait=[('T', trot), tqs] + ttab)
                self.op('V', 'tensor_tensor', tmp2[:, :], tmp1[:, :], TA1[:, :],
                        op=ALU.mult)
                self.op('V', 'tensor_tensor', tmp3[:, :], rotq_t[b], TA2[:, :],
                        op=ALU.mult)
                self.op('V', 'tensor_tensor', tmp2[:, :], tmp2[:, :], tmp3[:, :],
                        op=ALU.add)
                self.op('V', 'tensor_tensor', qf[:, pc * 512:(pc + 1) * 512],
                        tmp2[:, :], rq_t[b], op=ALU.mult,
                        wait=[('T', trq)], inc=('V', 1))
                free_rot[b] = free_rq[b] = self.vfence()
                wk = [('V', free_rotk[b])] if b in free_rotk else None
                self.op('T', 'matmul', rotk_t[b], PK[:, :],
                        kraw[:, pc * 512:(pc + 1) * 512], start=True, stop=True,
                        wait=wk, inc=('T', 1))
                trotk = self.val('T')
                self.op('V', 'tensor_scalar', tmp1k[:, :], kraw[:, pc * 512:(pc + 1) * 512],
                        ks_t[:, :], None, op0=ALU.mult, wait=[('T', trotk), tks])
                self.op('V', 'tensor_tensor', tmp2k[:, :], tmp1k[:, :], TB1[:, :],
                        op=ALU.mult)
                self.op('V', 'tensor_tensor', tmp3k[:, :], rotk_t[b], TB2[:, :],
                        op=ALU.mult)
                self.op('V', 'tensor_tensor', kf[cur][:, pc * 512:(pc + 1) * 512],
                        tmp2k[:, :], tmp3k[:, :], op=ALU.add, inc=('V', 1))
                free_rotk[b] = self.vfence()
            self.waitl('T', [('V', v) for v in
                             list(free_rot.values()) + list(free_rotk.values())])

            if self.sub < 40:
                vwin_prev = self.val('V')
                continue
            for pc in range(4):
                hA, hB = 2 * pc, 2 * pc + 1
                p0exp = self.val('S')
                first_exp = True
                sexp_l = {}
                for k, (mc, qlo, qhi) in enumerate(chunks(w)):
                    src = kf[prv] if mc < 4 else kf[cur]
                    vsrc = mc % 4
                    SA = pb5[:, :] if k % 2 == 0 else pb1[:, 0:512]
                    SB = pb6[:, :] if k % 2 == 0 else pb1[:, 512:1024]
                    wS = [('S', sexp_l[k - 2])] if k - 2 in sexp_l else [('S', p0exp)]
                    self.op('T', 'matmul', SA[:, qlo:qhi],
                            src[0:64, pc * 512 + vsrc * 128:pc * 512 + (vsrc + 1) * 128],
                            qf[0:64, pc * 512 + qlo:pc * 512 + qhi],
                            start=True, stop=True, tile_position=(0, 0), wait=wS)
                    self.op('T', 'matmul', SB[:, qlo:qhi],
                            src[64:128, pc * 512 + vsrc * 128:pc * 512 + (vsrc + 1) * 128],
                            qf[64:128, pc * 512 + qlo:pc * 512 + qhi],
                            start=True, stop=True, tile_position=(64, 0),
                            inc=('T', 1))
                    tS = self.val('T')
                    mwait = [('T', tS)]
                    if mc >= 4:
                        qs0 = (mc - 4) * 128
                        self.op('V', 'tensor_tensor', SA[:, qs0:qs0 + 128],
                                SA[:, qs0:qs0 + 128], self.M0_sb[:, :], op=ALU.add,
                                wait=mwait)
                        mwait = None
                        self.op('V', 'tensor_tensor', SB[:, qs0:qs0 + 128],
                                SB[:, qs0:qs0 + 128], self.M0_sb[:, :], op=ALU.add,
                                inc=('V', 1))
                    if mc <= 3:
                        qs0 = mc * 128
                        self.op('V', 'tensor_tensor', SA[:, qs0:qs0 + 128],
                                SA[:, qs0:qs0 + 128], self.M512_sb[:, :], op=ALU.add,
                                wait=mwait)
                        self.op('V', 'tensor_tensor', SB[:, qs0:qs0 + 128],
                                SB[:, qs0:qs0 + 128], self.M512_sb[:, :], op=ALU.add,
                                inc=('V', 1))
                    vmask = self.vfence()
                    if self.sub < 43:
                        continue
                    rksrc = rk8[prv] if mc < 4 else rk8[cur]
                    we = [('V', vmask)]
                    if first_exp:
                        we.append(('T', tav_prev))
                        first_exp = False
                    self.op('S', 'activation', Ph[0][:, mc * 512 + qlo:mc * 512 + qhi],
                            SA[:, qlo:qhi], AF.Exp,
                            scale=rksrc[:, (pc * 4 + vsrc) * 2:(pc * 4 + vsrc) * 2 + 1],
                            wait=we)
                    self.op('S', 'activation', Ph[1][:, mc * 512 + qlo:mc * 512 + qhi],
                            SB[:, qlo:qhi], AF.Exp,
                            scale=rksrc[:, (pc * 4 + vsrc) * 2 + 1:(pc * 4 + vsrc) * 2 + 2],
                            inc=('S', 1))
                    sexp_l[k] = self.val('S')
                if self.sub < 46:
                    continue
                self.wait('T', 'S', self.sfence())
                for qsub in range(4):
                    mcs = [mc for (mc, qlo, qhi) in chunks(w) if qlo <= qsub * 128 < qhi]
                    for ki, mc in enumerate(mcs):
                        vv4 = Vw4[prv] if mc < 4 else Vw4[cur]
                        st_ = (qsub == 0 and ki == 0)
                        sp_ = (qsub == 3 and ki == len(mcs) - 1)
                        self.op('T', 'matmul', pb1[0:65, qsub * 128:(qsub + 1) * 128],
                                vv4[:, mc % 4, hA, :],
                                Ph[0][:, mc * 512 + qsub * 128:mc * 512 + (qsub + 1) * 128],
                                start=st_, stop=sp_, skip_group_check=True)
                        self.op('T', 'matmul',
                                pb1[0:65, 512 + qsub * 128:512 + (qsub + 1) * 128],
                                vv4[:, mc % 4, hB, :],
                                Ph[1][:, mc * 512 + qsub * 128:mc * 512 + (qsub + 1) * 128],
                                start=st_, stop=sp_, skip_group_check=True,
                                inc=(('T', 1) if sp_ else None))
                tav = self.val('T')
                tav_prev = tav
                if self.sub < 47:
                    continue
                for hh, base in ((hA, 0), (hB, 512)):
                    self.op('V', 'reciprocal', drl[0:1, :], pb1[64:65, base:base + 512],
                            wait=[('T', tav)] if base == 0 else None, inc=('V', 1))
                    vr1 = self.vfence()
                    if self.sub < 48:
                        continue
                    self.op('T', 'matmul', pb7[0:64, 0:512], self.ones1b[0:1, 0:64],
                            drl[0:1, :], start=True, stop=True,
                            wait=[('V', vr1)], inc=('T', 1))
                    tb1 = self.val('T')
                    self.op('V', 'tensor_copy', dnl[:, :], pb7[0:64, 0:512],
                            wait=[('T', tb1)])
                    if self.sub < 49:
                        continue
                    self.op('V', 'tensor_tensor',
                            Ow[(hh % 2) * 64:(hh % 2) * 64 + 64, pc * 512:(pc + 1) * 512],
                            pb1[0:64, base:base + 512], dnl[:, :], op=ALU.mult,
                            inc=('V', 1))
                self.wait('T', 'V', self.vfence())

            if self.sub < 50:
                vwin_prev = self.val('V')
                continue
            for oc in range(NCH):
                pbank = pb3 if oc % 2 == 0 else pb1
                for pc in range(4):
                    self.op('T', 'matmul', pbank[:, 0:512],
                            wo_sb[pc][:, oc * 128:(oc + 1) * 128],
                            Ow[:, pc * 512:(pc + 1) * 512],
                            start=(pc == 0), stop=(pc == 3),
                            inc=(('T', 1) if pc == 3 else None))
                tyw = self.val('T')
                wv = [('T', tyw)]
                if oc == 0 and zast_prev is not None:
                    wv += zast_prev
                if l == 0:
                    self.op('V', 'tensor_tensor', tmp1[:, :],
                            zw3[:, oc, 1:513], A0w[:, 1:513], op=ALU.mult, wait=wv)
                    self.op('V', 'tensor_tensor', tmp1[:, :],
                            tmp1[:, :], B0w[:, 1:513], op=ALU.add)
                    self.op('V', 'tensor_tensor', za3[:, oc, 1:513],
                            pbank[:, 0:512], tmp1[:, :], op=ALU.add, inc=('V', 1))
                else:
                    self.op('V', 'tensor_tensor', za3[:, oc, 1:513],
                            pbank[:, 0:512], zw3[:, oc, 1:513], op=ALU.add,
                            wait=wv, inc=('V', 1))
                self.wait('T', 'V', self.vfence())
            if w == 0 or self.sub == 61:
                self.op('V', 'memset', za3[:, :, 0:1], 0.0, inc=('V', 1))
            else:
                self.op('V', 'tensor_copy', za3[:, :, 0:1],
                        zhalo.rearrange("p (c n) -> p c n", n=1)[:, :, :], inc=('V', 1))
            if self.sub != 61:
                self.op('V', 'tensor_copy', zhalo.rearrange("p (c n) -> p c n", n=1)[:, :, :],
                        za3[:, :, 512:513], inc=('V', 1))
            else:
                self.op('V', 'nop', inc=('V', 1))
            vza = self.vfence()
            zast_prev = [self.dmaS(Rout[:, c, w * 512:(w + 1) * 512],
                                   za3[:, c, 1:513],
                                   wait=[('V', vza)] if c == 0 else None)
                         for c in range(NCH)]
            zstore += zast_prev
            if self.sub == 62:
                tstats_prev = self.val('T')
                vwin_prev = self.val('V')
                continue
            wzb = [('T', tstats_prev)] if tstats_prev else None
            self.op('V', 'tensor_copy', zab[:, :], za[:, :], wait=wzb)
            self.op('V', 'tensor_tensor', za2[:, :], zab[:, :], zab[:, :],
                    op=ALU.mult, inc=('V', 1))
            vst = self.vfence()
            self.wait('T', 'V', vst)
            for qty, (src3, off, cb) in enumerate(
                    [(zab, 1, 64), (zab, 0, 96), (za2, 1, 128), (za2, 0, 160)]):
                s3 = src3.rearrange("p (c n) -> p c n", n=513)
                crange = range(0, 4) if qty in (0, 2) else range(4, 8)
                for ms in range(4):
                    for ci, c in enumerate(crange):
                        st_ = stats_first[0]
                        stats_first[0] = False
                        bend = (w == 7 and qty == 3 and ms == 3 and ci == 3)
                        lastq = (qty == 3 and ms == 3 and ci == 3)
                        self.op('T', 'matmul',
                                pb0[:, cb + w * 4 + ms:cb + w * 4 + ms + 1],
                                s3[:, c, off + ms * 128:off + (ms + 1) * 128],
                                self.onesb[:, :], start=st_, stop=bend,
                                skip_group_check=True,
                                inc=(('T', 1) if lastq else None))
            tstats_prev = self.val('T')
            vwin_prev = self.val('V')

        if self.sub < 5 or self.nwin < 8:
            stF.close()
            stR.close()
            return
        self.op('V', 'tensor_copy', scmb[:, 0:32], pb0[:, 96:128],
                wait=[('T', tstats_prev)])
        self.op('V', 'tensor_copy', scmb[:, 32:64], pb0[:, 160:192])
        self.op('V', 'tensor_tensor', scmb[:, 0:32], scmb[:, 0:32], pb0[:, 64:96],
                op=ALU.add)
        self.op('V', 'tensor_tensor', scmb[:, 32:64], scmb[:, 32:64], pb0[:, 128:160],
                op=ALU.add, inc=('V', 1))
        rows2 = self._ln_rows(f'ln2_{l}', scmb, 0, 32, a2row, b2row,
                              ('V', self.val('V')), st=stF)
        if self.stage == 'F0q':
            stF.close()
            stR.close()
            return
        stF.close()

        stD = contextlib.ExitStack()
        dsb = lambda nm, shp, dt: self.sbuf(f'L{l}d{nm}', shp, dt, st=stD)
        gate2 = [('T', self.val('T')), ('V', self.val('V')),
                 ('S', self.val('S'))] + rows2 + self.allw('LS')
        w1s = [dsb(f'w1_{c}', [128, 5460], BF16) for c in range(NCH)]
        w2rows = [128] * 21 + [42]
        w2s = [dsb(f'w2_{j}', [w2rows[j], 1024], BF16) for j in range(NUC)]
        tw1 = []
        for c in range(NCH):
            tw1.append(self.dmaG(w1s[c][:, :], g[f'w1_{l}'][c * 128:(c + 1) * 128, :],
                                 wait=gate2 if c == 0 else None))
        for j in range(NUC):
            self.dmaG(w2s[j][:, :], g[f'w2_{l}'][j * 128:j * 128 + w2rows[j], :])
        wload2 = self.allw('LG')
        if self.sub == 74:
            self.waitl('G', wload2)
            stD.close()
            stR.close()
            return
        for c in range(NCH):
            self.op('V', 'tensor_scalar', w1s[c][:, :], w1s[c][:, :],
                    lgc[:, c:c + 1], None, op0=ALU.mult,
                    wait=[tw1[c], tlg], inc=('V', 1))
        vw1 = self.vfence()
        self.wait('T', 'V', vw1)
        if self.sub == 73:
            stD.close()
            stR.close()
            return
        a_cols = [(j * 128, min(2730, (j + 1) * 128)) for j in range(NUC)]
        g_cols = [(2730 + a, 2730 + b) for (a, b) in a_cols]
        for j in range(2 * NUC):
            lo, hi = (a_cols + g_cols)[j]
            m = hi - lo
            for c in range(NCH):
                self.op('T', 'matmul', pb7[0:m, j:j + 1], w1s[c][:, lo:hi],
                        lbb[:, c:c + 1], start=(c == 0), stop=(c == NCH - 1),
                        skip_group_check=True,
                        inc=(('T', 1) if c == NCH - 1 and j == 2 * NUC - 1 else None))
        tlb = self.val('T')
        b1l = self.dmaY(b1t[:, :], g[f'b1col{l}'])
        self.op('V', 'tensor_tensor', b1t[:, :], b1t[:, :], pb7[:, 0:44],
                op=ALU.add, wait=[('T', tlb), b1l], inc=('V', 1))
        self.wait('T', 'V', self.vfence())

        if self.sub == 71:
            stD.close()
            stR.close()
            return
        zsp = [dsb(f'zsp{i}', [128, 8 * 257], F32) for i in range(2)]
        zsp3 = [t.rearrange("p (c n) -> p c n", n=257) for t in zsp]
        x1sp = [dsb(f'x1sp{i}', [128, 8 * 256], F32) for i in range(2)] if l == 1 else None
        h2 = dsb('h2', [128, 8 * 256], BF16)
        A2w = dsb('A2w', [128, 256], F32)
        B2w = dsb('B2w', [128, 256], F32)
        usb = [dsb(f'u{i}', [128, 256], BF16) for i in range(2)]
        gel = dsb('gel', [128, 256], F32)
        gel_b = dsb('gelb', [128, 256], F32)
        zot = [dsb(f'zot{i}', [128, 256], F32) for i in range(2)]
        tmpd = dsb('tmpd', [128, 256], F32)

        NSP = {70: 2, 72: 1}.get(self.sub, 16)
        vsp_done = [0, 0]
        out_prev = {0: None, 1: None}
        store_out = []
        for sp in range(NSP):
            cur = sp % 2
            wl = list(zstore)
            if vsp_done[cur]:
                wl.append(('V', vsp_done[cur]))
            if sp == 0:
                wl += gate2 + wload2
                tz = [self.dmaY(zsp3[cur][:, c, 1:257], Rout[:, c, 0:256],
                                wait=wl if c == 0 else None) for c in range(NCH)]
                self.op('V', 'memset', zsp3[cur][:, :, 0:1], 0.0, wait=tz, inc=('V', 1))
            else:
                tz = [self.dmaY(zsp3[cur][:, c, :],
                                Rout[:, c, sp * 256 - 1:(sp + 1) * 256],
                                wait=wl if c == 0 else None) for c in range(NCH)]
            if l == 1:
                wx = [('V', vsp_done[cur])] if vsp_done[cur] else []
                if sp == 0:
                    wx += gate2
                x1v = x1sp[cur].rearrange("p (c n) -> p c n", n=256)
                tx1 = [self.dmaY(x1v[:, c, :],
                                 self.rr['R1'][:, c, sp * 256:(sp + 1) * 256],
                                 wait=(wx or None) if c == 0 else None)
                       for c in range(NCH)]
            vprev = self.vfence()
            self.op('T', 'matmul', pb7[:, 0:256], self.ones1b[:, :],
                    a2row[0:1, sp * 256:(sp + 1) * 256], start=True, stop=True,
                    wait=[('V', vprev)] + (rows2 if sp == 0 else []))
            self.op('T', 'matmul', pb7[:, 256:512], self.ones1b[:, :],
                    b2row[0:1, sp * 256:(sp + 1) * 256], start=True, stop=True,
                    inc=('T', 1))
            tb2 = self.val('T')
            self.op('V', 'tensor_copy', A2w[:, :], pb7[:, 0:256], wait=[('T', tb2)])
            self.op('V', 'tensor_copy', B2w[:, :], pb7[:, 256:512], inc=('V', 1))
            self.wait('T', 'V', self.vfence())
            for c in range(NCH):
                off = 0 if c >= 4 else 1
                wz = tz if c == 0 else None
                self.op('V', 'tensor_tensor', tmpd[:, :], zsp3[cur][:, c, off:off + 256],
                        A2w[:, :], op=ALU.mult, wait=wz)
                self.op('V', 'tensor_tensor', h2[:, c * 256:(c + 1) * 256],
                        tmpd[:, :], B2w[:, :], op=ALU.add,
                        inc=(('V', 1) if c == NCH - 1 else None))
            vh2 = self.vfence()
            self.wait('T', 'V', vh2)
            gel2 = [gel, gel_b]
            vus = {}
            def emit_ag(j):
                pag = pb5 if j % 2 == 0 else pb6
                alo, ahi = a_cols[j]
                glo, ghi = g_cols[j]
                m = ahi - alo
                for c in range(NCH):
                    self.op('T', 'matmul', pag[0:m, 0:256], w1s[c][:, alo:ahi],
                            h2[:, c * 256:(c + 1) * 256],
                            start=(c == 0), stop=(c == NCH - 1),
                            skip_group_check=True)
                    self.op('T', 'matmul', pag[0:m, 256:512], w1s[c][:, glo:ghi],
                            h2[:, c * 256:(c + 1) * 256],
                            start=False, stop=(c == NCH - 1),
                            skip_group_check=True,
                            inc=(('T', 1) if c == NCH - 1 else None))
                tag = self.val('T')
                self.op('S', 'activation', gel2[j % 2][0:m, :], pag[0:m, 256:512], AF.Gelu,
                        bias=b1t[0:m, 22 + j:23 + j], wait=[('T', tag)], inc=('S', 1))
                sgel = self.sfence()
                self.op('V', 'scalar_tensor_tensor', usb[j % 2][0:m, :],
                        pag[0:m, 0:256], b1t[0:m, j:j + 1], gel2[j % 2][0:m, :],
                        op0=ALU.add, op1=ALU.mult, wait=[('S', sgel)], inc=('V', 1))
                vus[j] = self.vfence()
            def emit_w2(j):
                m = a_cols[j][1] - a_cols[j][0]
                self.wait('T', 'V', vus[j])
                for oc2 in range(NCH):
                    pob = pb1 if oc2 < 4 else pb3
                    colo = (oc2 % 4) * 256
                    self.op('T', 'matmul', pob[:, colo:colo + 256],
                            w2s[j][:, oc2 * 128:(oc2 + 1) * 128], usb[j % 2][0:m, :],
                            start=(j == 0 and oc2 % 2 == 0), stop=(j == NUC - 1),
                            skip_group_check=True,
                            inc=(('T', 1) if oc2 == NCH - 1 else None))
            emit_ag(0)
            for j in range(1, NUC):
                emit_ag(j)
                emit_w2(j - 1)
            emit_w2(NUC - 1)
            tw2d = self.val('T')
            for oc2 in range(NCH):
                pob = pb1 if oc2 < 4 else pb3
                colo = (oc2 % 4) * 256
                zb = zot[oc2 % 2]
                wv = [('T', tw2d)] if oc2 == 0 else []
                if oc2 == 0 and sp == 0:
                    wv.append(tb2c)
                if out_prev.get(oc2 % 2) is not None:
                    wv.append(out_prev[oc2 % 2])
                self.op('V', 'scalar_tensor_tensor', zb[:, :],
                        pob[:, colo:colo + 256], b2c[:, oc2:oc2 + 1],
                        zsp3[cur][:, oc2, 1:257], op0=ALU.add, op1=ALU.add,
                        wait=wv or None)
                if l == 1:
                    self.op('V', 'tensor_tensor', zb[:, :], zb[:, :],
                            x1sp[cur][:, oc2 * 256:(oc2 + 1) * 256], op=ALU.add,
                            wait=tx1 if oc2 == 0 else None)
                vzo = self.vfence()
                tkst = self.dmaS(Rz[:, oc2, sp * 256:(sp + 1) * 256], zb[:, :],
                                 wait=[('V', vzo)])
                out_prev[oc2 % 2] = tkst
                store_out.append(tkst)
            self.wait('T', 'V', self.vfence())
            vsp_done[cur] = self.val('V')
        self.ffn_gate = [('T', self.val('T')), ('V', self.val('V')),
                         ('S', self.val('S'))] + store_out
        stD.close()
        stR.close()

    def _finish(self):
        for att in ('stA', 'stH0'):
            st = getattr(self, att, None)
            if st is not None:
                st.close()
        toks = self.allw('LS') + self.allw('LG') + self.allw('LH')
        self.waitl('G', toks)


_PROG_CACHE = {}


def _build(stage, nwin=8, sub=50):
    mk = MK(stage, nwin, sub)
    nc = mk.build()
    return nc


def kernel(**inputs):
    sh, cores = host_prep(inputs)
    nc = _build(BUILD_STAGE)
    in_maps = []
    for b in range(8):
        m = dict(sh)
        m.update(cores[b])
        in_maps.append(m)
    res = run_bass_kernel_spmd(nc, in_maps, core_ids=list(range(8)))
    outs = []
    key = 'R1' if BUILD_STAGE == 'A' else 'outT'
    for r in res.results:
        outs.append(np.ascontiguousarray(r[key].T))
    return np.stack(outs, axis=0)


# version 10
# speedup vs baseline: 1.3039x; 1.0059x over previous
import contextlib
import numpy as np
import ml_dtypes
import concourse.bass as bass
import concourse.mybir as mybir
from concourse.bass_utils import run_bass_kernel_spmd
from concourse.alu_op_type import AluOpType as ALU

F32 = mybir.dt.float32
BF16 = mybir.dt.bfloat16
AF = mybir.ActivationFunctionType

N = 4096
D = 1024
NCH = 8
NQB = 8
E = 64
NGRP = 4
HID = 2730
NUC = 22
BUILD_STAGE = 'full'


def _rotary_tables():
    half = np.arange(0, 64, 2, dtype=np.float64)
    inv_freq = 1.0 / 10000.0 ** (half / 64.0)
    t = np.arange(N, dtype=np.float64)
    freqs = np.concatenate([t[:, None] * inv_freq[None, :]] * 2, axis=-1)
    base = (half + 0.4 * 64.0) / (1.4 * 64.0)
    power = (t - N // 2) / 256.0
    scale = np.concatenate([base[None, :] ** power[:, None]] * 2, axis=-1)
    cos = np.cos(freqs); sin = np.sin(freqs)
    sgn = np.concatenate([-np.ones(32), np.ones(32)])[None, :]
    def pack(tab):
        return np.ascontiguousarray(np.concatenate([tab.T, tab.T], axis=0), np.float32)
    return (pack(cos * scale), pack(sin * scale * sgn),
            pack(cos / scale), pack(sin / scale * sgn))


def _rot_perm(sv):
    P = np.zeros((128, 128), dtype=np.float32)
    for half in (0, 1):
        for ep in range(64):
            e = (ep + 32) % 64
            P[half * 64 + e, half * 64 + ep] = sv[e]
    return P


def _col_tiles(v, width):
    out = np.zeros((128, width), dtype=np.float32)
    nfull = len(v) // 128
    for c in range(nfull):
        out[:, c] = v[c * 128:(c + 1) * 128]
    rem = len(v) - nfull * 128
    if rem:
        out[:rem, nfull] = v[nfull * 128:]
    return np.ascontiguousarray(out)


def host_prep(ins):
    f = np.float32
    sh = {}
    Wq, Wk, Wv = ins['L0_attn_Wq'], ins['L0_attn_Wk'], ins['L0_attn_Wv']
    sh['WqT'] = np.ascontiguousarray(np.asarray(Wq, f).transpose(1, 0, 2).reshape(D, 1024))
    sh['WkT'] = np.ascontiguousarray(np.asarray(Wk, f).transpose(1, 0, 2).reshape(D, 1024))
    sh['WvT'] = np.ascontiguousarray(np.asarray(Wv, f).transpose(1, 0, 2).reshape(D, 1024))
    sh['Wpp'] = np.ascontiguousarray(
        np.asarray(ins['L0_attn_Wp'], f).reshape(64, 16, 1024).transpose(1, 0, 2).reshape(1024, 1024))
    sh['bp_col'] = np.ascontiguousarray(np.asarray(ins['L0_attn_bp'], f).reshape(8, 128).T)
    for l in range(2):
        sh[f'wqkv{l}'] = np.ascontiguousarray(np.asarray(ins['L0_ffn_wqkv'][l], f))
        sh[f'wo{l}'] = np.ascontiguousarray(np.asarray(ins['L0_ffn_wo'][l], f))
        sh[f'w1_{l}'] = np.ascontiguousarray(np.asarray(ins['L0_ffn_w1'][l], f))
        sh[f'w2_{l}'] = np.ascontiguousarray(np.asarray(ins['L0_ffn_w2'][l], f))
        sh[f'b1col{l}'] = _col_tiles(np.asarray(ins['L0_ffn_b1'][l], f), 44)
        sh[f'b2col{l}'] = np.ascontiguousarray(np.asarray(ins['L0_ffn_b2'][l], f).reshape(8, 128).T)
        sh[f'lgcol{l}'] = np.ascontiguousarray(np.asarray(ins['L0_ffn_ln_g'][l], f).reshape(8, 128).T)
        sh[f'lbcol{l}'] = np.ascontiguousarray(np.asarray(ins['L0_ffn_ln_b'][l], f).reshape(8, 128).T)
        qs = np.asarray(ins['L0_ffn_q_scale'][l], f)
        ks = np.asarray(ins['L0_ffn_k_scale'][l], f)
        sh[f'qs_t{l}'] = np.ascontiguousarray(np.tile(qs, 2)[:, None])
        sh[f'ks_t{l}'] = np.ascontiguousarray(np.tile(ks, 2)[:, None])
        sh[f'PQ{l}'] = _rot_perm(qs)
        sh[f'PK{l}'] = _rot_perm(ks)
    sh['TAc'], sh['TAs'], sh['TBc'], sh['TBs'] = _rotary_tables()
    pm = np.arange(128)[:, None]; pq = np.arange(128)[None, :]
    sh['M0'] = np.ascontiguousarray(np.where(pq >= pm, 0.0, -1e30).astype(f))
    sh['M512'] = np.ascontiguousarray(np.where(pq <= pm, 0.0, -1e30).astype(f))
    sh['identb'] = np.ascontiguousarray(np.eye(128, dtype=ml_dtypes.bfloat16))
    E2 = np.zeros((2, 128), dtype=ml_dtypes.bfloat16)
    E2[0, 0:64] = 1; E2[1, 64:128] = 1
    sh['E2h'] = E2
    x = ins['L0_attn_query']
    cores = [{'xT': np.ascontiguousarray(np.asarray(x[b], f).T)} for b in range(8)]
    return sh, cores


class MK:
    def __init__(self, stage='full', nwin=8, sub=50):
        self.nwin = nwin
        self.sub = sub
        self.stage = stage
        self.nc = bass.Bass("TRN2", target_bir_lowering=False, debug=False)
        self.es = contextlib.ExitStack()
        self.q = {'T': [], 'V': [], 'S': [], 'G': [], 'Y': []}
        self.cnt = {}

    def sbuf(self, name, shape, dtype, st=None):
        return (st or self.es).enter_context(self.nc.sbuf_tensor(name, shape, dtype))

    def psum(self, name, shape, dtype=F32):
        return self.es.enter_context(self.nc.psum_tensor(name, shape, dtype))

    def sem(self, name):
        h = self.es.enter_context(self.nc.semaphore(name))
        self.cnt[name] = 0
        self.semh = getattr(self, 'semh', {})
        self.semh[name] = h
        return h

    def emit(self, eng, fn):
        self.q[eng].append(fn)

    def bump(self, sem, k=1):
        self.cnt[sem] += k
        return self.cnt[sem]

    def val(self, sem):
        return self.cnt[sem]

    def op(self, eng, meth, *a, inc=None, wait=None, **kw):
        if inc is not None:
            incv = [(s, k) for (s, k) in ([inc] if isinstance(inc, tuple) else inc) if s]
        else:
            incv = []
        wait = list(wait) if wait else []
        if eng in ('V', 'S') and meth != 'dma_start':
            wait.append((eng, self.cnt[eng]))
            incv.append((eng, 1))
        merged = {}
        for sm, k in incv:
            merged[sm] = merged.get(sm, 0) + k
        incv = list(merged.items())
        for sm, k in incv:
            self.bump(sm, k)
        semh = self.semh
        def fn(e):
            for sm, v in wait:
                e.wait_ge(semh[sm], v)
            instr = getattr(e, meth)(*a, **kw)
            for sm, k in incv:
                instr.then_inc(semh[sm], k)
        self.emit(eng, fn)

    def wait(self, eng, sem, v):
        semh = self.semh
        self.emit(eng, lambda e: e.wait_ge(semh[sem], v))

    def vfence(self):
        self.op('V', 'tensor_copy', self.fsc[0:1, 1:2], self.fsc[0:1, 0:1])
        return self.cnt['V']

    def sfence(self):
        self.op('S', 'copy', self.fsc[0:1, 3:4], self.fsc[0:1, 2:3])
        return self.cnt['S']

    def waitl(self, eng, tokens):
        semh = self.semh
        toks = list(tokens)
        def fn(e):
            for sm, v in toks:
                e.wait_ge(semh[sm], v)
        self.emit(eng, fn)

    NLANES = 4

    def _dma(self, eng, queue, dst, src, wait=None, accum=False):
        st = self.lanes[queue]
        lane = st['rr']
        st['rr'] = (lane + 1) % self.NLANES
        name = f'{queue}{lane}'
        prev = self.cnt[name]
        wait = list(wait) if wait else []
        if prev > 0:
            wait.append((name, prev))
        kw = {'out': dst, 'in_': src}
        if accum:
            kw['accum_op'] = ALU.add
        self.op(eng, 'dma_start', inc=(name, 16), wait=wait, **kw)
        return (name, self.cnt[name])

    def dmaY(self, dst, src, wait=None):
        return self._dma('Y', 'LH', dst, src, wait)

    def dmaS(self, dst, src, wait=None):
        return self._dma('S', 'LS', dst, src, wait)

    def dmaG(self, dst, src, wait=None, accum=False):
        return self._dma('G', 'LG', dst, src, wait, accum)

    def allw(self, queue):
        return [(f'{queue}{i}', self.cnt[f'{queue}{i}'])
                for i in range(self.NLANES) if self.cnt[f'{queue}{i}'] > 0]

    def build(self):
        nc = self.nc
        with self.es:
            self._declare_dram()
            self._globals()
            blk = self.es.enter_context(nc.Block())
            self._stage_a()
            if self.stage in ('F0', 'F0q'):
                self._ffn_layer(0)
            elif self.stage == 'full':
                self._ffn_layer(0)
                self._ffn_layer(1)
            self._finish()
            with nc.allow_low_precision(reason="bf16 matmul operands by design"):
                for tag, attr in [('T', 'tensor'), ('V', 'vector'), ('S', 'scalar'),
                                  ('G', 'gpsimd'), ('Y', 'sync')]:
                    fns = self.q[tag]
                    def body(eng, fns=fns):
                        for fn in fns:
                            fn(eng)
                    getattr(blk, attr)(body)
        return nc

    def _declare_dram(self):
        nc = self.nc
        g = self.g = {}
        def d(name, shape, dtype=F32, kind='ExternalInput'):
            g[name] = nc.dram_tensor(name, shape, dtype, kind=kind).ap()
        d('xT', [D, N])
        for k, shp in [('WqT', [D, 1024]), ('WkT', [D, 1024]), ('WvT', [D, 1024]),
                       ('Wpp', [1024, 1024]), ('bp_col', [128, 8]),
                       ('TAc', [128, N]), ('TAs', [128, N]), ('TBc', [128, N]),
                       ('TBs', [128, N]), ('M0', [128, 128]), ('M512', [128, 128])]:
            d(k, shp)
        d('identb', [128, 128], BF16)
        d('E2h', [2, 128], BF16)
        for l in range(2):
            for k, shp in [(f'wqkv{l}', [D, 1536]), (f'wo{l}', [512, 1024]),
                           (f'w1_{l}', [D, 5460]), (f'w2_{l}', [HID, 1024]),
                           (f'b1col{l}', [128, 44]), (f'b2col{l}', [128, 8]),
                           (f'lgcol{l}', [128, 8]), (f'lbcol{l}', [128, 8]),
                           (f'qs_t{l}', [128, 1]), (f'ks_t{l}', [128, 1]),
                           (f'PQ{l}', [128, 128]), (f'PK{l}', [128, 128])]:
                d(k, shp)
        d('yacc', [D, N], kind='ExternalOutput' if self.stage in ('Y', 'Y3') else 'Internal')
        d('rowsc', [2, 4096], kind='Internal')
        d('R1', [D, N], kind='ExternalOutput' if self.stage == 'A' else 'Internal')
        if self.stage == 'A0':
            d('dbgA0', [128, 64 + N], F32, 'ExternalOutput')
        if self.stage == 'A1':
            d('dbgA1', [128, 2 * N], BF16, 'ExternalOutput')
        if self.stage == 'KV':
            d('dbgKV', [128, 2 * N + 32 * 4 * 65], BF16, 'ExternalOutput')
            d('dbgW', [128, 8 * 256 * 2], BF16, 'ExternalOutput')
        if self.stage == 'X1':
            d('dbgX', [128, 4 * 512], F32, 'ExternalOutput')
        if self.stage == 'CAN':
            d('dbgC', [128, 32 * 16], F32, 'ExternalOutput')
        if self.stage in ('QOST', 'QOST3'):
            d('dbgQ', [128, 2 * 512 + 1024 + 2 * 1024], BF16, 'ExternalOutput')
        d('RA0', [D, N], kind='ExternalOutput' if self.stage == 'F0q' else 'Internal')
        d('RZ1', [D, N], kind='ExternalOutput' if self.stage == 'F0' else 'Internal')
        d('RA1', [D, N], kind='Internal')
        if self.stage != 'A':
            d('outT', [D, N], F32, 'ExternalOutput')
        self.rr = {k: g[k].rearrange("(c p) n -> p c n", p=128)
                   for k in ['xT', 'yacc', 'R1', 'RA0', 'RZ1', 'RA1']
                   + (['outT'] if self.stage != 'A' else [])}

    def _globals(self):
        for s in ['T', 'V', 'S']:
            self.sem(s)
        self.lanes = {}
        for q in ['LH', 'LS', 'LG']:
            self.lanes[q] = {'rr': 0}
            for i in range(self.NLANES):
                self.sem(f'{q}{i}')
        self.onesb = self.sbuf('onesb', [128, 1], BF16)
        self.ones1b = self.sbuf('ones1b', [1, 128], BF16)
        self.bd = self.sbuf('bd', [128, 2], BF16)
        self.E2 = self.sbuf('E2', [2, 128], BF16)
        self.identb_sb = self.sbuf('identb_sb', [128, 128], BF16)
        self.bp_sb = self.sbuf('bp_sb', [128, 8], F32)
        self.M0_sb = self.sbuf('M0_sb', [128, 128], F32)
        self.M512_sb = self.sbuf('M512_sb', [128, 128], F32)
        self.cns = self.sbuf('cns', [128, 4], F32)
        self.fsc = self.sbuf('fsc', [1, 4], F32)
        if self.stage == 'CAN':
            self.cansb = self.sbuf('cansb', [128, 32 * 16], F32)

    def canary(self, idx):
        if self.stage != 'CAN' or idx >= 16:
            return
        if not hasattr(self, '_can_init'):
            self.op('V', 'memset', self.cansb[:, :], -7.0)
            self._can_init = True
        c = self.cansb[:, idx * 32:idx * 32 + 32]
        self.op('V', 'tensor_copy', c[:, 0:8], self.bp_sb[:, :], wait=[('V', self.val('V'))])
        self.op('V', 'tensor_copy', c[:, 8:9], self.onesb[:, :])
        self.op('V', 'tensor_copy', c[:, 9:10], self.ones1b[0:1, :].rearrange("o (a b) -> o a b", b=1)[:, 0, :] if False else self.cns[:, 0:1])
        self.op('V', 'tensor_copy', c[:, 10:26], self.identb_sb[:, 0:16], inc=('V', 1))

    def _load_consts(self):
        g = self.g
        for dst, src in [(self.bp_sb, g['bp_col']), (self.M0_sb, g['M0']),
                         (self.M512_sb, g['M512']), (self.identb_sb, g['identb']),
                         (self.E2, g['E2h'])]:
            self.dmaY(dst[:, :], src)
        w = self.allw('LH')
        for t, v in [(self.onesb, 1.0), (self.ones1b, 1.0)]:
            self.op('V', 'memset', t[:, :], v)
        self.op('V', 'memset', self.bd[:, :], 0.0)
        self.op('V', 'memset', self.bd[0:64, 0:1], 1.0)
        self.op('V', 'memset', self.bd[64:128, 1:2], 1.0)
        self.op('V', 'memset', self.fsc[:, :], 0.0)
        self.op('V', 'memset', self.cns[:, 0:1], 1e-5)
        self.op('V', 'memset', self.cns[:, 1:2], 1e-24)
        self.op('V', 'memset', self.cns[:, 2:3], 2.079441542, inc=('V', 1))
        self.consts_v = self.val('V')
        self.consts_lh = w

    def _ln_rows(self, tag, psums, cS, cS2, arow, brow, wait_t, st=None):
        mc = self.sbuf(f'{tag}_mc', [128, 32], F32, st=st)
        vr = self.sbuf(f'{tag}_vr', [128, 32], F32, st=st)
        rc = self.sbuf(f'{tag}_rc', [128, 32], F32, st=st)
        ab = self.sbuf(f'{tag}_ab', [128, 64], BF16, st=st)
        tr = self.sbuf(f'{tag}_tr', [32, 256], BF16, st=st)
        ptr = self.ptr_bf
        self.op('V', 'tensor_scalar', mc[:, :], psums[:, cS:cS + 32], 1.0 / D, None,
                op0=ALU.mult, wait=[wait_t])
        self.op('V', 'tensor_scalar', vr[:, :], psums[:, cS2:cS2 + 32], 1.0 / D, None, op0=ALU.mult)
        self.op('V', 'tensor_tensor', rc[:, :], mc[:, :], mc[:, :], op=ALU.mult)
        self.op('V', 'tensor_tensor', vr[:, :], vr[:, :], rc[:, :], op=ALU.subtract,
                inc=('V', 1))
        v1 = self.vfence()
        self.op('S', 'activation', vr[:, :], vr[:, :], AF.Ln, bias=self.cns[:, 0:1], wait=[('V', v1)])
        self.op('S', 'activation', rc[:, :], vr[:, :], AF.Exp, scale=-0.5, inc=('S', 1))
        s1 = self.sfence()
        self.op('V', 'tensor_copy', ab[:, 0:32], rc[:, :], wait=[('S', s1)])
        self.op('V', 'scalar_tensor_tensor', ab[:, 32:64], mc[:, :], -1.0, rc[:, :],
                op0=ALU.mult, op1=ALU.mult, inc=('V', 1))
        v2 = self.vfence()
        self.op('T', 'transpose', ptr[0:32, 0:128], ab[:, 0:32], self.identb_sb[:, :],
                wait=[('V', v2)])
        self.op('T', 'transpose', ptr[0:32, 128:256], ab[:, 32:64], self.identb_sb[:, :],
                inc=('T', 1))
        t1 = self.val('T')
        self.op('V', 'tensor_copy', tr[:, 0:128], ptr[0:32, 0:128], wait=[('T', t1)])
        self.op('V', 'tensor_copy', tr[:, 128:256], ptr[0:32, 128:256], inc=('V', 1))
        v3 = self.vfence()
        rs = self.g['rowsc']
        rsa = rs[0:1, :].rearrange("o (r n) -> (o r) n", n=128)
        rsb = rs[1:2, :].rearrange("o (r n) -> (o r) n", n=128)
        ta = self.dmaG(rsa, tr[:, 0:128], wait=[('V', v3)])
        tb = self.dmaG(rsb, tr[:, 128:256])
        t3 = self.dmaG(arow[0:1, :], rs[0:1, :], wait=[ta])
        t4 = self.dmaG(brow[0:1, :], rs[1:2, :], wait=[tb])
        return [t3, t4]

    def _stage_a(self):
        g = self.g
        self._load_consts()
        self.pb0 = self.psum('pb0', [128, 512])
        self.pb1 = self.psum('pb1', [128, 1024])
        self.pb3 = self.psum('pb3', [128, 1024])
        self.pb5 = self.psum('pb5', [128, 512])
        self.pb6 = self.psum('pb6', [128, 512])
        self.pb7 = self.psum('pb7', [128, 512])
        self.ptr_bf = self.pb7.bitcast(BF16)
        pb0, pb1, pb3 = self.pb0, self.pb1, self.pb3
        pOa, pOb, pm = self.pb5, self.pb6, self.pb7

        self.stH0 = contextlib.ExitStack()
        arow0 = self.sbuf('arow0', [1, N], BF16, st=self.stH0)
        brow0 = self.sbuf('brow0', [1, N], BF16, st=self.stH0)
        self.arow0, self.brow0 = arow0, brow0
        stA = self.stA = contextlib.ExitStack()
        h = [self.sbuf(f'h{c}', [128, N], BF16, st=stA) for c in range(NCH)]
        sub = contextlib.ExitStack()
        xspan = [self.sbuf(f'xspan{i}', [128, 8 * 512], F32, st=sub) for i in range(2)]
        xb = [self.sbuf(f'xbspan{i}', [128, 8 * 512], BF16, st=sub) for i in range(2)]
        x2 = [self.sbuf(f'x2span{i}', [128, 8 * 512], BF16, st=sub) for i in range(2)]
        arow = self.sbuf('arowA', [1, N], BF16, st=sub)
        brow = self.sbuf('browA', [1, N], BF16, st=sub)

        vcopy_done = [0] * 8
        tsum_done = [0] * 8
        for s in range(8):
            b = s % 2
            wl = [('V', vcopy_done[s - 2])] if s >= 2 else None
            lh = self.dmaY(xspan[b].rearrange("p (c n) -> p c n", n=512)[:, :, :],
                           self.rr['xT'][:, :, s * 512:(s + 1) * 512], wait=wl)
            wv = ([lh] if s > 0 else self.allw('LH')) + \
                 ([('T', tsum_done[s - 2])] if s >= 2 else [])
            self.op('V', 'tensor_copy', xb[b][:, :], xspan[b][:, :], wait=wv)
            self.op('V', 'tensor_tensor', x2[b][:, :], xb[b][:, :], xb[b][:, :],
                    op=ALU.mult, inc=('V', 1))
            vcopy_done[s] = self.vfence()
            self.wait('T', 'V', vcopy_done[s])
            for c in range(NCH):
                for ms in range(4):
                    st = (s == 0 and c == 0 and ms == 0)
                    last = (c == NCH - 1 and ms == 3)
                    burst_end = (s == 7 and last)
                    self.op('T', 'matmul', pb0[:, s * 4 + ms:s * 4 + ms + 1],
                            xb[b][:, c * 512 + ms * 128: c * 512 + (ms + 1) * 128],
                            self.onesb[:, :], start=st, stop=False, skip_group_check=True)
                    self.op('T', 'matmul', pb0[:, 32 + s * 4 + ms:33 + s * 4 + ms],
                            x2[b][:, c * 512 + ms * 128: c * 512 + (ms + 1) * 128],
                            self.onesb[:, :], start=False, stop=burst_end, skip_group_check=True,
                            inc=(('T', 1) if last else None))
            tsum_done[s] = self.val('T')
        rows_done = self._ln_rows('lnA', pb0, 0, 32, arow, brow, ('T', self.val('T')), st=sub)
        if self.stage == 'A0':
            dbg = self.g['dbgA0']
            sums_sb = self.sbuf('dbg_sums', [128, 64], F32, st=sub)
            self.op('V', 'tensor_copy', sums_sb[:, :], pb0[:, 0:64],
                    wait=[rows_done], inc=('V', 1))
            vv = self.val('V')
            self.dmaS(dbg[0:128, 0:64], sums_sb[:, :], wait=[('V', vv)])
            self.dmaS(dbg[0:1, 64:64 + N // 2].bitcast(BF16), arow[0:1, :])
            self.dmaS(dbg[1:2, 64:64 + N // 2].bitcast(BF16), brow[0:1, :])
            sub.close()
            return

        hbuild_done = [0] * 8
        for s in range(8):
            b = s % 2
            wl = [('V', hbuild_done[s - 2])] if s >= 2 else [('V', vcopy_done[7])]
            lh = self.dmaY(xspan[b].rearrange("p (c n) -> p c n", n=512)[:, :, :],
                           self.rr['xT'][:, :, s * 512:(s + 1) * 512], wait=wl)
            wt = [('V', hbuild_done[s - 1] if s >= 1 else vcopy_done[7])]
            if s == 0:
                wt += rows_done
            self.op('T', 'matmul', pb1[:, 0:512], self.ones1b[:, :],
                    arow[0:1, s * 512:(s + 1) * 512], start=True, stop=True, wait=wt)
            self.op('T', 'matmul', pb1[:, 512:1024], self.ones1b[:, :],
                    brow[0:1, s * 512:(s + 1) * 512], start=True, stop=True, inc=('T', 1))
            tb = self.val('T')
            self.waitl('V', [lh])
            self.wait('V', 'T', tb)
            for c in range(NCH):
                self.op('V', 'tensor_tensor', xb[b][:, 0:512],
                        xspan[b][:, c * 512:(c + 1) * 512], pb1[:, 0:512], op=ALU.mult)
                self.op('V', 'tensor_tensor', h[c][:, s * 512:(s + 1) * 512],
                        xb[b][:, 0:512], pb1[:, 512:1024], op=ALU.add,
                        inc=(('V', 1) if c == NCH - 1 else None))
            hbuild_done[s] = self.vfence()
        h_done = self.vfence()
        self.canary(0)
        h_done = self.val('V')
        if self.stage == 'A1':
            dbg = self.g['dbgA1']
            self.dmaS(dbg[:, 0:N], h[0][:, :], wait=[('V', h_done)])
            self.dmaS(dbg[:, N:2 * N], h[5][:, :])
            sub.close()
            return
        sub.close()

        KT = [self.sbuf(f'KTg{p}', [128, N], BF16, st=stA) for p in range(2)]
        Vg = self.sbuf('Vg', [128, 32 * 4 * 65], BF16, st=stA)
        Vg4 = Vg.rearrange("p (m h e) -> p m h e", h=4, e=65)
        Wq_g = [self.sbuf(f'Wqg{c}', [128, 256], BF16, st=stA) for c in range(NCH)]
        Wk_g = [self.sbuf(f'Wkg{c}', [128, 256], BF16, st=stA) for c in range(NCH)]
        Wv_g = [self.sbuf(f'Wvg{c}', [128, 256], BF16, st=stA) for c in range(NCH)]
        Wp_g = [self.sbuf(f'Wpg{p}', [128, 1024], BF16, st=stA) for p in range(2)]
        Qg = self.sbuf('Qg', [128, 2 * 512], BF16, st=stA)
        Pa = self.sbuf('Pa', [128, 1024], BF16, st=stA)
        Pb = self.sbuf('Pb', [128, 1024], BF16, st=stA)
        dn = self.sbuf('dn', [64, 512], F32, st=stA)
        drow = self.sbuf('drow', [1, 512], BF16, st=stA)
        drowf = self.sbuf('drowf', [1, 512], F32, st=stA)
        Ost = self.sbuf('Ost', [128, 2 * 512], BF16, st=stA)
        ysb = [self.sbuf(f'ysb{i}', [128, 512], F32, st=stA) for i in range(2)]
        yac = [self.sbuf(f'yac{i}', [128, 512], F32, st=stA) for i in range(2)]
        xsp = [self.sbuf(f'xsp{i}', [128, 512], F32, st=stA) for i in range(2)]
        x1f = [self.sbuf(f'x1f{i}', [128, 512], F32, st=stA) for i in range(2)]
        x1b = [self.sbuf(f'x1b{i}', [128, 512], BF16, st=stA) for i in range(2)]
        x1s = [self.sbuf(f'x1s{i}', [128, 512], BF16, st=stA) for i in range(2)]
        ylastlg = [None, None]
        yregion = {}
        yaclastv = [0, 0]
        x1_store_ls = [None, None]
        x1_sums_t = [0, 0]
        h0_first = [True]

        ngrp = {'Y': 1, 'Y3': 3}.get(self.stage, NGRP)
        for grp in range(ngrp):
            for c in range(NCH):
                self.dmaG(Wq_g[c][:, :], g['WqT'][c * 128:(c + 1) * 128, grp * 256:(grp + 1) * 256],
                          wait=([('V', h_done)] if grp == 0 and c == 0 else None))
                self.dmaG(Wk_g[c][:, :], g['WkT'][c * 128:(c + 1) * 128, grp * 256:(grp + 1) * 256])
                self.dmaG(Wv_g[c][:, :], g['WvT'][c * 128:(c + 1) * 128, grp * 256:(grp + 1) * 256])
            for p in range(2):
                self.dmaG(Wp_g[p][:, :],
                          g['Wpp'][grp * 256 + p * 128: grp * 256 + (p + 1) * 128, :])
            lgw = self.allw('LG')

            self.op('V', 'memset', Vg4[:, :, :, 64:65], 1.0, inc=('V', 1))
            vones = self.vfence()
            self.waitl('T', lgw)
            self.wait('T', 'V', max(h_done, vones))
            kvtgt = [pm[:, :], pOa[:, :], pOb[:, :], pb1[:, 0:512]]
            kv_ev = {}
            kv0v = self.vfence()
            kv0s = self.sfence()
            gi = [0]
            def kv_group(mms, evac):
                i = gi[0] % 4
                if gi[0] - 4 in kv_ev:
                    w = [('V', kv_ev[gi[0] - 4])]
                else:
                    w = [('V', kv0v), ('S', kv0s)]
                if w:
                    self.waitl('T', w)
                mms(kvtgt[i])
                tk = self.val('T')
                evac(kvtgt[i], tk)
                kv_ev[gi[0]] = self.vfence()
                gi[0] += 1
            for nc2 in range(NCH):
                for p in range(2):
                    def mms(tgt, nc2=nc2, p=p):
                        for c in range(NCH):
                            self.op('T', 'matmul', tgt, Wk_g[c][:, p * 128:(p + 1) * 128],
                                    h[c][:, nc2 * 512:(nc2 + 1) * 512],
                                    start=(c == 0), stop=(c == NCH - 1),
                                    inc=(('T', 1) if c == NCH - 1 else None))
                    def evac(tgt, tk, nc2=nc2, p=p):
                        self.op('V', 'tensor_copy', KT[p][:, nc2 * 512:(nc2 + 1) * 512],
                                tgt, wait=[('T', tk)], inc=('V', 1))
                    kv_group(mms, evac)
                for ms in range(4):
                    def mms(tgt, nc2=nc2, ms=ms):
                        for c in range(NCH):
                            self.op('T', 'matmul', tgt[0:128, 0:256],
                                    h[c][:, nc2 * 512 + ms * 128: nc2 * 512 + (ms + 1) * 128],
                                    Wv_g[c][:, :], start=(c == 0), stop=(c == NCH - 1),
                                    inc=(('T', 1) if c == NCH - 1 else None))
                    def evac(tgt, tk, nc2=nc2, ms=ms):
                        mcg = nc2 * 4 + ms
                        self.op('V', 'tensor_copy', Vg4[:, mcg, :, 0:64],
                                tgt[0:128, 0:256].rearrange("p (hh e) -> p hh e", e=64),
                                wait=[('T', tk)], inc=('V', 1))
                    kv_group(mms, evac)
            self.wait('T', 'V', kv_ev[gi[0] - 1])

            if self.stage == 'KV' and grp == 0:
                dbg = self.g['dbgKV']
                vlast = self.vfence()
                self.dmaS(dbg[:, 0:N], KT[0][:, :], wait=[('V', vlast)])
                self.dmaS(dbg[:, N:2 * N], KT[1][:, :])
                self.dmaS(dbg[:, 2 * N:], Vg[:, :])
                dW = self.g['dbgW']
                for c in range(NCH):
                    self.dmaS(dW[:, c * 256:(c + 1) * 256], Wk_g[c][:, :])
                    self.dmaS(dW[:, 2048 + c * 256:2048 + (c + 1) * 256], Wv_g[c][:, :])
                return
            self.canary(1 + grp * 3)
            for qb in range(NQB):
                if qb == 1:
                    self.canary(2 + grp * 3)
                for p in range(2):
                    for c in range(NCH):
                        self.op('T', 'matmul', pm[:, :], Wq_g[c][:, p * 128:(p + 1) * 128],
                                h[c][:, qb * 512:(qb + 1) * 512],
                                start=(c == 0), stop=(c == NCH - 1),
                                inc=(('T', 1) if c == NCH - 1 else None))
                    tq = self.val('T')
                    self.op('V', 'tensor_copy', Qg[:, p * 512:(p + 1) * 512], pm[:, :],
                            wait=[('T', tq)], inc=('V', 1))
                    self.wait('T', 'V', self.vfence())
                for p in range(2):
                    sexp = {}
                    tavm = {}
                    p0exp = self.val('S')
                    for mc in range(33):
                        if mc < 32:
                            Sp = pb1 if mc % 2 == 0 else pb3
                            Pj = Pa if mc % 2 == 0 else Pb
                            wS = [('S', sexp[mc - 2])] if mc >= 2 else [('S', p0exp)]
                            self.op('T', 'matmul', Sp[:, 0:512],
                                    KT[p][0:64, mc * 128:(mc + 1) * 128],
                                    Qg[0:64, p * 512:(p + 1) * 512],
                                    start=True, stop=True, tile_position=(0, 0),
                                    wait=wS)
                            self.op('T', 'matmul', Sp[:, 512:1024],
                                    KT[p][64:128, mc * 128:(mc + 1) * 128],
                                    Qg[64:128, p * 512:(p + 1) * 512],
                                    start=True, stop=True, tile_position=(64, 0),
                                    inc=('T', 1))
                            ts = self.val('T')
                            we = [('T', ts)]
                            if mc - 2 in tavm:
                                we.append(('T', tavm[mc - 2]))
                            self.op('S', 'activation', Pj[:, :], Sp[:, :], AF.Exp,
                                    scale=0.125, wait=we)
                            sexp[mc] = self.val('S')
                        if mc >= 1:
                            j = mc - 1
                            Pj = Pa if j % 2 == 0 else Pb
                            self.wait('T', 'S', sexp[j])
                            self.op('T', 'matmul', pOa[0:65, :],
                                    Vg4[:, j, 2 * p, :], Pj[:, 0:512],
                                    start=(j == 0), stop=(j == 31))
                            self.op('T', 'matmul', pOb[0:65, :],
                                    Vg4[:, j, 2 * p + 1, :], Pj[:, 512:1024],
                                    start=(j == 0), stop=(j == 31),
                                    inc=('T', 1))
                            tavm[j] = self.val('T')
                    tav = self.val('T')
                    for hh, pO in ((0, pOa), (1, pOb)):
                        self.op('V', 'reciprocal', drow[0:1, :], pO[64:65, :],
                                wait=[('T', tav)] if hh == 0 else None, inc=('V', 1))
                        vr1 = self.vfence()
                        self.op('T', 'matmul', pm[0:64, 0:512], self.ones1b[0:1, 0:64],
                                drow[0:1, :], start=True, stop=True,
                                wait=[('V', vr1)], inc=('T', 1))
                        tb1 = self.val('T')
                        self.op('V', 'tensor_copy', dn[:, :], pm[0:64, 0:512], wait=[('T', tb1)])
                        self.op('V', 'tensor_tensor',
                                Ost[hh * 64:(hh + 1) * 64, p * 512:(p + 1) * 512],
                                pO[0:64, :], dn[:, :], op=ALU.mult, inc=('V', 1))
                    self.wait('T', 'V', self.vfence())
                if self.stage == 'Y' and grp == 0 and qb == NQB - 1:
                    self._stopY = True
                if self.stage == 'X1' and grp == 3 and qb == 0:
                    ostsnap = self.sbuf('dbg_ostsnap', [128, 1024], BF16)
                    self.op('V', 'tensor_copy', ostsnap[:, :], Ost[:, :],
                            wait=[('V', self.val('V'))], inc=('V', 1))
                    self._ostsnap = ostsnap
                if self.stage == 'QOST3' and grp == 3 and qb == 0:
                    dbg = self.g['dbgQ']
                    vv = self.val('V')
                    self.dmaS(dbg[:, 0:1024], Qg[:, :], wait=[('V', vv)])
                    self.dmaS(dbg[:, 1024:2048], Pa[:, :])
                    self.dmaS(dbg[:, 2048:3072], Ost[:, :])
                    return
                if self.stage == 'QOST' and grp == 0 and qb == 0:
                    dbg = self.g['dbgQ']
                    vv = self.val('V')
                    self.dmaS(dbg[:, 0:1024], Qg[:, :], wait=[('V', vv)])
                    self.dmaS(dbg[:, 1024:2048], Pa[:, :])
                    self.dmaS(dbg[:, 2048:3072], Ost[:, :])
                    return
                for oc in range(8):
                    for p in range(2):
                        self.op('T', 'matmul', pb1[:, 0:512],
                                Wp_g[p][:, oc * 128:(oc + 1) * 128],
                                Ost[:, p * 512:(p + 1) * 512], start=(p == 0), stop=(p == 1),
                                inc=(('T', 1) if p == 1 else None))
                    ty = self.val('T')
                    if grp < 3:
                        b = oc % 2
                        wv = [('T', ty)]
                        if ylastlg[b]:
                            wv.append(ylastlg[b])
                        self.op('V', 'tensor_copy', ysb[b][:, :], pb1[:, 0:512],
                                wait=wv, inc=('V', 1))
                        vy = self.vfence()
                        wdma = [('V', vy)]
                        if (qb, oc) in yregion:
                            wdma.append(yregion[(qb, oc)])
                        ylastlg[b] = self.dmaG(
                            self.rr['yacc'][:, oc, qb * 512:(qb + 1) * 512],
                            ysb[b][:, :], wait=wdma, accum=(grp > 0))
                        yregion[(qb, oc)] = ylastlg[b]
                        self.wait('T', 'V', vy)
                    else:
                        b = oc % 2
                        wld = [yregion[(qb, oc)]]
                        if yaclastv[b]:
                            wld.append(('V', yaclastv[b]))
                        lh1 = self.dmaY(yac[b][:, :],
                                        self.rr['yacc'][:, oc, qb * 512:(qb + 1) * 512], wait=wld)
                        lh2 = self.dmaY(xsp[b][:, :],
                                        self.rr['xT'][:, oc, qb * 512:(qb + 1) * 512],
                                        wait=[('V', yaclastv[b])] if yaclastv[b] else None)
                        wv = [('T', ty), lh1, lh2]
                        if x1_store_ls[b]:
                            wv.append(x1_store_ls[b])
                        self.op('V', 'tensor_tensor', x1f[b][:, :], pb1[:, 0:512],
                                yac[b][:, :], op=ALU.add, wait=wv)
                        self.op('V', 'scalar_tensor_tensor', x1f[b][:, :], x1f[b][:, :],
                                self.bp_sb[:, oc:oc + 1], xsp[b][:, :],
                                op0=ALU.add, op1=ALU.add, inc=('V', 1))
                        vx1 = self.vfence()
                        yaclastv[b] = vx1
                        self.wait('T', 'V', vx1)
                        x1_store_ls[b] = self.dmaS(
                            self.rr['R1'][:, oc, qb * 512:(qb + 1) * 512],
                            x1f[b][:, :], wait=[('V', vx1)])
                        wv2 = [('T', x1_sums_t[b])] if x1_sums_t[b] else None
                        self.op('V', 'tensor_copy', x1b[b][:, :], x1f[b][:, :], wait=wv2)
                        self.op('V', 'tensor_tensor', x1s[b][:, :], x1b[b][:, :],
                                x1b[b][:, :], op=ALU.mult, inc=('V', 1))
                        vs1 = self.vfence()
                        self.wait('T', 'V', vs1)
                        for ms in range(4):
                            st = h0_first[0]
                            h0_first[0] = False
                            bend = (qb == NQB - 1 and oc == NCH - 1 and ms == 3)
                            self.op('T', 'matmul', pb0[:, 64 + qb * 4 + ms:65 + qb * 4 + ms],
                                    x1b[b][:, ms * 128:(ms + 1) * 128], self.onesb[:, :],
                                    start=st, stop=False, skip_group_check=True)
                            self.op('T', 'matmul', pb0[:, 96 + qb * 4 + ms:97 + qb * 4 + ms],
                                    x1s[b][:, ms * 128:(ms + 1) * 128], self.onesb[:, :],
                                    start=False, stop=bend, skip_group_check=True,
                                    inc=(('T', 1) if ms == 3 else None))
                        x1_sums_t[b] = self.val('T')
                        if self.stage == 'X1' and grp == 3 and qb == 0 and oc == 1:
                            dbg = self.g['dbgX']
                            ycp = self.sbuf('dbg_ycp', [128, 512], F32)
                            bcp = self.sbuf('dbg_bcp', [128, 8], F32)
                            self.op('V', 'tensor_copy', ycp[:, :], pb1[:, 0:512],
                                    wait=[('V', self.val('V'))])
                            self.op('V', 'tensor_copy', bcp[:, :], self.bp_sb[:, :],
                                    inc=('V', 1))
                            vv = self.val('V')
                            self.dmaS(dbg[:, 0:512], x1f[0][:, :], wait=[('V', vv)])
                            self.dmaS(dbg[:, 512:1024], ycp[:, :])
                            self.dmaS(dbg[:, 1024:1536], yac[0][:, :])
                            self.dmaS(dbg[:, 1536:1544], bcp[:, :])
                            wcp = self.sbuf('dbg_wcp', [128, 512], F32)
                            self.op('V', 'tensor_copy', wcp[:, 0:256], self._ostsnap[:, 0:256])
                            self.op('V', 'tensor_copy', wcp[:, 256:512], Ost[:, 0:256],
                                    inc=('V', 1))
                            self.dmaS(dbg[:, 1544:2048], wcp[:, 0:504], wait=[('V', self.val('V'))])
                            return
        self.canary(15)
        self.h0_rows_done = self._ln_rows('ln0', pb0, 64, 96, arow0, brow0,
                                          ('T', self.val('T')), st=stA)
        self.stageA_end = [('T', self.val('T')), ('V', self.val('V')),
                           ('S', self.val('S'))] + self.h0_rows_done
        if self.stage == 'CAN':
            self.dmaS(self.g['dbgC'], self.cansb[:, :], wait=[('V', self.val('V'))])

    def _ffn_layer(self, l):
        g = self.g
        pb0, pb1, pb3 = self.pb0, self.pb1, self.pb3
        pb5, pb6, pb7 = self.pb5, self.pb6, self.pb7
        if l == 0:
            self.stA.close()
            self.stA = None
            gate = list(self.stageA_end)
            Rin = self.rr['R1']
        else:
            self.stH0.close()
            self.stH0 = None
            gate = list(self.ffn_gate)
            Rin = self.rr['RZ1']
        rin_store = self.allw('LS')
        Rout = self.rr['RA0'] if l == 0 else self.rr['RA1']
        Rz = self.rr['RZ1'] if l == 0 else self.rr['outT']

        stR = contextlib.ExitStack()
        rsb = lambda nm, shp, dt: self.sbuf(f'L{l}{nm}', shp, dt, st=stR)
        a2row = rsb('a2row', [1, N], BF16)
        b2row = rsb('b2row', [1, N], BF16)
        b1t = rsb('b1t', [128, 44], F32)
        b2c = rsb('b2c', [128, 8], F32)
        lgc = rsb('lgc', [128, 8], F32)
        lbb = rsb('lbb', [128, 8], BF16)

        stF = contextlib.ExitStack()
        sb = lambda nm, shp, dt: self.sbuf(f'L{l}{nm}', shp, dt, st=stF)
        wqkv = [sb(f'wqkv{c}', [128, 1536], BF16) for c in range(NCH)]
        wo_sb = [sb(f'wo{p}', [128, 1024], BF16) for p in range(4)]
        PQ = sb('PQ', [128, 128], BF16)
        PK = sb('PK', [128, 128], BF16)
        qs_t = sb('qs', [128, 1], F32)
        ks_t = sb('ks', [128, 1], F32)
        self.dmaG(wqkv[0][:, :], g[f'wqkv{l}'][0:128, :], wait=gate)
        for c in range(1, NCH):
            self.dmaG(wqkv[c][:, :], g[f'wqkv{l}'][c * 128:(c + 1) * 128, :])
        for p in range(4):
            self.dmaG(wo_sb[p][:, :], g[f'wo{l}'][p * 128:(p + 1) * 128, :])
        self.dmaG(PQ[:, :], g[f'PQ{l}'])
        self.dmaG(PK[:, :], g[f'PK{l}'])
        self.dmaG(lbb[:, :], g[f'lbcol{l}'])
        tqs = self.dmaY(qs_t[:, :], g[f'qs_t{l}'], wait=gate)
        tks = self.dmaY(ks_t[:, :], g[f'ks_t{l}'])
        tlg = self.dmaY(lgc[:, :], g[f'lgcol{l}'])
        tb2c = self.dmaY(b2c[:, :], g[f'b2col{l}'])
        wload = self.allw('LG') + self.allw('LH')

        zwin = sb('zwin', [128, 8 * 513], F32)
        zw3 = zwin.rearrange("p (c n) -> p c n", n=513)
        hw = sb('hw', [128, 8 * 512], BF16)
        qraw = sb('qraw', [128, 4 * 512], BF16)
        kraw = sb('kraw', [128, 4 * 512], BF16)
        q2b = sb('q2b', [128, 512], BF16)
        k2b = sb('k2b', [128, 4 * 512], BF16)
        qf = sb('qf', [128, 4 * 512], BF16)
        kf = [sb(f'kf{i}', [128, 4 * 512], BF16) for i in range(2)]
        Vw = [sb(f'Vw{i}', [128, 4 * 8 * 65], BF16) for i in range(2)]
        Vw4 = [t.rearrange("p (m hh e) -> p m hh e", hh=8, e=65) for t in Vw]
        rk8 = [sb(f'rk8{i}', [128, 32], F32) for i in range(2)]
        rqrow = sb('rqrow', [2, 512], BF16)
        tmp1 = sb('tmp1', [128, 512], F32)
        tmp2 = sb('tmp2', [128, 512], F32)
        tmp3 = sb('tmp3', [128, 512], F32)
        tmp1k = sb('tmp1k', [128, 512], F32)
        tmp2k = sb('tmp2k', [128, 512], F32)
        tmp3k = sb('tmp3k', [128, 512], F32)
        TA1 = sb('TA1', [128, 512], F32)
        TA2 = sb('TA2', [128, 512], F32)
        TB1 = sb('TB1', [128, 512], F32)
        TB2 = sb('TB2', [128, 512], F32)
        A0w = sb('A0w', [128, 513], F32)
        B0w = sb('B0w', [128, 513], F32)
        Ph = [sb(f'Ph{i}', [128, 8 * 512], BF16) for i in range(2)]
        dnl = sb('dnl', [64, 512], F32)
        drl = sb('drl', [1, 512], BF16)
        drlf = sb('drlf', [1, 512], F32)
        Ow = sb('Ow', [128, 4 * 512], BF16)
        za = sb('za', [128, 8 * 513], F32)
        za3 = za.rearrange("p (c n) -> p c n", n=513)
        zhalo = sb('zhalo', [128, 8], F32)
        zab = sb('zab', [128, 8 * 513], BF16)
        za2 = sb('za2', [128, 8 * 513], BF16)
        scmb = sb('scmb', [128, 64], F32)

        def chunks(w):
            out = []
            for mc in range(8):
                if w == 0 and mc < 4:
                    continue
                out.append((mc, max(0, mc - 4) * 128, min(3, mc) * 128 + 128))
            return out

        zstore = []
        stats_first = [True]
        tstats_prev = 0
        tav_prev = 0
        vwin_prev = 0
        zast_prev = None
        for w in range(self.nwin):
            cur = w % 2
            prv = 1 - cur
            wl = list(rin_store)
            if vwin_prev:
                wl.append(('V', vwin_prev))
            if w == 0:
                tz = [self.dmaY(zw3[:, c, 1:513], Rin[:, c, 0:512],
                                wait=(wl + wload if c == 0 else wl)) for c in range(NCH)]
                self.op('V', 'memset', zw3[:, :, 0:1], 0.0, inc=('V', 1))
            else:
                tz = [self.dmaY(zw3[:, c, :], Rin[:, c, w * 512 - 1:(w + 1) * 512],
                                wait=wl) for c in range(NCH)]
            wt = [('V', vwin_prev)] if vwin_prev else None
            ttab = [self.dmaY(TA1[:, :], g['TAc'][:, w * 512:(w + 1) * 512], wait=wt),
                    self.dmaY(TA2[:, :], g['TAs'][:, w * 512:(w + 1) * 512], wait=wt),
                    self.dmaY(TB1[:, :], g['TBc'][:, w * 512:(w + 1) * 512], wait=wt),
                    self.dmaY(TB2[:, :], g['TBs'][:, w * 512:(w + 1) * 512], wait=wt)]

            if l == 0:
                vh0 = self.val('V')
                tw = [('V', vh0)]
                if w == 0:
                    self.op('T', 'matmul', pb1[:, 0:512], self.ones1b[:, :],
                            self.arow0[0:1, 0:512], start=True, stop=True,
                            wait=tw + gate)
                    self.op('T', 'matmul', pb3[:, 0:512], self.ones1b[:, :],
                            self.brow0[0:1, 0:512], start=True, stop=True, inc=('T', 1))
                    tb = self.val('T')
                    self.op('V', 'memset', A0w[:, 0:1], 0.0, wait=[('T', tb)])
                    self.op('V', 'memset', B0w[:, 0:1], 0.0)
                    self.op('V', 'tensor_copy', A0w[:, 1:513], pb1[:, 0:512])
                    self.op('V', 'tensor_copy', B0w[:, 1:513], pb3[:, 0:512], inc=('V', 1))
                else:
                    self.op('T', 'matmul', pb1[:, 0:512], self.ones1b[:, :],
                            self.arow0[0:1, w * 512 - 1:(w + 1) * 512 - 1],
                            start=True, stop=True, wait=tw)
                    self.op('T', 'matmul', pb1[:, 512:513], self.ones1b[:, :],
                            self.arow0[0:1, (w + 1) * 512 - 1:(w + 1) * 512],
                            start=True, stop=True)
                    self.op('T', 'matmul', pb3[:, 0:512], self.ones1b[:, :],
                            self.brow0[0:1, w * 512 - 1:(w + 1) * 512 - 1],
                            start=True, stop=True)
                    self.op('T', 'matmul', pb3[:, 512:513], self.ones1b[:, :],
                            self.brow0[0:1, (w + 1) * 512 - 1:(w + 1) * 512],
                            start=True, stop=True, inc=('T', 1))
                    tb = self.val('T')
                    self.op('V', 'tensor_copy', A0w[:, 0:512], pb1[:, 0:512],
                            wait=[('T', tb)])
                    self.op('V', 'tensor_copy', A0w[:, 512:513], pb1[:, 512:513])
                    self.op('V', 'tensor_copy', B0w[:, 0:512], pb3[:, 0:512])
                    self.op('V', 'tensor_copy', B0w[:, 512:513], pb3[:, 512:513],
                            inc=('V', 1))
                for c in range(NCH):
                    off = 0 if c >= 4 else 1
                    wz = tz if c == 0 else None
                    self.op('V', 'tensor_tensor', tmp1[:, :],
                            zw3[:, c, off:off + 512], A0w[:, off:off + 512],
                            op=ALU.mult, wait=wz)
                    self.op('V', 'tensor_tensor', hw[:, c * 512:(c + 1) * 512],
                            tmp1[:, :], B0w[:, off:off + 512], op=ALU.add,
                            inc=(('V', 1) if c == NCH - 1 else None))
            else:
                for c in range(NCH):
                    off = 0 if c >= 4 else 1
                    wz = tz if c == 0 else None
                    self.op('V', 'tensor_copy', hw[:, c * 512:(c + 1) * 512],
                            zw3[:, c, off:off + 512], wait=wz,
                            inc=(('V', 1) if c == NCH - 1 else None))
            vh = self.vfence()
            if self.sub < 2:
                vwin_prev = self.val('V')
                continue

            self.waitl('T', wload + [('V', vh)])
            self.op('V', 'memset', Vw4[cur][:, :, :, 64:65], 1.0, inc=('V', 1))
            vmset = self.vfence()
            bank_free = {}
            for pc in range(4):
                for part, bid, pbank, dst in ((0, 0, pb1, qraw), (1, 1, pb3, kraw)):
                    if bid in bank_free:
                        self.wait('T', 'V', bank_free[bid])
                    for c in range(NCH):
                        self.op('T', 'matmul', pbank[:, 0:512],
                                wqkv[c][:, part * 512 + pc * 128:part * 512 + (pc + 1) * 128],
                                hw[:, c * 512:(c + 1) * 512],
                                start=(c == 0), stop=(c == NCH - 1),
                                inc=(('T', 1) if c == NCH - 1 else None))
                    tqk = self.val('T')
                    self.op('V', 'tensor_copy', dst[:, pc * 512:(pc + 1) * 512],
                            pbank[:, 0:512], wait=[('T', tqk)], inc=('V', 1))
                    bank_free[bid] = self.vfence()
            if self.sub == 21:
                vwin_prev = self.val('V')
                continue
            self.wait('T', 'V', vmset)
            for ms in range(4):
                bid = 2 + ms % 2
                tgt = pb1[:, 512:1024] if ms % 2 == 0 else pb3[:, 512:1024]
                if bid in bank_free:
                    self.wait('T', 'V', bank_free[bid])
                for c in range(NCH):
                    self.op('T', 'matmul', tgt,
                            hw[:, c * 512 + ms * 128:c * 512 + (ms + 1) * 128],
                            wqkv[c][:, 1024:1536],
                            start=(c == 0), stop=(c == NCH - 1),
                            inc=(('T', 1) if c == NCH - 1 else None))
                tv = self.val('T')
                self.op('V', 'tensor_copy', Vw4[cur][:, ms, :, 0:64],
                        tgt.rearrange("p (hh e) -> p hh e", e=64),
                        wait=[('T', tv)], inc=('V', 1))
                bank_free[bid] = self.vfence()
            self.waitl('T', [('V', v) for v in bank_free.values()])

            if self.sub < 3:
                vwin_prev = self.val('V')
                continue
            KNORMS_ONLY = (self.sub == 31)
            for pc in range(4):
                self.op('V', 'tensor_tensor', k2b[:, pc * 512:(pc + 1) * 512],
                        kraw[:, pc * 512:(pc + 1) * 512],
                        kraw[:, pc * 512:(pc + 1) * 512], op=ALU.mult,
                        inc=(('V', 1) if pc == 3 else None))
            self.wait('T', 'V', self.vfence())
            for pc in range(4):
                for ms in range(4):
                    cc = (pc * 4 + ms) * 2
                    self.op('T', 'matmul', pb7[:, cc:cc + 2],
                            k2b[:, pc * 512 + ms * 128:pc * 512 + (ms + 1) * 128],
                            self.bd[:, :], start=True, stop=True,
                            skip_group_check=True,
                            inc=(('T', 1) if pc == 3 and ms == 3 else None))
            tkn = self.val('T')
            self.op('S', 'activation', rk8[cur][:, :], pb7[:, 0:32], AF.Ln,
                    bias=self.cns[:, 1:2], wait=[('T', tkn)])
            self.op('S', 'activation', rk8[cur][:, :], rk8[cur][:, :], AF.Exp,
                    scale=-0.5, bias=self.cns[:, 2:3], inc=('S', 1))
            self.wait('T', 'S', self.sfence())
            if KNORMS_ONLY:
                vwin_prev = self.val('V')
                continue

            rotq_t = [pb7[:, 0:512], pb3[:, 0:512]]
            rotk_t = [pb1[:, 0:512], pb3[:, 512:1024]]
            rq_t = [pb5[:, 0:512], pb6[:, 0:512]]
            qn_t = [pb5[0:2, 0:512], pb6[0:2, 0:512]]
            free_rq = {}
            free_rot = {}
            free_rotk = {}
            for pc in range(4):
                b = pc % 2
                wr = [('V', free_rot[b])] if b in free_rot else None
                self.op('T', 'matmul', rotq_t[b], PQ[:, :],
                        qraw[:, pc * 512:(pc + 1) * 512], start=True, stop=True,
                        wait=wr, inc=('T', 1))
                trot = self.val('T')
                self.op('V', 'tensor_tensor', q2b[:, :], qraw[:, pc * 512:(pc + 1) * 512],
                        qraw[:, pc * 512:(pc + 1) * 512], op=ALU.mult, inc=('V', 1))
                vq2 = self.vfence()
                wq = [('V', vq2)]
                if b in free_rq:
                    wq.append(('V', free_rq[b]))
                self.op('T', 'matmul', qn_t[b], self.bd[:, :], q2b[:, :],
                        start=True, stop=True, wait=wq, inc=('T', 1))
                tqn = self.val('T')
                self.op('S', 'activation', rqrow[:, :], qn_t[b], AF.Ln,
                        bias=self.cns[0:2, 1:2], wait=[('T', tqn)])
                self.op('S', 'activation', rqrow[:, :], rqrow[:, :], AF.Exp,
                        scale=-0.5, inc=('S', 1))
                sqn = self.sfence()
                self.op('T', 'matmul', rq_t[b], self.E2[:, :], rqrow[:, :],
                        start=True, stop=True, wait=[('S', sqn)], inc=('T', 1))
                trq = self.val('T')
                self.op('V', 'tensor_scalar', tmp1[:, :], qraw[:, pc * 512:(pc + 1) * 512],
                        qs_t[:, :], None, op0=ALU.mult, wait=[('T', trot), tqs] + ttab)
                self.op('V', 'tensor_tensor', tmp2[:, :], tmp1[:, :], TA1[:, :],
                        op=ALU.mult)
                self.op('V', 'tensor_tensor', tmp3[:, :], rotq_t[b], TA2[:, :],
                        op=ALU.mult)
                self.op('V', 'tensor_tensor', tmp2[:, :], tmp2[:, :], tmp3[:, :],
                        op=ALU.add)
                self.op('V', 'tensor_tensor', qf[:, pc * 512:(pc + 1) * 512],
                        tmp2[:, :], rq_t[b], op=ALU.mult,
                        wait=[('T', trq)], inc=('V', 1))
                free_rot[b] = free_rq[b] = self.vfence()
                wk = [('V', free_rotk[b])] if b in free_rotk else None
                self.op('T', 'matmul', rotk_t[b], PK[:, :],
                        kraw[:, pc * 512:(pc + 1) * 512], start=True, stop=True,
                        wait=wk, inc=('T', 1))
                trotk = self.val('T')
                self.op('V', 'tensor_scalar', tmp1k[:, :], kraw[:, pc * 512:(pc + 1) * 512],
                        ks_t[:, :], None, op0=ALU.mult, wait=[('T', trotk), tks])
                self.op('V', 'tensor_tensor', tmp2k[:, :], tmp1k[:, :], TB1[:, :],
                        op=ALU.mult)
                self.op('V', 'tensor_tensor', tmp3k[:, :], rotk_t[b], TB2[:, :],
                        op=ALU.mult)
                self.op('V', 'tensor_tensor', kf[cur][:, pc * 512:(pc + 1) * 512],
                        tmp2k[:, :], tmp3k[:, :], op=ALU.add, inc=('V', 1))
                free_rotk[b] = self.vfence()
            self.waitl('T', [('V', v) for v in
                             list(free_rot.values()) + list(free_rotk.values())])

            if self.sub < 40:
                vwin_prev = self.val('V')
                continue
            for pc in range(4):
                hA, hB = 2 * pc, 2 * pc + 1
                p0exp = self.val('S')
                first_exp = True
                sexp_l = {}
                for k, (mc, qlo, qhi) in enumerate(chunks(w)):
                    src = kf[prv] if mc < 4 else kf[cur]
                    vsrc = mc % 4
                    SA = pb5[:, :] if k % 2 == 0 else pb1[:, 0:512]
                    SB = pb6[:, :] if k % 2 == 0 else pb1[:, 512:1024]
                    wS = [('S', sexp_l[k - 2])] if k - 2 in sexp_l else [('S', p0exp)]
                    self.op('T', 'matmul', SA[:, qlo:qhi],
                            src[0:64, pc * 512 + vsrc * 128:pc * 512 + (vsrc + 1) * 128],
                            qf[0:64, pc * 512 + qlo:pc * 512 + qhi],
                            start=True, stop=True, tile_position=(0, 0), wait=wS)
                    self.op('T', 'matmul', SB[:, qlo:qhi],
                            src[64:128, pc * 512 + vsrc * 128:pc * 512 + (vsrc + 1) * 128],
                            qf[64:128, pc * 512 + qlo:pc * 512 + qhi],
                            start=True, stop=True, tile_position=(64, 0),
                            inc=('T', 1))
                    tS = self.val('T')
                    mwait = [('T', tS)]
                    if mc >= 4:
                        qs0 = (mc - 4) * 128
                        self.op('V', 'tensor_tensor', SA[:, qs0:qs0 + 128],
                                SA[:, qs0:qs0 + 128], self.M0_sb[:, :], op=ALU.add,
                                wait=mwait)
                        mwait = None
                        self.op('V', 'tensor_tensor', SB[:, qs0:qs0 + 128],
                                SB[:, qs0:qs0 + 128], self.M0_sb[:, :], op=ALU.add,
                                inc=('V', 1))
                    if mc <= 3:
                        qs0 = mc * 128
                        self.op('V', 'tensor_tensor', SA[:, qs0:qs0 + 128],
                                SA[:, qs0:qs0 + 128], self.M512_sb[:, :], op=ALU.add,
                                wait=mwait)
                        self.op('V', 'tensor_tensor', SB[:, qs0:qs0 + 128],
                                SB[:, qs0:qs0 + 128], self.M512_sb[:, :], op=ALU.add,
                                inc=('V', 1))
                    vmask = self.vfence()
                    if self.sub < 43:
                        continue
                    rksrc = rk8[prv] if mc < 4 else rk8[cur]
                    we = [('V', vmask)]
                    if first_exp:
                        we.append(('T', tav_prev))
                        first_exp = False
                    self.op('S', 'activation', Ph[0][:, mc * 512 + qlo:mc * 512 + qhi],
                            SA[:, qlo:qhi], AF.Exp,
                            scale=rksrc[:, (pc * 4 + vsrc) * 2:(pc * 4 + vsrc) * 2 + 1],
                            wait=we)
                    self.op('S', 'activation', Ph[1][:, mc * 512 + qlo:mc * 512 + qhi],
                            SB[:, qlo:qhi], AF.Exp,
                            scale=rksrc[:, (pc * 4 + vsrc) * 2 + 1:(pc * 4 + vsrc) * 2 + 2],
                            inc=('S', 1))
                    sexp_l[k] = self.val('S')
                if self.sub < 46:
                    continue
                self.wait('T', 'S', sexp_l[max(sexp_l)])
                for qsub in range(4):
                    mcs = [mc for (mc, qlo, qhi) in chunks(w) if qlo <= qsub * 128 < qhi]
                    for ki, mc in enumerate(mcs):
                        vv4 = Vw4[prv] if mc < 4 else Vw4[cur]
                        st_ = (qsub == 0 and ki == 0)
                        sp_ = (qsub == 3 and ki == len(mcs) - 1)
                        self.op('T', 'matmul', pb1[0:65, qsub * 128:(qsub + 1) * 128],
                                vv4[:, mc % 4, hA, :],
                                Ph[0][:, mc * 512 + qsub * 128:mc * 512 + (qsub + 1) * 128],
                                start=st_, stop=sp_, skip_group_check=True)
                        self.op('T', 'matmul',
                                pb1[0:65, 512 + qsub * 128:512 + (qsub + 1) * 128],
                                vv4[:, mc % 4, hB, :],
                                Ph[1][:, mc * 512 + qsub * 128:mc * 512 + (qsub + 1) * 128],
                                start=st_, stop=sp_, skip_group_check=True,
                                inc=(('T', 1) if sp_ else None))
                tav = self.val('T')
                tav_prev = tav
                if self.sub < 47:
                    continue
                for hh, base in ((hA, 0), (hB, 512)):
                    self.op('V', 'reciprocal', drl[0:1, :], pb1[64:65, base:base + 512],
                            wait=[('T', tav)] if base == 0 else None, inc=('V', 1))
                    vr1 = self.vfence()
                    if self.sub < 48:
                        continue
                    self.op('T', 'matmul', pb7[0:64, 0:512], self.ones1b[0:1, 0:64],
                            drl[0:1, :], start=True, stop=True,
                            wait=[('V', vr1)], inc=('T', 1))
                    tb1 = self.val('T')
                    self.op('V', 'tensor_copy', dnl[:, :], pb7[0:64, 0:512],
                            wait=[('T', tb1)])
                    if self.sub < 49:
                        continue
                    self.op('V', 'tensor_tensor',
                            Ow[(hh % 2) * 64:(hh % 2) * 64 + 64, pc * 512:(pc + 1) * 512],
                            pb1[0:64, base:base + 512], dnl[:, :], op=ALU.mult,
                            inc=('V', 1))
                self.wait('T', 'V', self.vfence())

            if self.sub < 50:
                vwin_prev = self.val('V')
                continue
            for oc in range(NCH):
                pbank = pb3 if oc % 2 == 0 else pb1
                for pc in range(4):
                    self.op('T', 'matmul', pbank[:, 0:512],
                            wo_sb[pc][:, oc * 128:(oc + 1) * 128],
                            Ow[:, pc * 512:(pc + 1) * 512],
                            start=(pc == 0), stop=(pc == 3),
                            inc=(('T', 1) if pc == 3 else None))
                tyw = self.val('T')
                wv = [('T', tyw)]
                if oc == 0 and zast_prev is not None:
                    wv += zast_prev
                if l == 0:
                    self.op('V', 'tensor_tensor', tmp1[:, :],
                            zw3[:, oc, 1:513], A0w[:, 1:513], op=ALU.mult, wait=wv)
                    self.op('V', 'tensor_tensor', tmp1[:, :],
                            tmp1[:, :], B0w[:, 1:513], op=ALU.add)
                    self.op('V', 'tensor_tensor', za3[:, oc, 1:513],
                            pbank[:, 0:512], tmp1[:, :], op=ALU.add, inc=('V', 1))
                else:
                    self.op('V', 'tensor_tensor', za3[:, oc, 1:513],
                            pbank[:, 0:512], zw3[:, oc, 1:513], op=ALU.add,
                            wait=wv, inc=('V', 1))
                self.wait('T', 'V', self.vfence())
            if w == 0 or self.sub == 61:
                self.op('V', 'memset', za3[:, :, 0:1], 0.0, inc=('V', 1))
            else:
                self.op('V', 'tensor_copy', za3[:, :, 0:1],
                        zhalo.rearrange("p (c n) -> p c n", n=1)[:, :, :], inc=('V', 1))
            if self.sub != 61:
                self.op('V', 'tensor_copy', zhalo.rearrange("p (c n) -> p c n", n=1)[:, :, :],
                        za3[:, :, 512:513], inc=('V', 1))
            else:
                self.op('V', 'nop', inc=('V', 1))
            vza = self.vfence()
            zast_prev = [self.dmaS(Rout[:, c, w * 512:(w + 1) * 512],
                                   za3[:, c, 1:513],
                                   wait=[('V', vza)] if c == 0 else None)
                         for c in range(NCH)]
            zstore += zast_prev
            if self.sub == 62:
                tstats_prev = self.val('T')
                vwin_prev = self.val('V')
                continue
            wzb = [('T', tstats_prev)] if tstats_prev else None
            self.op('V', 'tensor_copy', zab[:, :], za[:, :], wait=wzb)
            self.op('V', 'tensor_tensor', za2[:, :], zab[:, :], zab[:, :],
                    op=ALU.mult, inc=('V', 1))
            vst = self.vfence()
            self.wait('T', 'V', vst)
            for qty, (src3, off, cb) in enumerate(
                    [(zab, 1, 64), (zab, 0, 96), (za2, 1, 128), (za2, 0, 160)]):
                s3 = src3.rearrange("p (c n) -> p c n", n=513)
                crange = range(0, 4) if qty in (0, 2) else range(4, 8)
                for ms in range(4):
                    for ci, c in enumerate(crange):
                        st_ = stats_first[0]
                        stats_first[0] = False
                        bend = (w == 7 and qty == 3 and ms == 3 and ci == 3)
                        lastq = (qty == 3 and ms == 3 and ci == 3)
                        self.op('T', 'matmul',
                                pb0[:, cb + w * 4 + ms:cb + w * 4 + ms + 1],
                                s3[:, c, off + ms * 128:off + (ms + 1) * 128],
                                self.onesb[:, :], start=st_, stop=bend,
                                skip_group_check=True,
                                inc=(('T', 1) if lastq else None))
            tstats_prev = self.val('T')
            vwin_prev = self.val('V')

        if self.sub < 5 or self.nwin < 8:
            stF.close()
            stR.close()
            return
        self.op('V', 'tensor_copy', scmb[:, 0:32], pb0[:, 96:128],
                wait=[('T', tstats_prev)])
        self.op('V', 'tensor_copy', scmb[:, 32:64], pb0[:, 160:192])
        self.op('V', 'tensor_tensor', scmb[:, 0:32], scmb[:, 0:32], pb0[:, 64:96],
                op=ALU.add)
        self.op('V', 'tensor_tensor', scmb[:, 32:64], scmb[:, 32:64], pb0[:, 128:160],
                op=ALU.add, inc=('V', 1))
        rows2 = self._ln_rows(f'ln2_{l}', scmb, 0, 32, a2row, b2row,
                              ('V', self.val('V')), st=stF)
        if self.stage == 'F0q':
            stF.close()
            stR.close()
            return
        stF.close()

        stD = contextlib.ExitStack()
        dsb = lambda nm, shp, dt: self.sbuf(f'L{l}d{nm}', shp, dt, st=stD)
        gate2 = [('T', self.val('T')), ('V', self.val('V')),
                 ('S', self.val('S'))] + rows2 + self.allw('LS')
        w1s = [dsb(f'w1_{c}', [128, 5460], BF16) for c in range(NCH)]
        w2rows = [128] * 21 + [42]
        w2s = [dsb(f'w2_{j}', [w2rows[j], 1024], BF16) for j in range(NUC)]
        tw1 = []
        for c in range(NCH):
            tw1.append(self.dmaG(w1s[c][:, :], g[f'w1_{l}'][c * 128:(c + 1) * 128, :],
                                 wait=gate2 if c == 0 else None))
        for j in range(NUC):
            self.dmaG(w2s[j][:, :], g[f'w2_{l}'][j * 128:j * 128 + w2rows[j], :])
        wload2 = self.allw('LG')
        if self.sub == 74:
            self.waitl('G', wload2)
            stD.close()
            stR.close()
            return
        for c in range(NCH):
            self.op('V', 'tensor_scalar', w1s[c][:, :], w1s[c][:, :],
                    lgc[:, c:c + 1], None, op0=ALU.mult,
                    wait=[tw1[c], tlg], inc=('V', 1))
        vw1 = self.vfence()
        self.wait('T', 'V', vw1)
        if self.sub == 73:
            stD.close()
            stR.close()
            return
        a_cols = [(j * 128, min(2730, (j + 1) * 128)) for j in range(NUC)]
        g_cols = [(2730 + a, 2730 + b) for (a, b) in a_cols]
        for j in range(2 * NUC):
            lo, hi = (a_cols + g_cols)[j]
            m = hi - lo
            for c in range(NCH):
                self.op('T', 'matmul', pb7[0:m, j:j + 1], w1s[c][:, lo:hi],
                        lbb[:, c:c + 1], start=(c == 0), stop=(c == NCH - 1),
                        skip_group_check=True,
                        inc=(('T', 1) if c == NCH - 1 and j == 2 * NUC - 1 else None))
        tlb = self.val('T')
        b1l = self.dmaY(b1t[:, :], g[f'b1col{l}'])
        self.op('V', 'tensor_tensor', b1t[:, :], b1t[:, :], pb7[:, 0:44],
                op=ALU.add, wait=[('T', tlb), b1l], inc=('V', 1))
        self.wait('T', 'V', self.vfence())

        if self.sub == 71:
            stD.close()
            stR.close()
            return
        zsp = [dsb(f'zsp{i}', [128, 8 * 257], F32) for i in range(2)]
        zsp3 = [t.rearrange("p (c n) -> p c n", n=257) for t in zsp]
        x1sp = [dsb(f'x1sp{i}', [128, 8 * 256], F32) for i in range(2)] if l == 1 else None
        h2 = dsb('h2', [128, 8 * 256], BF16)
        A2w = dsb('A2w', [128, 256], F32)
        B2w = dsb('B2w', [128, 256], F32)
        usb = [dsb(f'u{i}', [128, 256], BF16) for i in range(2)]
        gel = dsb('gel', [128, 256], F32)
        gel_b = dsb('gelb', [128, 256], F32)
        zot = [dsb(f'zot{i}', [128, 256], F32) for i in range(2)]
        tmpd = dsb('tmpd', [128, 256], F32)

        NSP = {70: 2, 72: 1}.get(self.sub, 16)
        vsp_done = [0, 0]
        out_prev = {0: None, 1: None}
        store_out = []
        for sp in range(NSP):
            cur = sp % 2
            wl = list(zstore)
            if vsp_done[cur]:
                wl.append(('V', vsp_done[cur]))
            if sp == 0:
                wl += gate2 + wload2
                tz = [self.dmaY(zsp3[cur][:, c, 1:257], Rout[:, c, 0:256],
                                wait=wl if c == 0 else None) for c in range(NCH)]
                self.op('V', 'memset', zsp3[cur][:, :, 0:1], 0.0, wait=tz, inc=('V', 1))
            else:
                tz = [self.dmaY(zsp3[cur][:, c, :],
                                Rout[:, c, sp * 256 - 1:(sp + 1) * 256],
                                wait=wl if c == 0 else None) for c in range(NCH)]
            if l == 1:
                wx = [('V', vsp_done[cur])] if vsp_done[cur] else []
                if sp == 0:
                    wx += gate2
                x1v = x1sp[cur].rearrange("p (c n) -> p c n", n=256)
                tx1 = [self.dmaY(x1v[:, c, :],
                                 self.rr['R1'][:, c, sp * 256:(sp + 1) * 256],
                                 wait=(wx or None) if c == 0 else None)
                       for c in range(NCH)]
            vprev = self.vfence()
            self.op('T', 'matmul', pb7[:, 0:256], self.ones1b[:, :],
                    a2row[0:1, sp * 256:(sp + 1) * 256], start=True, stop=True,
                    wait=[('V', vprev)] + (rows2 if sp == 0 else []))
            self.op('T', 'matmul', pb7[:, 256:512], self.ones1b[:, :],
                    b2row[0:1, sp * 256:(sp + 1) * 256], start=True, stop=True,
                    inc=('T', 1))
            tb2 = self.val('T')
            self.op('V', 'tensor_copy', A2w[:, :], pb7[:, 0:256], wait=[('T', tb2)])
            self.op('V', 'tensor_copy', B2w[:, :], pb7[:, 256:512], inc=('V', 1))
            self.wait('T', 'V', self.vfence())
            for c in range(NCH):
                off = 0 if c >= 4 else 1
                wz = tz if c == 0 else None
                self.op('V', 'tensor_tensor', tmpd[:, :], zsp3[cur][:, c, off:off + 256],
                        A2w[:, :], op=ALU.mult, wait=wz)
                self.op('V', 'tensor_tensor', h2[:, c * 256:(c + 1) * 256],
                        tmpd[:, :], B2w[:, :], op=ALU.add,
                        inc=(('V', 1) if c == NCH - 1 else None))
            vh2 = self.vfence()
            self.wait('T', 'V', vh2)
            gel2 = [gel, gel_b]
            vus = {}
            def emit_ag(j):
                pag = pb5 if j % 2 == 0 else pb6
                alo, ahi = a_cols[j]
                glo, ghi = g_cols[j]
                m = ahi - alo
                for c in range(NCH):
                    self.op('T', 'matmul', pag[0:m, 0:256], w1s[c][:, alo:ahi],
                            h2[:, c * 256:(c + 1) * 256],
                            start=(c == 0), stop=(c == NCH - 1),
                            skip_group_check=True)
                    self.op('T', 'matmul', pag[0:m, 256:512], w1s[c][:, glo:ghi],
                            h2[:, c * 256:(c + 1) * 256],
                            start=False, stop=(c == NCH - 1),
                            skip_group_check=True,
                            inc=(('T', 1) if c == NCH - 1 else None))
                tag = self.val('T')
                self.op('S', 'activation', gel2[j % 2][0:m, :], pag[0:m, 256:512], AF.Gelu,
                        bias=b1t[0:m, 22 + j:23 + j], wait=[('T', tag)], inc=('S', 1))
                sgel = self.sfence()
                self.op('V', 'scalar_tensor_tensor', usb[j % 2][0:m, :],
                        pag[0:m, 0:256], b1t[0:m, j:j + 1], gel2[j % 2][0:m, :],
                        op0=ALU.add, op1=ALU.mult, wait=[('S', sgel)], inc=('V', 1))
                vus[j] = self.vfence()
            def emit_w2(j):
                m = a_cols[j][1] - a_cols[j][0]
                self.wait('T', 'V', vus[j])
                for oc2 in range(NCH):
                    pob = pb1 if oc2 < 4 else pb3
                    colo = (oc2 % 4) * 256
                    self.op('T', 'matmul', pob[:, colo:colo + 256],
                            w2s[j][:, oc2 * 128:(oc2 + 1) * 128], usb[j % 2][0:m, :],
                            start=(j == 0 and oc2 % 2 == 0), stop=(j == NUC - 1),
                            skip_group_check=True,
                            inc=(('T', 1) if oc2 == NCH - 1 else None))
            emit_ag(0)
            for j in range(1, NUC):
                emit_ag(j)
                emit_w2(j - 1)
            emit_w2(NUC - 1)
            tw2d = self.val('T')
            for oc2 in range(NCH):
                pob = pb1 if oc2 < 4 else pb3
                colo = (oc2 % 4) * 256
                zb = zot[oc2 % 2]
                wv = [('T', tw2d)] if oc2 == 0 else []
                if oc2 == 0 and sp == 0:
                    wv.append(tb2c)
                if out_prev.get(oc2 % 2) is not None:
                    wv.append(out_prev[oc2 % 2])
                self.op('V', 'scalar_tensor_tensor', zb[:, :],
                        pob[:, colo:colo + 256], b2c[:, oc2:oc2 + 1],
                        zsp3[cur][:, oc2, 1:257], op0=ALU.add, op1=ALU.add,
                        wait=wv or None)
                if l == 1:
                    self.op('V', 'tensor_tensor', zb[:, :], zb[:, :],
                            x1sp[cur][:, oc2 * 256:(oc2 + 1) * 256], op=ALU.add,
                            wait=tx1 if oc2 == 0 else None)
                vzo = self.vfence()
                tkst = self.dmaS(Rz[:, oc2, sp * 256:(sp + 1) * 256], zb[:, :],
                                 wait=[('V', vzo)])
                out_prev[oc2 % 2] = tkst
                store_out.append(tkst)
            self.wait('T', 'V', self.vfence())
            vsp_done[cur] = self.val('V')
        self.ffn_gate = [('T', self.val('T')), ('V', self.val('V')),
                         ('S', self.val('S'))] + store_out
        stD.close()
        stR.close()

    def _finish(self):
        for att in ('stA', 'stH0'):
            st = getattr(self, att, None)
            if st is not None:
                st.close()
        toks = self.allw('LS') + self.allw('LG') + self.allw('LH')
        self.waitl('G', toks)


_PROG_CACHE = {}


def _build(stage, nwin=8, sub=50):
    mk = MK(stage, nwin, sub)
    nc = mk.build()
    return nc


def kernel(**inputs):
    sh, cores = host_prep(inputs)
    nc = _build(BUILD_STAGE)
    in_maps = []
    for b in range(8):
        m = dict(sh)
        m.update(cores[b])
        in_maps.append(m)
    res = run_bass_kernel_spmd(nc, in_maps, core_ids=list(range(8)))
    outs = []
    key = 'R1' if BUILD_STAGE == 'A' else 'outT'
    for r in res.results:
        outs.append(np.ascontiguousarray(r[key].T))
    return np.stack(outs, axis=0)
